# version 15
# baseline (speedup 1.0000x reference)
import numpy as np
import ml_dtypes
from contextlib import ExitStack

import concourse.bass as bass
import concourse.mybir as mybir
from concourse.bass_utils import run_bass_kernel_spmd

F32 = mybir.dt.float32
BF16 = mybir.dt.bfloat16
AX = mybir.AxisListType.X
ALU = mybir.AluOpType
ACT = mybir.ActivationFunctionType
BFNP = ml_dtypes.bfloat16

T = 2048
D = 1024
NT = T // 128
NCT = D // 128
THETA = 1e5
LN_EPS = 1e-5
GEPS = 0.5 + 1e-8

_BUILD_UPTO = "full"
import os as _os
_ATTN_PAIRS = int(_os.environ.get("K_ATTN_PAIRS", "8"))
_ATTN_QT = int(_os.environ.get("K_ATTN_QT", "16"))
_K_WO = int(_os.environ.get("K_WO", "1"))
_K_MOE_TC = int(_os.environ.get("K_MOE_TC", "4"))
_K_MOE_E = int(_os.environ.get("K_MOE_E", "9"))
_PTEV = int(_os.environ.get("K_PTEV", "1"))
_STAGES = ("mingru", "ln1", "pre", "attn", "rms2", "gate", "moe", "full")


def _stage_ge(a, b):
    return _STAGES.index(a) >= _STAGES.index(b)



def _host_consts():
    c = {}
    c["cident_bf"] = np.eye(128, dtype=BFNP)
    c["cident_f"] = np.eye(128, dtype=np.float32)
    tri = np.where(np.arange(128)[None, :] <= np.arange(128)[:, None], 0.0, -1e9)
    c["ctrimask"] = tri.astype(np.float32)
    c["ctrimask_bf"] = tri.astype(BFNP)
    csel = np.zeros((8, 8 * 128), np.float32)
    for e in range(8):
        csel[e, e * 128:(e + 1) * 128] = 1.0
    c["csel"] = csel
    t = np.arange(T, dtype=np.float64)
    freqs = THETA ** (-np.arange(16, dtype=np.float64) / 16.0)
    ang = np.outer(freqs, t)
    cosv, sinv = np.cos(ang), np.sin(ang)
    C = np.zeros((128, T), np.float64)
    S = np.zeros((128, T), np.float64)
    for base in (32, 96):
        C[base:base + 16] = cosv
        C[base + 16:base + 32] = cosv
        S[base:base + 16] = -sinv
        S[base + 16:base + 32] = sinv
    C[0:32] = 1.0
    C[64:96] = 1.0
    c["ccq"] = C.astype(BFNP)
    c["csq"] = S.astype(BFNP)
    return c


def _q_col_perm():
    perm = np.empty(D, np.int64)
    for j in range(D):
        jp, r = divmod(j, 128)
        h = 2 * jp + r // 64
        rr = r % 64
        if rr < 32:
            o = 64 * h + rr
        elif rr < 48:
            o = 64 * h + 32 + 2 * (rr - 32)
        else:
            o = 64 * h + 32 + 2 * (rr - 48) + 1
        perm[j] = o
    return perm


def _host_weights(ins):
    w = {}
    for nm, key in (("wbd_f", "L0_attn_Wf"), ("wbd_i", "L0_attn_Wi"),
                    ("wbd_h", "L0_attn_Wh")):
        W = np.asarray(ins[key], dtype=np.float32)
        outw = np.zeros((128, NCT * 128), np.float32)
        for ct in range(NCT):
            for gl in range(16):
                g = 16 * ct + gl
                outw[8 * gl:8 * gl + 8, ct * 128 + 8 * gl: ct * 128 + 8 * gl + 8] = W[g].T
        w[nm] = outw

    Wq = np.asarray(ins["L0_ffn_Wq"], dtype=np.float32)[:, _q_col_perm()]
    w["wq"] = Wq.reshape(NCT, 128, D).transpose(1, 0, 2).reshape(128, NCT * D)

    Wkva = np.asarray(ins["L0_ffn_Wkva"], dtype=np.float32)
    kva = np.zeros((D, 192), np.float32)
    kva[:, 0:128] = Wkva[:, 0:128]
    kva[:, 160:176] = Wkva[:, 128:160][:, 0::2]
    kva[:, 176:192] = Wkva[:, 128:160][:, 1::2]
    w["wkva"] = kva.reshape(NCT, 128, 192).transpose(1, 0, 2).reshape(128, NCT * 192)

    w["wkvb"] = np.asarray(ins["L0_ffn_Wkvb"], dtype=np.float32)

    Wo = np.asarray(ins["L0_ffn_Wo"], dtype=np.float32)
    rows = np.empty(512, np.int64)
    for g in range(4):
        for a in range(4):
            h = 4 * g + a
            rows[g * 128 + a * 32: g * 128 + a * 32 + 32] = 64 * h + np.arange(32)
    w["wo"] = Wo[rows].reshape(4, 128, D).transpose(1, 0, 2).reshape(128, 4 * D)

    w["wgate"] = np.asarray(ins["L0_ffn_Wgate"], dtype=np.float32) \
        .reshape(NCT, 128, 8).transpose(1, 0, 2).reshape(128, NCT * 8)

    gg = np.concatenate([np.asarray(ins["L0_ffn_Wg"], dtype=np.float32),
                         np.asarray(ins["L0_ffn_sWg"], dtype=np.float32)], 0)
    uu = np.concatenate([np.asarray(ins["L0_ffn_Wu"], dtype=np.float32),
                         np.asarray(ins["L0_ffn_sWu"], dtype=np.float32)], 0)
    dd = np.concatenate([np.asarray(ins["L0_ffn_Wd"], dtype=np.float32),
                         np.asarray(ins["L0_ffn_sWd"], dtype=np.float32)], 0)
    w["wgu"] = gg.reshape(9, NCT, 128, 512).transpose(0, 2, 1, 3).reshape(9, 128, NCT * 512)
    w["wuu"] = uu.reshape(9, NCT, 128, 512).transpose(0, 2, 1, 3).reshape(9, 128, NCT * 512)
    w["wdd"] = dd.reshape(9, 4, 128, D).transpose(0, 2, 1, 3).reshape(9, 128, 4 * D)

    w["wln1"] = np.asarray(ins["L0_ffn_w_ln1"], dtype=np.float32)
    w["wln2"] = np.asarray(ins["L0_ffn_w_ln2"], dtype=np.float32)
    w["ebias"] = np.asarray(ins["L0_ffn_expert_bias"], dtype=np.float32)
    return w



ENGS = ("t", "v", "s", "g", "y")
QLANES = {"y": 4, "g": 4, "s": 2}
QSEMS = [f"q{q}{i}" for q, n in QLANES.items() for i in range(n)]


class Prog:
    def __init__(self):
        self.stream = {e: [] for e in ENGS}
        self.cnt = {k: 0 for k in ("t", "v", "s", "g", *QSEMS)}
        self.lane_rr = {q: 0 for q in QLANES}

    def op(self, eng, fn, waits=(), sig=None, hwaits=()):
        w = tuple(ev for ev in waits if ev is not None)
        hw = tuple(ev for ev in hwaits if ev is not None)
        for ev_ in w + hw:
            assert isinstance(ev_, tuple) and len(ev_) == 2 and isinstance(ev_[1], int), \
                f"bad wait event: {ev_!r}"
        ev = None
        if sig is not None:
            self.cnt[sig] += 16 if sig.startswith("q") else 1
            ev = (sig, self.cnt[sig])
        self.stream[eng].append((w, fn, sig, hw))
        return ev

    def dma(self, queue, dst, src, waits=(), slow=False):
        lane = self.lane_rr[queue]
        self.lane_rr[queue] = (lane + 1) % QLANES[queue]
        sem = f"q{queue}{lane}"
        prev = self.cnt[sem]
        w = tuple(waits) + (((sem, prev),) if prev else ())
        return self.op(queue,
                       (lambda e, d=dst, s_=src, sl=slow:
                        e.dma_start(d, s_, allow_slow_non_contiguous=sl)),
                       waits=w, sig=sem)

    def realize(self, nc, sems):
        waited = {e: {} for e in ENGS}

        def runner(ename):
            def run(eng):
                wd = waited[ename]
                for (w, fn, sig, hw) in self.stream[ename]:
                    for (snm, val) in w:
                        if snm == ename and not (ename == "s" and snm == "s"):
                            continue
                        if wd.get(snm, -1) >= val:
                            continue
                        eng.wait_ge(sems[snm], val)
                        wd[snm] = val
                    for (snm, val) in hw:
                        if wd.get(snm, -1) >= val:
                            continue
                        eng.wait_ge(sems[snm], val)
                        wd[snm] = val
                    ins = fn(eng)
                    if sig is not None:
                        ins.then_inc(sems[sig], 16 if sig.startswith("q") else 1)
            return run

        with nc.Block() as block:
            block.tensor(runner("t"))
            block.vector(runner("v"))
            block.scalar(runner("s"))
            block.gpsimd(runner("g"))
            block.sync(runner("y"))


AR_BYTES = 144832

O_WGATE = 0
O_WQ = 128
O_WKVA = 16512
O_WKVB = 19584
O_CCQ = 20608
O_CSQ = 24704
O_WO = 28800

O_WBD = 36992
O_XBF = 43136
O_MGBUF = 51328
O_XSTAGE = 116864

O_XN = 36992
O_LN_ROWS = 69760
O_LN_REPS = 102528
O_LN_SQ = 127104

O_KP = 36992
O_P = 45184
O_PT = 53376
O_YGRP = 61568
O_SPARE = 65664
O_QH2 = 69760
O_KVFM = 102528
O_KVTM = 118912
O_LAT = 135296
O_KR = 139392

MOE_WSLOT = [128, 8320, 16512, 69760, 77952, 86144]
MOE_H = [36992, 41088, 45184, 49280, 53376, 57472, 61568, 94336, 98432]
XN2_OFF = [24704, 28800, 32896, 102528, 106624, 110720, 114816, 118912]
O_M2SQ = [45184, 53376]
O_M2ROW = 61568
O_MREP = 36992
O_YTM = 65664
O_COMB = 123008
O_LG = 65664
O_PR = 66176
O_MS = 66688
O_MS2 = 67200
O_COMBREP = 67712
O_SG = 135296
O_CSEL = 131200

O_TM = 128
O_SQ2 = 8320
O_X1B = 16512


def build_program(upto="full", debug_dump=()):
    nc = bass.Bass("TRN2", target_bir_lowering=False, detect_race_conditions=False)

    def din(name, shape, dt=F32):
        return nc.dram_tensor(name, list(shape), dt, kind="ExternalInput").ap()

    xin = din("xin", (T, D))
    d_cident_bf = din("cident_bf", (128, 128), BF16)
    d_cident_f = din("cident_f", (128, 128))
    d_ctrimask = din("ctrimask", (128, 128))
    d_ctrimask_bf = din("ctrimask_bf", (128, 128), BF16)
    d_csel = din("csel", (8, 8 * 128))
    d_ccq = din("ccq", (128, T), BF16)
    d_csq = din("csq", (128, T), BF16)
    d_wbd = {k: din("wbd_" + k, (128, NCT * 128)) for k in "fih"}
    d_wq = din("wq", (128, NCT * D))
    d_wkva = din("wkva", (128, NCT * 192))
    d_wkvb = din("wkvb", (128, 512))
    d_wo = din("wo", (128, 4 * D))
    d_wgate = din("wgate", (128, NCT * 8))
    d_wgu = din("wgu", (9, 128, NCT * 512))
    d_wuu = din("wuu", (9, 128, NCT * 512))
    d_wdd = din("wdd", (9, 128, 4 * D))
    d_wln1 = din("wln1", (D,))
    d_wln2 = din("wln2", (D,))
    d_ebias = din("ebias", (8,))

    out = nc.dram_tensor("out", [T, D], F32, kind="ExternalOutput").ap()
    x1sc = nc.dram_tensor("x1sc", [D, T], F32, kind="Internal").ap()
    dbg = {}
    for nm in debug_dump:
        shape = [8, T] if nm == "comb" else [D, T]
        dbg[nm] = nc.dram_tensor("dbg_" + nm, list(shape), F32, kind="ExternalOutput").ap()

    P = Prog()

    with ExitStack() as ctx:
        e = ctx.enter_context

        X = [e(nc.sbuf_tensor(f"X{i}", [128, T], F32)) for i in range(NCT)]
        ident_bf = e(nc.sbuf_tensor("ident_bf", [128, 128], BF16))
        ident_f = e(nc.sbuf_tensor("ident_f", [128, 128], F32))
        trimask = e(nc.sbuf_tensor("trimask", [128, 128], F32))
        trimask_bf = e(nc.sbuf_tensor("trimask_bf", [128, 128], BF16))
        gam1 = e(nc.sbuf_tensor("gam1", [128, NCT], F32))
        gam2 = e(nc.sbuf_tensor("gam2", [128, NCT], F32))
        ebias_sb = e(nc.sbuf_tensor("ebias_sb", [1, 8], F32))
        ebias_rep = e(nc.sbuf_tensor("ebias_rep", [128, 8], F32))
        colw = e(nc.sbuf_tensor("colw", [128, 24], F32))
        ones_c = e(nc.sbuf_tensor("ones_c", [128, 1], F32))
        ones_r = e(nc.sbuf_tensor("ones_r", [1, 128], F32))
        ysc = e(nc.sbuf_tensor("ysc", [128, 64], BF16))
        arena = e(nc.sbuf_tensor("arena", [128, AR_BYTES // 2], BF16))

        pbA = e(nc.psum_tensor("pbA", [128, 2048], F32))
        pbB = e(nc.psum_tensor("pbB", [128, 2048], F32))

        def bk(i):
            t_ = pbA if i < 4 else pbB
            j = i % 4
            return t_[:, j * 512:(j + 1) * 512]

        def bkb(i):
            t_ = pbA if i < 4 else pbB
            j = i % 4
            return t_[:, j * 512:(j + 1) * 512].bitcast(BF16)

        sems = {k: e(nc.semaphore("sem_" + k))
                for k in ("t", "v", "s", "g", *QSEMS)}

        def ab(off, n, p0=0, p1=128):
            return arena[p0:p1, off // 2: off // 2 + n]

        def af(off, n, p0=0, p1=128):
            assert off % 4 == 0
            return arena[p0:p1, off // 2: off // 2 + 2 * n].bitcast(F32)

        ev_id_bf = P.dma("y", ident_bf[:, :], d_cident_bf)
        ev_id_f = P.dma("y", ident_f[:, :], d_cident_f)
        P.dma("y", trimask[:, :], d_ctrimask)
        ev_tri_bf = P.dma("y", trimask_bf[:, :], d_ctrimask_bf)
        P.dma("y", ab(O_CCQ, T), d_ccq)
        ev_tab = P.dma("y", ab(O_CSQ, T), d_csq)
        P.dma("y", gam1[:, :], d_wln1.rearrange("(a b) -> b a", b=128), slow=True)
        ev_gam2 = P.dma("y", gam2[:, :], d_wln2.rearrange("(a b) -> b a", b=128), slow=True)
        ev_eb = P.dma("y", ebias_sb[0:1, :], d_ebias.rearrange("(o a) -> o a", o=1))
        ev_ones = P.op("g", lambda e_: e_.memset(ones_c[:, :], 1.0), sig="g")
        ev_ones2 = P.op("g", lambda e_: e_.memset(ones_r[:, :], 1.0), sig="g")

        ev_wbd = {}
        for i, k in enumerate("fih"):
            ev_wbd[k] = P.dma("g", ab(O_WBD + i * 2048, NCT * 128), d_wbd[k])
        ev_wq = P.dma("g", ab(O_WQ, NCT * D), d_wq)
        ev_wkva = P.dma("g", ab(O_WKVA, NCT * 192), d_wkva)
        ev_wkvb = P.dma("g", ab(O_WKVB, 512), d_wkvb)
        ev_wo = P.dma("g", ab(O_WO, 4 * D), d_wo)
        ev_wgate = P.dma("g", ab(O_WGATE, NCT * 8), d_wgate)

        ev_xready = [None] * NCT
        ev_stage_free = [None] * 4
        bankfree = {j: None for j in range(8)}
        for grp in range(4):
            ev_st = []
            for j in range(4):
                tt = grp * 4 + j
                st = af(O_XSTAGE + j * 4096, D)
                ev_st.append(P.dma("y", st, xin[tt * 128:(tt + 1) * 128, :],
                                   waits=(ev_stage_free[j],)))
            for ct in range(NCT):
                bank = ct % 4
                evt = None
                for j in range(4):
                    st = af(O_XSTAGE + j * 4096, D)
                    evt = P.op(
                        "t",
                        (lambda e_, b=bank, jj=j, s_=st, c=ct:
                         e_.transpose(bk(b)[:, jj * 128:(jj + 1) * 128],
                                      s_[:, c * 128:(c + 1) * 128], ident_f[:, :])),
                        waits=(ev_st[j], bankfree[bank], ev_id_f),
                        sig="t" if j == 3 else None)
                ev_ev = P.op(
                    "v",
                    (lambda e_, b=bank, c=ct, g=grp:
                     e_.tensor_copy(X[c][:, g * 512:(g + 1) * 512], bk(b))),
                    waits=(evt,), sig="v")
                bankfree[bank] = ev_ev
                if ct == NCT - 1:
                    for j in range(4):
                        ev_stage_free[j] = ev_ev
                if grp == 3:
                    ev_xready[ct] = ev_ev

        def mgb(seti, k):
            return af(O_MGBUF + (seti * 4 + k) * 8192, T)

        def xbf_t(seti):
            return ab(O_XBF + seti * 4096, T)

        wbd_bf = {k: ab(O_WBD + i * 2048, NCT * 128) for i, k in enumerate("fih")}

        ev_scan = [None] * NCT
        ev_sig2 = [None] * NCT
        ev_diff = [None] * NCT
        ev_hcopy = [None] * NCT
        ev_ef = [None] * NCT
        ev_ei = [None] * NCT
        ev_lnf = [None] * NCT
        ev_lni = [None] * NCT
        ev_f_mm = [None] * NCT
        ev_i_mm = [None] * NCT
        ev_h_mm = [None] * NCT
        ev_xbf = [None] * NCT

        for ct in range(NCT):
            si = ct % 2
            ev_xbf[ct] = P.op(
                "v", (lambda e_, c=ct, s=si: e_.tensor_copy(xbf_t(s)[:, :], X[c][:, :])),
                waits=(ev_xready[ct], ev_scan[ct - 2] if ct >= 2 else None),
                sig="v")
            for j in range(4):
                ev_f_mm[ct] = P.op(
                    "t",
                    (lambda e_, c=ct, jj=j, s=si:
                     e_.matmul(bk(jj), wbd_bf["f"][:, c * 128:(c + 1) * 128],
                               xbf_t(s)[:, jj * 512:(jj + 1) * 512],
                               start=True, stop=True)),
                    waits=(ev_xbf[ct], ev_wbd["f"],
                           ev_hcopy[ct - 1] if ct >= 1 else bankfree[j]),
                    sig="t" if j == 3 else None)
            for j in range(4):
                ev_i_mm[ct] = P.op(
                    "t",
                    (lambda e_, c=ct, jj=j, s=si:
                     e_.matmul(bk(4 + jj), wbd_bf["i"][:, c * 128:(c + 1) * 128],
                               xbf_t(s)[:, jj * 512:(jj + 1) * 512],
                               start=True, stop=True)),
                    waits=(ev_wbd["i"], ev_ei[ct - 1] if ct >= 1 else None),
                    sig="t" if j == 3 else None)
            Pd, Ph, Pf, Pm = (mgb(si, 0), mgb(si, 1), mgb(si, 2), mgb(si, 3))
            wb = ev_sig2[ct - 2] if ct >= 2 else None
            wb2 = ev_scan[ct - 2] if ct >= 2 else None
            ev_ef[ct] = P.op(
                "s", (lambda e_, p=Pd: e_.activation(p, pbA[:, :], ACT.Exp, scale=-1.0)),
                waits=(ev_f_mm[ct], wb, wb2), sig="s")
            ev_lnf[ct] = P.op(
                "s", (lambda e_, p=Pd: e_.activation(p, p, ACT.Ln, bias=1.0)),
                waits=(ev_ef[ct],), sig="s")
            ev_ei[ct] = P.op(
                "s", (lambda e_, p=Pf: e_.activation(p, pbB[:, :], ACT.Exp, scale=-1.0)),
                waits=(ev_i_mm[ct], wb, wb2), sig="s")
            ev_lni[ct] = P.op(
                "s", (lambda e_, p=Pf: e_.activation(p, p, ACT.Ln, bias=1.0)),
                waits=(ev_ei[ct],), sig="s")
            for j in range(4):
                ev_h_mm[ct] = P.op(
                    "t",
                    (lambda e_, c=ct, jj=j, s=si:
                     e_.matmul(bk(jj), wbd_bf["h"][:, c * 128:(c + 1) * 128],
                               xbf_t(s)[:, jj * 512:(jj + 1) * 512],
                               start=True, stop=True)),
                    waits=(ev_ef[ct], ev_wbd["h"]),
                    sig="t" if j == 3 else None)
            ev_diff[ct] = P.op(
                "v", (lambda e_, a=Pd, b=Pf: e_.tensor_sub(a, a, b)),
                waits=(ev_lnf[ct], ev_lni[ct]), sig="v")
            ev_hcopy[ct] = P.op(
                "v", (lambda e_, p=Ph: e_.tensor_copy(p, pbA[:, :])),
                waits=(ev_h_mm[ct],), sig="v")
            if ct % 2 == 1:
                for c2 in (ct - 1, ct):
                    s2 = c2 % 2
                    Qd, Qh, Qf, Qm = (mgb(s2, 0), mgb(s2, 1), mgb(s2, 2), mgb(s2, 3))
                    ev_fp = P.op(
                        "s", (lambda e_, a=Qd, o=Qf: e_.activation(o, a, ACT.Sigmoid, scale=-1.0)),
                        waits=(ev_diff[c2],), sig="s")
                    ev_sig2[c2] = P.op(
                        "s", (lambda e_, a=Qh, o=Qd: e_.activation(o, a, ACT.Sigmoid)),
                        waits=(ev_hcopy[c2], ev_fp), sig="s")
                    P.op("v", (lambda e_, a=Qh, o=Qm: e_.tensor_single_scalar(o, a, 0.0, ALU.is_ge)),
                         waits=(ev_sig2[c2],))
                    P.op("v", (lambda e_, a=Qh: e_.tensor_scalar(a, a, GEPS, None, ALU.add)))
                    P.op("v", (lambda e_, a=Qh, b=Qd: e_.tensor_sub(a, a, b)))
                    P.op("v", (lambda e_, a=Qh, b=Qm: e_.tensor_mul(a, a, b)))
                    P.op("v", (lambda e_, a=Qh, b=Qd: e_.tensor_add(a, a, b)))
                    P.op("v", (lambda e_, a=Qf, o=Qm:
                               e_.tensor_scalar(o, a, -1.0, 1.0, ALU.mult, ALU.add)))
                    P.op("v", (lambda e_, a=Qm, b=Qh: e_.tensor_mul(a, a, b)))
                    P.op("v", (lambda e_, f=Qf, v_=Qm, o=Qh:
                               e_.tensor_tensor_scan(o, f, v_, 0.0, ALU.mult, ALU.add)))
                    ev_scan[c2] = P.op(
                        "v", (lambda e_, c=c2, hh=Qh: e_.tensor_add(X[c][:, :], X[c][:, :], hh)),
                        sig="v")

        def dump_X(tag, waits):
            if tag in dbg:
                for ct in range(NCT):
                    P.dma("y", dbg[tag][ct * 128:(ct + 1) * 128, :], X[ct][:, :],
                          waits=waits)

        if upto == "mingru":
            dump_X("X", tuple(ev_scan))
            _finish(P, sems)
            P.realize(nc, sems)
            return nc

        def xn_t(ct, which):
            off = (O_XN + ct * 4096) if which == 1 else XN2_OFF[ct]
            return ab(off, T)

        rowsl = [af(O_LN_ROWS + i * 8192, T, 0, 1) for i in range(4)]
        repr_ = af(O_LN_REPS, T)
        repmr = af(O_LN_REPS + 8192, T)
        repr2 = af(O_LN_REPS + 16384, T)

        ev_sq = [None, None]
        ev_sums = None
        ev_sumsq = None
        ev_sumsq_ct = [None] * NCT
        for ct in range(NCT):
            sqb = af(O_LN_SQ + (ct % 2) * 8192, T)
            ev_sq[ct % 2] = P.op(
                "s", (lambda e_, c=ct, o=sqb: e_.activation(o, X[c][:, :], ACT.Square)),
                waits=(ev_scan[ct],
                       ev_sumsq_ct[ct - 2] if ct >= 2 else None), sig="s")
            for j in range(4):
                ev_sums = P.op(
                    "t",
                    (lambda e_, c=ct, jj=j:
                     e_.matmul(bk(jj)[0:1, :], ones_c[:, 0:1],
                               X[c][:, jj * 512:(jj + 1) * 512],
                               start=(c == 0), stop=(c == NCT - 1))),
                    waits=(ev_scan[ct], ev_ones,
                           ev_hcopy[NCT - 1] if ct == 0 else None),
                    sig="t" if j == 3 else None)
            for j in range(4):
                ev_sumsq = P.op(
                    "t",
                    (lambda e_, o=sqb, jj=j, c=ct:
                     e_.matmul(bk(4 + jj)[0:1, :], ones_c[:, 0:1],
                               o[:, jj * 512:(jj + 1) * 512],
                               start=(c == 0), stop=(c == NCT - 1))),
                    waits=(ev_sq[ct % 2],
                           ev_ei[NCT - 1] if ct == 0 else None),
                    sig="t" if j == 3 else None)
            ev_sumsq_ct[ct] = ev_sumsq

        ev_m = P.op("v", (lambda e_: e_.tensor_scalar(rowsl[0], pbA[0:1, :], 1.0 / D, None, ALU.mult)),
                    waits=(ev_sums,), sig="v")
        ev_q = P.op("v", (lambda e_: e_.tensor_scalar(rowsl[1], pbB[0:1, :], 1.0 / D, None, ALU.mult)),
                    waits=(ev_sumsq,), sig="v")
        P.op("v", (lambda e_: e_.tensor_mul(rowsl[2], rowsl[0], rowsl[0])))
        ev_var = P.op("v", (lambda e_: e_.tensor_sub(rowsl[1], rowsl[1], rowsl[2])), sig="v")
        ev_vare = P.op("v", (lambda e_: e_.tensor_scalar(rowsl[1], rowsl[1], LN_EPS, None, ALU.add)),
                       waits=(ev_var,), sig="v")
        ev_lnv = P.op("s", (lambda e_: e_.activation(rowsl[2], rowsl[1], ACT.Ln)),
                      waits=(ev_vare,), sig="s")
        ev_r = P.op("s", (lambda e_: e_.activation(rowsl[3], rowsl[2], ACT.Exp, scale=-0.5)),
                    waits=(ev_lnv,), sig="s")
        ev_rr = P.op("s", (lambda e_: e_.activation(rowsl[2], rowsl[2], ACT.Exp, scale=-1.0)),
                     waits=(ev_lnv, ev_r), sig="s")
        ev_msu = P.op("v", (lambda e_: e_.scalar_tensor_tensor(rowsl[2], rowsl[1], -LN_EPS,
                                                               rowsl[2], ALU.add, ALU.mult)),
                      waits=(ev_rr,), sig="v")
        ev_ms2 = P.op("v", (lambda e_: e_.tensor_scalar(rowsl[2], rowsl[2], LN_EPS, None, ALU.add)),
                      sig="v")
        ev_ln2 = P.op("s", (lambda e_: e_.activation(rowsl[2], rowsl[2], ACT.Ln)),
                      waits=(ev_ms2,), sig="s")
        ev_r2 = P.op("s", (lambda e_: e_.activation(rowsl[2], rowsl[2], ACT.Exp, scale=-0.5)),
                     waits=(ev_ln2,), sig="s")
        ev_mr = P.op("v", (lambda e_: e_.tensor_mul(rowsl[0], rowsl[0], rowsl[3])),
                     waits=(ev_r,), sig="v")

        def replicate(dst, src_row, waits, bank_wait):
            last = None
            for j in range(4):
                last = P.op(
                    "t",
                    (lambda e_, jj=j, s_=src_row:
                     e_.matmul(bk(jj), ones_r[0:1, :],
                               s_[0:1, jj * 512:(jj + 1) * 512], start=True, stop=True)),
                    waits=tuple(waits) + (ev_ones2, bank_wait),
                    sig="t" if j == 3 else None)
            return P.op("v", (lambda e_, d=dst: e_.tensor_copy(d, pbA[:, :])),
                        waits=(last,), sig="v")

        ev_repr = replicate(repr_, rowsl[3], (ev_r, ev_m), ev_m)
        ev_repmr = replicate(repmr, rowsl[0], (ev_mr,), ev_repr)
        ev_repr2 = replicate(repr2, rowsl[2], (ev_r2,), ev_repmr)

        ev_xn1 = [None] * NCT
        ev_x1 = [None] * NCT
        for ct in range(NCT):
            sqb = af(O_LN_SQ + (ct % 2) * 8192, T)
            P.op("v", (lambda e_, c=ct, o=sqb: e_.scalar_tensor_tensor(
                o, X[c][:, :], 1.0, repr_, ALU.mult, ALU.mult)),
                waits=(ev_repr, ev_sumsq))
            ev_x1[ct] = P.op("v", (lambda e_, c=ct, a=sqb: e_.tensor_sub(X[c][:, :], a, repmr)),
                             waits=(ev_repmr,), sig="v")
            P.op("v", (lambda e_, c=ct, o=sqb: e_.tensor_mul(o, X[c][:, :], repr2)),
                 waits=(ev_repr2,))
            ev_xn1[ct] = P.op(
                "v", (lambda e_, c=ct, a=sqb: e_.tensor_scalar(
                    xn_t(c, 1), a, gam1[:, c:c + 1], None, ALU.mult)),
                sig="v")
        ev_x1dma = [None] * NCT
        for ct in range(NCT):
            ev_x1dma[ct] = P.dma("y", x1sc[ct * 128:(ct + 1) * 128, :], X[ct][:, :],
                                 waits=(ev_x1[ct],))
        ev_x1dma_all = ev_x1dma[NCT - 1]

        if upto == "ln1":
            dump_X("X", tuple(ev_x1))
            if "xn1" in dbg:
                for ct in range(NCT):
                    evc = P.op("v", (lambda e_, c=ct: e_.tensor_copy(
                        af(O_LN_SQ, T), xn_t(c, 1))), waits=(ev_xn1[ct],), sig="v")
                    P.dma("y", dbg["xn1"][ct * 128:(ct + 1) * 128, :], af(O_LN_SQ, T),
                          waits=(evc,))
            _finish(P, sems)
            P.realize(nc, sems)
            return nc

        lat_bf = ab(O_LAT, T)
        kr_b = ab(O_KR, T, 0, 64)
        kvtm_all = ab(O_KVTM, NT * 512)
        wkva_bf = ab(O_WKVA, NCT * 192)
        wkvb_bf = ab(O_WKVB, 512)
        wq_bf = ab(O_WQ, NCT * D)
        wo_bf = ab(O_WO, 4 * D)
        ccq = ab(O_CCQ, T)
        csq = ab(O_CSQ, T)
        smask = list(range(16, 32)) + list(range(0, 16))

        ev_lat_mm = None
        for tc in range(4):
            for cc in range(NCT):
                ev_lat_mm = P.op(
                    "t",
                    (lambda e_, c=cc, j=tc:
                     e_.matmul(bk(j), wkva_bf[:, c * 192: c * 192 + 128],
                               xn_t(c, 1)[:, j * 512:(j + 1) * 512],
                               start=(c == 0), stop=(c == NCT - 1))),
                    waits=(ev_xn1[cc], ev_wkva),
                    sig="t" if cc == NCT - 1 else None)
        ev_lat = P.op("v", (lambda e_: e_.tensor_copy(lat_bf[:, :], pbA[:, :])),
                      waits=(ev_lat_mm,), sig="v")
        ev_kr_mm = None
        for tc in range(4):
            for cc in range(NCT):
                ev_kr_mm = P.op(
                    "t",
                    (lambda e_, c=cc, j=tc:
                     e_.matmul(bk(4 + j)[0:64, :], wkva_bf[:, c * 192 + 128: c * 192 + 192],
                               xn_t(c, 1)[:, j * 512:(j + 1) * 512],
                               start=(c == 0), stop=(c == NCT - 1))),
                    waits=(ev_xn1[cc], ev_wkva),
                    sig="t" if cc == NCT - 1 else None)
        ev_kr_raw = P.op("v", (lambda e_: e_.tensor_copy(kr_b[32:64, :], pbB[32:64, :])),
                         waits=(ev_kr_mm,), sig="v")
        krs = ab(O_SPARE, T, 0, 64)
        P.op("v", (lambda e_: e_.stream_shuffle(krs[32:64, :], kr_b[32:64, :], smask)),
             waits=(ev_kr_raw, ev_tab))
        P.op("v", (lambda e_: e_.tensor_mul(kr_b[32:64, :], kr_b[32:64, :], ccq[32:64, :])))
        P.op("v", (lambda e_: e_.tensor_mul(krs[32:64, :], krs[32:64, :], csq[32:64, :])))
        ev_kr = P.op("v", (lambda e_: e_.tensor_add(kr_b[32:64, :], kr_b[32:64, :], krs[32:64, :])),
                     sig="v")

        ev_kvtm_mm = [None] * NT
        ev_kvtm = [None] * NT
        for kt in range(NT):
            b = kt % 4
            ev_kvtm_mm[kt] = P.op(
                "t",
                (lambda e_, k=kt, bb=b:
                 e_.matmul(bk(bb), lat_bf[:, k * 128:(k + 1) * 128], wkvb_bf[:, :],
                           start=True, stop=True)),
                waits=(ev_lat, ev_wkvb, ev_kvtm[kt - 4] if kt >= 4 else None),
                sig="t")
            ev_kvtm[kt] = P.op(
                "v", (lambda e_, k=kt, bb=b: e_.tensor_copy(kvtm_all[:, k * 512:(k + 1) * 512], bk(bb))),
                waits=(ev_kvtm_mm[kt],), sig="v")
        ev_kvfm = [None] * 4
        for g in range(4):
            ev_mm = None
            for tc in range(4):
                ev_mm = P.op(
                    "t",
                    (lambda e_, gg=g, j=tc:
                     e_.matmul(bk(4 + j), wkvb_bf[:, gg * 128:(gg + 1) * 128],
                               lat_bf[:, j * 512:(j + 1) * 512], start=True, stop=True)),
                    waits=(ev_lat, ev_kvfm[g - 1] if g >= 1 else ev_kr_raw),
                    sig="t" if tc == 3 else None)
            ev_kvfm[g] = P.op(
                "v", (lambda e_, gg=g: e_.tensor_copy(ab(O_KVFM + gg * 4096, T), pbB[:, :])),
                waits=(ev_mm,), sig="v")

        qsh = ab(O_SPARE, T)
        ev_qh2 = [None] * 8
        ev_qh2cp = [None] * 8
        ev_qproj_last = None
        for jp in range(8):
            ev_mm = None
            for tc in range(4):
                b = tc if jp % 2 == 0 else 4 + tc
                for cc in range(NCT):
                    ev_mm = P.op(
                        "t",
                        (lambda e_, c=cc, j=tc, p_=jp, bb=b:
                         e_.matmul(bk(bb), wq_bf[:, c * D + p_ * 128: c * D + p_ * 128 + 128],
                                   xn_t(c, 1)[:, j * 512:(j + 1) * 512],
                                   start=(c == 0), stop=(c == NCT - 1))),
                        waits=(ev_xn1[cc], ev_wq,
                               ev_qh2cp[jp - 2] if jp >= 2 else
                               (ev_kvfm[3] if jp % 2 == 1 else ev_kvtm[NT - 1])),
                        sig="t" if cc == NCT - 1 else None)
            psrc = pbA if jp % 2 == 0 else pbB
            ev_qh2cp[jp] = P.op(
                "v", (lambda e_, p_=jp, ps=psrc: e_.tensor_scalar(
                    ab(O_QH2 + p_ * 4096, T), ps[:, :], 0.125, None, ALU.mult)),
                waits=(ev_mm, ev_kr if jp == 0 else None), sig="v")
            ev_qproj_last = ev_mm
            P.op("v", (lambda e_, p_=jp: e_.stream_shuffle(qsh[:, :], ab(O_QH2 + p_ * 4096, T), smask)),
                 waits=(ev_tab,))
            P.op("v", (lambda e_, p_=jp: e_.tensor_mul(ab(O_QH2 + p_ * 4096, T),
                                                       ab(O_QH2 + p_ * 4096, T), ccq[:, :])))
            P.op("v", (lambda e_: e_.tensor_mul(qsh[:, :], qsh[:, :], csq[:, :])))
            ev_qh2[jp] = P.op(
                "v", (lambda e_, p_=jp: e_.tensor_add(ab(O_QH2 + p_ * 4096, T),
                                                      ab(O_QH2 + p_ * 4096, T), qsh[:, :])),
                sig="v")

        ev_ebr_mm = P.op(
            "t", (lambda e_: e_.matmul(bk(0)[:, 0:8], ones_r[0:1, :], ebias_sb[0:1, :],
                                       start=True, stop=True)),
            waits=(ev_eb, ev_ones2, ev_qh2cp[6]), sig="t")
        ev_ebr = P.op("v", (lambda e_: e_.tensor_copy(ebias_rep[:, :], bk(0)[:, 0:8])),
                      waits=(ev_ebr_mm, ev_qh2cp[7]), sig="v")

        if upto == "pre":
            dump_X("X", tuple(ev_x1))
            _finish(P, sems)
            P.realize(nc, sems)
            return nc

        ev_exp_prev = [ev_ebr, ev_ebr]
        ev_pair_done = [None] * 8
        ev_kp_used = [None, None]
        ev_kp_ready = [None, None]
        ev_kp_last_dma = None
        ev_ygrp_prev = {}
        ev_ytm_free = None
        ev_ynorm_prev = [None, None]
        ev_ptevac_prev = [None, None]
        ev_av_prev = [None, None]
        ev_rl_read_prev = [None, None]
        ev_wo_done = [None] * 4
        ev_wo_evac = {}

        def kp_view(par, p0=0, p1=128):
            return ab(O_KP + par * 4096, T, p0, p1)

        def assemble_kp(jp, extra=()):
            nonlocal ev_kp_last_dma
            par = jp % 2
            last = None
            for (hh, dstlo) in ((2 * jp, 0), (2 * jp + 1, 64)):
                g, a = hh // 4, hh % 4
                src = ab(O_KVFM + g * 4096, T, a * 32, a * 32 + 32)
                last = P.dma("y", kp_view(par, dstlo, dstlo + 32), src,
                             waits=(ev_kvfm[g], ev_kp_used[par]) + tuple(extra))
                last = P.dma("y", kp_view(par, dstlo + 32, dstlo + 64), kr_b[32:64, :],
                             waits=(ev_kr,))
            ev_kp_last_dma = last
            return last

        ev_kp_ready[0] = assemble_kp(0, extra=(ev_qproj_last,))
        ev_kp_ready[1] = assemble_kp(1, extra=(ev_qproj_last,))

        pend_pt = []
        pend_rest = []

        ptev_stash = [None]

        def flush_ptev():
            if ptev_stash[0] is not None:
                ptev_stash[0]()
                ptev_stash[0] = None

        def flush_all():
            nonlocal pend_pt, pend_rest
            for fnp in pend_pt:
                fnp()
            pend_pt = []
            flush_ptev()
            for fnp in pend_rest:
                fnp()
            pend_rest = []

        def PS(p):
            return pbA if p == 0 else pbB

        ev_ptdone_prev = [None, None]
        ev_ptevac_prev2 = [None, None]
        ev_max_prev = [None, None]

        for jp in range(_ATTN_PAIRS):
            par = jp % 2
            for h2 in range(2):
                h = 2 * jp + h2
                base = 64 * h2
                for qt in range(_ATTN_QT):
                    p = qt % 2
                    W = (qt + 1) * 128
                    nch = (W + 511) // 512
                    cs = p * 4
                    ps = PS(p)
                    while len(pend_rest) >= 2:
                        pend_rest.pop(0)()
                    ev_smm = None
                    for j in range(nch):
                        n = min(512, W - j * 512)
                        last_chunk = (j == nch - 1)
                        ev_smm = P.op(
                            "t",
                            (lambda e_, jj=j, nn=n, p_=jp, b_=base, q_=qt, pr=par, ps_=ps, lc=last_chunk:
                             e_.matmul(ps_[:, jj * 512: jj * 512 + nn],
                                       ab(O_QH2 + p_ * 4096, T, b_, b_ + 64)[:, q_ * 128:(q_ + 1) * 128],
                                       kp_view(pr, b_, b_ + 64)[:, jj * 512: jj * 512 + nn],
                                       start=True, stop=not lc)),
                            waits=(ev_qh2[jp], ev_kp_ready[par], ev_exp_prev[p],
                                   ev_ptevac_prev[p], ev_ptevac_prev2[p],
                                   ev_ynorm_prev[p] if nch >= 3 else None),
                            sig=None)
                    ev_smm = P.op(
                        "t",
                        (lambda e_, q_=qt, ps_=ps:
                         e_.matmul(ps_[:, q_ * 128:(q_ + 1) * 128],
                                   ident_bf[:, :], trimask_bf[:, :],
                                   start=False, stop=True)),
                        waits=(ev_tri_bf, ev_id_bf), sig="t")
                    ev_pair_done[jp] = ev_smm
                    ev_kp_used[par] = ev_smm
                    for fnp in pend_pt:
                        fnp()
                    pend_pt = []
                    ev_max = P.op(
                        "v",
                        (lambda e_, w_=W, c=cs, ps_=ps: e_.reduce_max(colw[:, c:c + 1], ps_[:, 0:w_],
                                                                      axis=AX, negate=True)),
                        waits=(ev_smm,), hwaits=(ev_rl_read_prev[p],), sig="v")
                    ev_max_prev[p] = ev_max
                    flush_ptev()
                    ev_exp = P.op(
                        "s",
                        (lambda e_, w_=W, c=cs, pr=p, ps_=ps:
                         e_.activation(ab(O_P + pr * 4096, T)[:, 0:w_], ps_[:, 0:w_], ACT.Exp,
                                       bias=colw[:, c:c + 1], scale=1.0,
                                       accum_out=colw[:, c + 2:c + 3])),
                        waits=(ev_max, ev_ptdone_prev[p]),
                        sig="s")
                    ev_rl = P.op(
                        "v",
                        (lambda e_, c=cs: e_.reciprocal(colw[:, c + 3:c + 4], colw[:, c + 2:c + 3])),
                        waits=(ev_exp,), sig="v")
                    ev_exp_prev[p] = ev_exp

                    def make_pt(qt_=qt, pr=qt % 2, ev_exp_=ev_exp):
                        def tail_pt():
                            nkt = qt_ + 1
                            ps_ = PS(pr)
                            psb = ps_[:, 0:1024].bitcast(BF16)
                            ev_tr = None
                            for kt in range(nkt):
                                ev_tr = P.op(
                                    "t",
                                    (lambda e_, k=kt, pb=psb, p2=pr:
                                     e_.transpose(pb[:, k * 128:(k + 1) * 128],
                                                  ab(O_P + p2 * 4096, T)[:, k * 128:(k + 1) * 128],
                                                  ident_bf[:, :])),
                                    waits=(ev_exp_, ev_ptevac_prev[pr],
                                           ev_ptevac_prev2[pr], ev_id_bf),
                                    sig="t" if kt == nkt - 1 else None)
                            ev_ptdone_prev[pr] = ev_tr
                            n_all = nkt * 128
                            if _PTEV and n_all > 1024:
                                def emit_ptev(nc_=n_all, pb=psb, p2=pr, et=ev_tr):
                                    ev_ptevac_prev2[p2] = P.op(
                                        "v",
                                        (lambda e_:
                                         e_.tensor_copy(
                                             ab(O_PT + p2 * 4096, T)[:, 1024:nc_],
                                             pb[:, 1024:nc_])),
                                        waits=(et,), sig="v")
                                ptev_stash[0] = emit_ptev
                                n_s = 1024
                            else:
                                n_s = n_all
                            ev_pte = P.op(
                                "s",
                                (lambda e_, nc_=n_s, pb=psb, p2=pr:
                                 e_.copy(ab(O_PT + p2 * 4096, T)[:, 0:nc_], pb[:, 0:nc_])),
                                waits=(ev_tr,), sig="s")
                            ev_ptevac_prev[pr] = ev_pte
                            return ev_pte
                        return tail_pt

                    def make_rest(h_=h, qt_=qt, cs_=cs, h2_=h2, pr=qt % 2, ev_rl_=ev_rl):
                        def tail_rest():
                            nkt = qt_ + 1
                            ps_ = PS(pr)
                            ev_pte = ev_ptevac_prev[pr]
                            ev_pte2 = ev_ptevac_prev2[pr]
                            ev_mx = ev_max_prev[pr]
                            ev_av = None
                            for kt in range(nkt):
                                ev_av = P.op(
                                    "t",
                                    (lambda e_, k=kt, hh=h_, p2=pr, q2=qt_, ps2=ps_:
                                     e_.matmul(ps2[:, 1024:1056],
                                               ab(O_PT + p2 * 4096, T)[:, k * 128:(k + 1) * 128],
                                               kvtm_all[:, k * 512 + 32 * hh: k * 512 + 32 * hh + 32],
                                               start=(k == 0), stop=(k == q2))),
                                    waits=(ev_pte, ev_pte2, ev_kvtm[kt],
                                           ev_ynorm_prev[pr], ev_mx),
                                    sig="t" if kt == nkt - 1 else None)
                            ev_av_prev[pr] = ev_av
                            ev_yn = P.op(
                                "v",
                                (lambda e_, c=cs_, q2=qt_, hh2=h2_, ps2=ps_: e_.tensor_scalar(
                                    ab(O_YTM, 1024)[:, q2 * 64 + hh2 * 32: q2 * 64 + hh2 * 32 + 32],
                                    ps2[:, 1024:1056],
                                    colw[:, c + 3:c + 4], None, ALU.mult)),
                                waits=(ev_av, ev_ytm_free), hwaits=(ev_rl_,), sig="v")
                            ev_ynorm_prev[pr] = ev_yn
                            ev_rl_read_prev[pr] = ev_yn
                        return tail_rest

                    pend_pt.append(make_pt())
                    pend_rest.append(make_rest())
            if jp + 2 <= 7:
                ev_kp_ready[par] = assemble_kp(jp + 2)
            flush_all()
            rbase = 64 * (jp % 2)
            ev_tre = {}
            for qg in range(4):
                last_tr = None
                for qq in range(4):
                    qt_ = qg * 4 + qq
                    last_tr = P.op(
                        "t",
                        (lambda e_, q_=qt_, b_=qg, rb=rbase:
                         e_.transpose(pbA[:, b_ * 512: b_ * 512 + 512]
                                      .bitcast(BF16)[rb:rb + 64, (q_ % 4) * 128:(q_ % 4 + 1) * 128],
                                      ab(O_YTM, 1024)[:, q_ * 64:(q_ + 1) * 64],
                                      ident_bf[:, :], tile_position=(0, rb))),
                        waits=(ev_ynorm_prev[0], ev_ynorm_prev[1],
                               ev_exp_prev[0], ev_ptevac_prev[0],
                               ev_ptevac_prev2[0], ev_ygrp_prev.get(qg)),
                        sig="t" if qq == 3 else None)
                ev_tre[qg] = last_tr
            ev_ytm_free = ev_tre[3]
            for qg in range(4):
                ev_ygrp_prev[qg] = P.op(
                    "v",
                    (lambda e_, b_=qg, rb=rbase:
                     e_.tensor_copy(ab(O_YGRP, T, rb, rb + 64)[:, b_ * 512:(b_ + 1) * 512],
                                    pbA[:, b_ * 512: b_ * 512 + 512].bitcast(BF16)[rb:rb + 64, 0:512])),
                    waits=(ev_tre[qg],), sig="v")
            ev_exp_prev[0] = ev_ygrp_prev[3]
            if jp % 2 == 1 and _K_WO:
                g = jp // 2
                last_mm = None
                for cc in range(NCT):
                    for tc in range(4):
                        b = tc
                        last_mm = P.op(
                            "t",
                            (lambda e_, c=cc, j=tc, gg=g, bb=b:
                             e_.matmul(bk(bb),
                                       wo_bf[:, gg * D + c * 128: gg * D + c * 128 + 128],
                                       ab(O_YGRP, T)[:, j * 512:(j + 1) * 512],
                                       start=True, stop=True)),
                            waits=(ev_ygrp_prev[0], ev_ygrp_prev[1],
                                   ev_ygrp_prev[2], ev_ygrp_prev[3], ev_wo,
                                   ev_exp_prev[0], ev_exp_prev[1],
                                   ev_wo_evac.get((cc - 1, tc))),
                            sig="t")
                        ev_wo_evac[(cc, tc)] = P.op(
                            "v",
                            (lambda e_, c=cc, j=tc, bb=b:
                             e_.scalar_tensor_tensor(X[c][:, j * 512:(j + 1) * 512],
                                                     bk(bb), 1.0,
                                                     X[c][:, j * 512:(j + 1) * 512],
                                                     ALU.mult, ALU.add)),
                            waits=(last_mm,) + ((tuple(ev_x1dma)) if g == 0 else ()),
                            sig="v")
                ev_wo_done[g] = last_mm
                ev_exp_prev[0] = ev_wo_evac[(NCT - 1, 3)]
                ev_exp_prev[1] = ev_wo_evac[(NCT - 1, 3)]
                ev_wo_evac = {}

        flush_all()
        ev_attn_done = ev_wo_done[3]
        ev_attn_vdone = ev_exp_prev[0]

        if upto == "attn":
            dump_X("X", (ev_attn_vdone,))
            _finish(P, sems)
            P.realize(nc, sems)
            return nc

        m2row = af(O_M2ROW, T, 0, 1)
        mrep = af(O_MREP, T)
        ev_sq2p = [None, None]
        ev_sumsq2 = None
        ev_sumsq2_ct = [None] * NCT
        for ct in range(NCT):
            sqb = af(O_M2SQ[ct % 2], T)
            ev_sq2p[ct % 2] = P.op(
                "s", (lambda e_, c=ct, o=sqb: e_.activation(o, X[c][:, :], ACT.Square)),
                waits=(ev_attn_vdone, ev_attn_done,
                       ev_sumsq2_ct[ct - 2] if ct >= 2 else None,
                       ev_ptevac_prev[0] if ct == 0 else None,
                       ev_ptevac_prev[1] if ct == 0 else None),
                sig="s")
            for j in range(4):
                ev_sumsq2 = P.op(
                    "t",
                    (lambda e_, o=sqb, jj=j, c=ct:
                     e_.matmul(bk(jj)[0:1, :], ones_c[:, 0:1],
                               o[:, jj * 512:(jj + 1) * 512],
                               start=(c == 0), stop=(c == NCT - 1))),
                    waits=(ev_sq2p[ct % 2],),
                    sig="t" if j == 3 else None)
            ev_sumsq2_ct[ct] = ev_sumsq2
        ev_q2 = P.op("v", (lambda e_: e_.tensor_scalar(m2row, pbA[0:1, :], 1.0 / D, None, ALU.mult)),
                     waits=(ev_sumsq2, ev_kp_last_dma), sig="v")
        ev_q2e = P.op("v", (lambda e_: e_.tensor_scalar(m2row, m2row, LN_EPS, None, ALU.add)),
                      waits=(ev_q2,), sig="v")
        ev_l2 = P.op("s", (lambda e_: e_.activation(m2row, m2row, ACT.Ln)),
                     waits=(ev_q2e,), sig="s")
        ev_r2b = P.op("s", (lambda e_: e_.activation(m2row, m2row, ACT.Exp, scale=-0.5)),
                      waits=(ev_l2,), sig="s")
        ev_mrep = replicate(mrep, m2row, (ev_r2b,), ev_q2)
        ev_xn2 = [None] * NCT
        for ct in range(NCT):
            sqb = af(O_M2SQ[ct % 2], T)
            P.op("v", (lambda e_, c=ct, o=sqb: e_.tensor_mul(o, X[c][:, :], mrep)),
                 waits=(ev_mrep, ev_sumsq2))
            ev_xn2[ct] = P.op(
                "v", (lambda e_, c=ct, a=sqb: e_.tensor_scalar(
                    xn_t(c, 2), a, gam2[:, c:c + 1], None, ALU.mult)),
                waits=(ev_gam2,), sig="v")

        if upto == "rms2":
            dump_X("X", tuple(ev_xn2))
            _finish(P, sems)
            P.realize(nc, sems)
            return nc

        wgate_bf = ab(O_WGATE, NCT * 8)
        comb_fm = af(O_COMB, T, 0, 8)
        LG = af(O_LG, 128)
        PR = af(O_PR, 128)
        MS = af(O_MS, 128)
        MS2 = af(O_MS2, 128)

        def a3(t_, d1=16, d2=8, bcast_from=None):
            h_ = t_.tensor if hasattr(t_, "tensor") else None
            return t_

        def v3(base_off):
            return bass.AP(arena, base_off // 4 * 0 + 0, [[0, 0]])

        def lg3(off):
            return bass.AP(arena, off // 4, [[AR_BYTES // 4, 128], [8, 16], [1, 8]]).bitcast_hack \
                if False else None

        arf = arena[:, :].bitcast(F32)

        def t3(off):
            return bass.AP(arf.tensor, off // 4, [[AR_BYTES // 4, 128], [8, 16], [1, 8]]) \
                if False else None

        def ap3(off, last_stride=1):
            base = af(off, 128)
            t_ = base.tensor
            return bass.AP(t_, base.offset, [[base.ap[0][0], 128], [8 if last_stride else 0, 16], [last_stride, 8]])

        def apb(off, col):
            base = af(off, 128)
            t_ = base.tensor
            return bass.AP(t_, base.offset + col, [[base.ap[0][0], 128], [1, 16], [0, 8]])

        ev_gate_prev = ev_xn2[NCT - 1]
        ev_gmm = None
        for tt in range(NT):
            for cc in range(NCT):
                ev_gmm = P.op(
                    "t",
                    (lambda e_, c=cc, t_=tt:
                     e_.matmul(bk(6)[:, 8 * t_: 8 * t_ + 8], xn_t(c, 2)[:, t_ * 128:(t_ + 1) * 128],
                               wgate_bf[:, c * 8:(c + 1) * 8],
                               start=(c == 0), stop=(c == NCT - 1))),
                    waits=(ev_xn2[cc], ev_wgate,
                           ev_dn_evac_guard if False else None,
                           ev_attn_done if (tt == 0 and cc == 0) else None),
                    sig="t" if (tt == NT - 1 and cc == NCT - 1) else None)
        eb3 = bass.AP(ebias_rep, 0, [[8, 128], [0, 16], [1, 8]])
        ev_lg = P.op(
            "v", (lambda e_: e_.tensor_tensor(ap3(O_LG), bk(6)[:, 0:128].rearrange(
                "p (a b) -> p a b", b=8), eb3, ALU.add)),
            waits=(ev_gmm, ev_ebr), sig="v")
        ev_m8 = P.op("v", (lambda e_: e_.reduce_max(MS[:, 0:16], ap3(O_LG), axis=AX, negate=True)),
                     hwaits=(ev_lg,), sig="v")
        ev_sub = P.op("v", (lambda e_: e_.tensor_tensor(ap3(O_LG), ap3(O_LG), apb(O_MS, 0), ALU.add)),
                      hwaits=(ev_m8,), sig="v")
        ev_e8 = P.op("s", (lambda e_: e_.activation(LG, LG, ACT.Exp)),
                     waits=(ev_sub,), sig="s")
        ev_l8 = P.op("v", (lambda e_: e_.reduce_sum(MS[:, 16:32], ap3(O_LG), axis=AX)),
                     waits=(ev_e8,), sig="v")
        ev_rs = P.op("v", (lambda e_: e_.reciprocal(MS[:, 32:48], MS[:, 16:32])),
                     hwaits=(ev_l8,), sig="v")
        ev_pr = P.op("v", (lambda e_: e_.tensor_tensor(ap3(O_PR), ap3(O_LG), apb(O_MS, 32), ALU.mult)),
                     hwaits=(ev_rs,), sig="v")
        ev_m1 = P.op("v", (lambda e_: e_.reduce_max(MS[:, 0:16], ap3(O_PR), axis=AX)),
                     hwaits=(ev_pr,), sig="v")
        ev_mk1 = P.op("v", (lambda e_: e_.tensor_tensor(ap3(O_LG), ap3(O_PR), apb(O_MS, 0), ALU.is_ge)),
                      hwaits=(ev_m1,), sig="v")
        ev_p2 = P.op("v", (lambda e_: e_.scalar_tensor_tensor(ap3(O_MS2), ap3(O_LG), -1e9,
                                                              ap3(O_PR), ALU.mult, ALU.add)),
                     hwaits=(ev_mk1,), sig="v")
        ev_m2 = P.op("v", (lambda e_: e_.reduce_max(MS[:, 48:64], ap3(O_MS2), axis=AX)),
                     hwaits=(ev_p2,), sig="v")
        ev_mk2 = P.op("v", (lambda e_: e_.tensor_tensor(ap3(O_MS2), ap3(O_MS2), apb(O_MS, 48), ALU.is_ge)),
                      hwaits=(ev_m2,), sig="v")
        ev_mks = P.op("v", (lambda e_: e_.tensor_tensor(ap3(O_LG), ap3(O_LG), ap3(O_MS2), ALU.add)),
                      hwaits=(ev_mk2,), sig="v")
        ev_ws = P.op("v", (lambda e_: e_.tensor_add(MS[:, 64:80], MS[:, 0:16], MS[:, 48:64])),
                     hwaits=(ev_m2,), sig="v")
        ev_rw = P.op("v", (lambda e_: e_.reciprocal(MS[:, 80:96], MS[:, 64:80])),
                     hwaits=(ev_ws,), sig="v")
        ev_cb1 = P.op("v", (lambda e_: e_.tensor_tensor(ap3(O_PR), ap3(O_PR), ap3(O_LG), ALU.mult)),
                      hwaits=(ev_mks,), sig="v")
        ev_comb = P.op("v", (lambda e_: e_.tensor_tensor(ap3(O_PR), ap3(O_PR), apb(O_MS, 80), ALU.mult)),
                       hwaits=(ev_rw, ev_cb1), sig="v")
        ev_combevac = None
        ev_ctr_prev = None
        for tt in range(NT):
            ev_ctr_prev = P.op(
                "t",
                (lambda e_, t_=tt: e_.transpose(pbB[0:8, 1536 + (t_ % 4) * 128: 1536 + (t_ % 4 + 1) * 128],
                                                PR[:, 8 * t_: 8 * t_ + 8], ident_f[:, :])),
                waits=(ev_comb, ev_combevac if tt % 4 == 0 and tt > 0 else None,
                       ev_attn_done if tt == 0 else None),
                sig="t")
            if tt % 4 == 3:
                ev_combevac = P.op(
                    "v",
                    (lambda e_, t_=tt: e_.tensor_copy(comb_fm[0:8, (t_ - 3) * 128:(t_ + 1) * 128],
                                                      pbB[0:8, 1536:2048])),
                    waits=(ev_ctr_prev,), sig="v")
        ev_comb_done = ev_combevac

        if upto == "gate":
            dump_X("X", (ev_comb_done,))
            if "comb" in dbg:
                P.dma("y", dbg["comb"][:, :], comb_fm[0:8, :], waits=(ev_comb_done,))
            _finish(P, sems)
            P.realize(nc, sems)
            return nc

        def wslot(i, n):
            return ab(MOE_WSLOT[i], n)

        slot_free = {0: ev_qh2[7], 1: ev_qh2[7], 2: ev_qh2[7],
                     3: ev_pair_done[5], 4: ev_pair_done[5], 5: ev_pair_done[5]}
        hfree = {i: ev_attn_done for i in range(9)}
        sgb = af(O_SG, 512)
        combrep = af(O_COMBREP, 512)
        csel_sb = af(O_CSEL, 1024, 0, 8)
        ev_csel = P.dma("y", csel_sb, d_csel, waits=(ev_attn_done,))
        ev_wg_dma = {}
        ev_wu_dma = {}
        ev_wd_dma = {}
        ev_hall = {}
        ev_prev_silu = None
        ev_down_read = None
        ev_hwr = {}
        ev_prev_hwr = None
        ev_sil_done = {}
        ev_umul_done = {}
        ev_dn_evac = {}
        ev_dn_free = {}
        ev_dn_last = {}
        ev_cre_prev = None
        ev_dn_done = None

        for tc in range(_K_MOE_TC):
            for e9 in range(_K_MOE_E):
                k = tc * 9 + e9
                sgu = (2 * k) % 4
                suu = sgu + 1
                ev_wg_dma[k] = P.dma("g", wslot(sgu, NCT * 512), d_wgu[e9],
                                     waits=(slot_free[sgu],))
                ev_wu_dma[k] = P.dma("g", wslot(suu, NCT * 512), d_wuu[e9],
                                     waits=(slot_free[suu],))
                if e9 < 8:
                    ev_crm = P.op(
                        "t",
                        (lambda e_, ee=e9, j=tc:
                         e_.matmul(bk(3), csel_sb[:, ee * 128:(ee + 1) * 128],
                                   comb_fm[0:8, j * 512:(j + 1) * 512],
                                   start=True, stop=True)),
                        waits=(ev_comb_done, ev_cre_prev, ev_csel,
                               ev_dn_free.get((tc, 3))), sig="t")
                    ev_cre = P.op(
                        "v", (lambda e_: e_.tensor_copy(combrep[:, :], bk(3))),
                        waits=(ev_crm,), sig="v")
                    ev_cre_prev = ev_cre
                else:
                    ev_cre = None
                ev_gmm = None
                ev_umm = None
                for ic in range(4):
                    bg = 4 + 2 * (ic % 2)
                    bu = bg + 1
                    for cc in range(NCT):
                        ev_gmm = P.op(
                            "t",
                            (lambda e_, c=cc, i_=ic, j=tc, s_=sgu, b_=bg:
                             e_.matmul(bk(b_),
                                       wslot(s_, NCT * 512)[:, c * 512 + i_ * 128: c * 512 + i_ * 128 + 128],
                                       xn_t(c, 2)[:, j * 512:(j + 1) * 512],
                                       start=(c == 0), stop=(c == NCT - 1))),
                            waits=(ev_wg_dma[k], ev_xn2[cc],
                                   ev_sil_done.get((k, ic - 2)) or ev_sil_done.get((k - 1, ic + 2)),
                                   ev_dn_free.get((tc, bg)) if e9 == 0 else None,
                                   ev_hwr.get((k - 1, ic))),
                            sig="t" if cc == NCT - 1 else None)
                    for cc in range(NCT):
                        ev_umm = P.op(
                            "t",
                            (lambda e_, c=cc, i_=ic, j=tc, s_=suu, b_=bu:
                             e_.matmul(bk(b_),
                                       wslot(s_, NCT * 512)[:, c * 512 + i_ * 128: c * 512 + i_ * 128 + 128],
                                       xn_t(c, 2)[:, j * 512:(j + 1) * 512],
                                       start=(c == 0), stop=(c == NCT - 1))),
                            waits=(ev_wu_dma[k],
                                   ev_umul_done.get((k, ic - 2)) or ev_umul_done.get((k - 1, ic + 2)),
                                   ev_dn_free.get((tc, bu)) if e9 == 0 else None),
                            sig="t" if cc == NCT - 1 else None)
                    ev_sl = P.op(
                        "s", (lambda e_, b_=bg: e_.activation(sgb, bk(b_), ACT.Silu)),
                        waits=(ev_gmm, ev_prev_silu, ev_prev_hwr), sig="s")
                    ev_prev_silu = ev_sl
                    ev_sil_done[(k, ic)] = ev_sl
                    ev_umul_done[(k, ic)] = P.op(
                        "v", (lambda e_, b_=bu: e_.tensor_mul(sgb, bk(b_), sgb)),
                        waits=(ev_sl, ev_umm), sig="v")
                    hdst = ab(MOE_H[e9], T)
                    if e9 < 8:
                        ev_h = P.op(
                            "v",
                            (lambda e_, i_=ic, hd=hdst:
                             e_.tensor_mul(hd[:, i_ * 512:(i_ + 1) * 512], sgb, combrep[:, :])),
                            waits=(ev_cre,
                                   hfree[e9] if ic == 0 else None,
                                   ev_down_read if ic == 0 else None),
                            sig="v")
                    else:
                        ev_h = P.op(
                            "v",
                            (lambda e_, i_=ic, hd=hdst:
                             e_.tensor_copy(hd[:, i_ * 512:(i_ + 1) * 512], sgb)),
                            waits=(hfree[e9] if ic == 0 else None,
                                   ev_down_read if ic == 0 else None),
                            sig="v")
                    ev_hwr[(k, ic)] = ev_h
                    ev_prev_hwr = ev_h
                slot_free[sgu] = ev_gmm
                slot_free[suu] = ev_umm
                ev_hall[(tc, e9)] = ev_hwr[(k, 3)]
            for e9 in range(2):
                k = tc * 9 + e9
                sdd = 4 + e9 % 2
                ev_wd_dma[k] = P.dma("g", wslot(sdd, 4 * D), d_wdd[e9],
                                     waits=(slot_free[sdd],))
            ev_dn_last = {}
            for e9 in range(_K_MOE_E):
                k = tc * 9 + e9
                sdd = 4 + e9 % 2
                ev_mm_e = None
                for cc in range(NCT):
                    for ic in range(4):
                        ev_mm_e = P.op(
                            "t",
                            (lambda e_, c=cc, i_=ic, ee=e9, s_=sdd:
                             e_.matmul(bk(c),
                                       wslot(s_, 4 * D)[:, i_ * D + c * 128: i_ * D + c * 128 + 128],
                                       ab(MOE_H[ee], T)[:, i_ * 512:(i_ + 1) * 512],
                                       start=(ee == 0 and i_ == 0),
                                       stop=(ee == _K_MOE_E - 1 and i_ == 3))),
                            waits=(ev_wd_dma[k], ev_hall[(tc, e9)],
                                   ev_dn_free.get((tc, cc)),
                                   ev_cre_prev if (cc == 3 and e9 == 0) else None,
                                   ev_prev_silu if (cc >= 4 and e9 == 0) else None,
                                   ev_prev_hwr if (cc >= 4 and e9 == 0) else None),
                            sig="t" if ic == 3 else None)
                    ev_dn_last[cc] = ev_mm_e
                slot_free[sdd] = ev_mm_e
                if e9 + 2 < _K_MOE_E:
                    k2 = tc * 9 + e9 + 2
                    ev_wd_dma[k2] = P.dma("g", wslot(sdd, 4 * D), d_wdd[e9 + 2],
                                          waits=(ev_mm_e,))
            ev_down_read = ev_mm_e
            for i in range(9):
                hfree[i] = None
            for cc in range(NCT):
                ev_e = P.op(
                    "v",
                    (lambda e_, c=cc, j=tc:
                     e_.scalar_tensor_tensor(X[c][:, j * 512:(j + 1) * 512], bk(c), 1.0,
                                             X[c][:, j * 512:(j + 1) * 512], ALU.mult, ALU.add)),
                    waits=(ev_dn_last[cc],), sig="v")
                ev_dn_free[(tc + 1, cc)] = ev_e
                ev_dn_evac[cc % 3] = ev_e
            ev_dn_done = ev_dn_evac[(NCT - 1) % 3]

        ev_moe_done = ev_dn_done

        if upto == "moe":
            dump_X("X", (ev_moe_done, ev_dn_evac.get(0), ev_dn_evac.get(1), ev_dn_evac.get(2)))
            _finish(P, sems)
            P.realize(nc, sems)
            return nc

        ev_dn_all = (ev_dn_evac.get(0), ev_dn_evac.get(1), ev_dn_evac.get(2))
        ev_x1b_free = [None, None]
        ev_xadd = [None] * NCT
        for ct in range(NCT):
            sl2 = ct % 2
            x1b = af(O_X1B + sl2 * 8192, T)
            ev_ld = P.dma("y", x1b, x1sc[ct * 128:(ct + 1) * 128, :],
                          waits=(ev_x1b_free[sl2], ev_moe_done,
                                 ev_dn_evac.get(0), ev_dn_evac.get(1), ev_dn_evac.get(2)))
            ev_xadd[ct] = P.op(
                "v", (lambda e_, c=ct, b_=x1b: e_.tensor_add(X[c][:, :], X[c][:, :], b_)),
                waits=(ev_ld,), sig="v")
            ev_x1b_free[sl2] = ev_xadd[ct]
        ev_tm_free = [None, None]
        ev_tr_prev = {0: None, 1: None}
        ev_ln2_first = tuple(x for x in ev_dn_all if x is not None)
        for tt in range(NT):
            sl = tt % 2
            tmb = af(O_TM + sl * 4096, D)
            sqb = af(O_SQ2 + sl * 4096, D)
            ev_tr = {}
            for cc in range(NCT):
                b = cc // 4
                ev_tr[b] = P.op(
                    "t",
                    (lambda e_, c=cc, t_=tt, b_=b:
                     e_.transpose(bk(b_)[:, (c % 4) * 128:(c % 4 + 1) * 128],
                                  X[c][:, t_ * 128:(t_ + 1) * 128], ident_f[:, :])),
                    waits=ev_ln2_first + (ev_xadd[cc],) + ((ev_tr_prev[b],) if cc % 4 == 0 else ()),
                    sig="t" if cc % 4 == 3 else None)
            ev_ln2_first = ()
            ev_cpa = P.op(
                "v", (lambda e_, o=tmb: e_.tensor_copy(o[:, 0:512], bk(0))),
                waits=(ev_tr[0], ev_tm_free[sl]), sig="v")
            ev_cpb = P.op(
                "v", (lambda e_, o=tmb: e_.tensor_copy(o[:, 512:1024], bk(1))),
                waits=(ev_tr[1],), sig="v")
            ev_tr_prev[0] = ev_cpa
            ev_tr_prev[1] = ev_cpb
            ev_sqq = P.op(
                "s", (lambda e_, a=tmb, o=sqb: e_.activation(o, a, ACT.Square)),
                waits=(ev_cpa, ev_cpb, ev_tm_free[sl]), sig="s")
            cb = 8 + 6 * sl
            ev_s1 = P.op("v", (lambda e_, a=tmb, c=cb: e_.reduce_sum(colw[:, c:c + 1], a, axis=AX)),
                         waits=(ev_cpb,), sig="v")
            ev_s2 = P.op("v", (lambda e_, o=sqb, c=cb: e_.reduce_sum(colw[:, c + 1:c + 2], o, axis=AX)),
                         waits=(ev_sqq,), sig="v")
            ev_c1 = P.op("v", (lambda e_, c=cb: e_.tensor_scalar(colw[:, c + 2:c + 3], colw[:, c:c + 1],
                                                                 1.0 / D, None, ALU.mult)),
                         hwaits=(ev_s1,), sig="v")
            ev_c2 = P.op("v", (lambda e_, c=cb: e_.tensor_scalar(colw[:, c + 3:c + 4], colw[:, c + 1:c + 2],
                                                                 1.0 / D, None, ALU.mult)),
                         hwaits=(ev_s2,), sig="v")
            ev_c3 = P.op("v", (lambda e_, c=cb: e_.tensor_mul(colw[:, c + 4:c + 5], colw[:, c + 2:c + 3],
                                                              colw[:, c + 2:c + 3])),
                         hwaits=(ev_c1,), sig="v")
            ev_c4 = P.op("v", (lambda e_, c=cb: e_.tensor_sub(colw[:, c + 3:c + 4], colw[:, c + 3:c + 4],
                                                              colw[:, c + 4:c + 5])),
                         hwaits=(ev_c2, ev_c3), sig="v")
            ev_c5 = P.op("v", (lambda e_, c=cb: e_.tensor_scalar(colw[:, c + 3:c + 4], colw[:, c + 3:c + 4],
                                                                 LN_EPS, None, ALU.add)),
                         hwaits=(ev_c4,), sig="v")
            ev_lnx = P.op("s", (lambda e_, c=cb: e_.activation(colw[:, c + 4:c + 5], colw[:, c + 3:c + 4],
                                                               ACT.Ln)),
                          waits=(ev_c5,), sig="s")
            ev_rx = P.op("s", (lambda e_, c=cb: e_.activation(colw[:, c + 4:c + 5], colw[:, c + 4:c + 5],
                                                              ACT.Exp, scale=-0.5)),
                         waits=(ev_lnx,), sig="s")
            ev_app = P.op(
                "v",
                (lambda e_, a=tmb, o=sqb, c=cb: e_.tensor_scalar(o, a, colw[:, c + 2:c + 3],
                                                                 colw[:, c + 4:c + 5],
                                                                 ALU.subtract, ALU.mult)),
                waits=(ev_rx,), hwaits=(ev_c1,), sig="v")
            ev_out = P.dma("y", out[tt * 128:(tt + 1) * 128, :], sqb, waits=(ev_app,))
            ev_tm_free[sl] = ev_out

        _finish(P, sems)
        P.realize(nc, sems)

    return nc


def _finish(P, sems):
    for sem in QSEMS:
        val = P.cnt[sem]
        if val:
            P.op("y", (lambda e_, s_=sem, v_=val: e_.wait_ge(sems[s_], v_)))
            P.op("t", (lambda e_, s_=sem, v_=val: e_.wait_ge(sems[s_], v_)))



_DEBUG_DUMP = ()


def kernel(**inputs):
    consts = _host_consts()
    weights = _host_weights(inputs)
    x_full = np.ascontiguousarray(np.asarray(inputs["L0_attn_x_t"], dtype=np.float32))

    nc = build_program(upto=_BUILD_UPTO, debug_dump=_DEBUG_DUMP)

    base = {}
    base.update({k: np.ascontiguousarray(v) for k, v in consts.items()})
    base.update({k: np.ascontiguousarray(v) for k, v in weights.items()})
    in_maps = []
    for core in range(8):
        m = dict(base)
        m["xin"] = np.ascontiguousarray(x_full[core])
        in_maps.append(m)

    res = run_bass_kernel_spmd(nc, in_maps, core_ids=list(range(8)))
    if _DEBUG_DUMP:
        kernel._last_results = res.results
    outp = np.stack([res.results[i]["out"] for i in range(8)], axis=0)
    return outp


# version 16
# speedup vs baseline: 1.0361x; 1.0361x over previous
import numpy as np
import ml_dtypes
from contextlib import ExitStack

import concourse.bass as bass
import concourse.mybir as mybir
from concourse.bass_utils import run_bass_kernel_spmd

F32 = mybir.dt.float32
BF16 = mybir.dt.bfloat16
AX = mybir.AxisListType.X
ALU = mybir.AluOpType
ACT = mybir.ActivationFunctionType
BFNP = ml_dtypes.bfloat16

T = 2048
D = 1024
NT = T // 128
NCT = D // 128
THETA = 1e5
LN_EPS = 1e-5
GEPS = 0.5 + 1e-8

_BUILD_UPTO = "full"
import os as _os
_ATTN_PAIRS = int(_os.environ.get("K_ATTN_PAIRS", "8"))
_ATTN_QT = int(_os.environ.get("K_ATTN_QT", "16"))
_K_WO = int(_os.environ.get("K_WO", "1"))
_K_MOE_TC = int(_os.environ.get("K_MOE_TC", "4"))
_K_MOE_E = int(_os.environ.get("K_MOE_E", "9"))
_PTEV = int(_os.environ.get("K_PTEV", "1"))
_STAGES = ("mingru", "ln1", "pre", "attn", "rms2", "gate", "moe", "full")


def _stage_ge(a, b):
    return _STAGES.index(a) >= _STAGES.index(b)



def _host_consts():
    c = {}
    c["cident_bf"] = np.eye(128, dtype=BFNP)
    c["cident_f"] = np.eye(128, dtype=np.float32)
    tri = np.where(np.arange(128)[None, :] <= np.arange(128)[:, None], 0.0, -1e9)
    c["ctrimask"] = tri.astype(np.float32)
    c["ctrimask_bf"] = tri.astype(BFNP)
    csel = np.zeros((8, 8 * 128), np.float32)
    for e in range(8):
        csel[e, e * 128:(e + 1) * 128] = 1.0
    c["csel"] = csel
    t = np.arange(T, dtype=np.float64)
    freqs = THETA ** (-np.arange(16, dtype=np.float64) / 16.0)
    ang = np.outer(freqs, t)
    cosv, sinv = np.cos(ang), np.sin(ang)
    C = np.zeros((128, T), np.float64)
    S = np.zeros((128, T), np.float64)
    for base in (32, 96):
        C[base:base + 16] = cosv
        C[base + 16:base + 32] = cosv
        S[base:base + 16] = -sinv
        S[base + 16:base + 32] = sinv
    C[0:32] = 1.0
    C[64:96] = 1.0
    c["ccq"] = C.astype(BFNP)
    c["csq"] = S.astype(BFNP)
    return c


def _q_col_perm():
    perm = np.empty(D, np.int64)
    for j in range(D):
        jp, r = divmod(j, 128)
        h = 2 * jp + r // 64
        rr = r % 64
        if rr < 32:
            o = 64 * h + rr
        elif rr < 48:
            o = 64 * h + 32 + 2 * (rr - 32)
        else:
            o = 64 * h + 32 + 2 * (rr - 48) + 1
        perm[j] = o
    return perm


def _host_weights(ins):
    w = {}
    for nm, key in (("wbd_f", "L0_attn_Wf"), ("wbd_i", "L0_attn_Wi"),
                    ("wbd_h", "L0_attn_Wh")):
        W = np.asarray(ins[key], dtype=np.float32)
        outw = np.zeros((128, NCT * 128), np.float32)
        for ct in range(NCT):
            for gl in range(16):
                g = 16 * ct + gl
                outw[8 * gl:8 * gl + 8, ct * 128 + 8 * gl: ct * 128 + 8 * gl + 8] = W[g].T
        w[nm] = outw

    Wq = np.asarray(ins["L0_ffn_Wq"], dtype=np.float32)[:, _q_col_perm()]
    w["wq"] = Wq.reshape(NCT, 128, D).transpose(1, 0, 2).reshape(128, NCT * D)

    Wkva = np.asarray(ins["L0_ffn_Wkva"], dtype=np.float32)
    kva = np.zeros((D, 192), np.float32)
    kva[:, 0:128] = Wkva[:, 0:128]
    kva[:, 160:176] = Wkva[:, 128:160][:, 0::2]
    kva[:, 176:192] = Wkva[:, 128:160][:, 1::2]
    w["wkva"] = kva.reshape(NCT, 128, 192).transpose(1, 0, 2).reshape(128, NCT * 192)

    w["wkvb"] = np.asarray(ins["L0_ffn_Wkvb"], dtype=np.float32)

    Wo = np.asarray(ins["L0_ffn_Wo"], dtype=np.float32)
    rows = np.empty(512, np.int64)
    for g in range(4):
        for a in range(4):
            h = 4 * g + a
            rows[g * 128 + a * 32: g * 128 + a * 32 + 32] = 64 * h + np.arange(32)
    w["wo"] = Wo[rows].reshape(4, 128, D).transpose(1, 0, 2).reshape(128, 4 * D)

    w["wgate"] = np.asarray(ins["L0_ffn_Wgate"], dtype=np.float32) \
        .reshape(NCT, 128, 8).transpose(1, 0, 2).reshape(128, NCT * 8)

    gg = np.concatenate([np.asarray(ins["L0_ffn_Wg"], dtype=np.float32),
                         np.asarray(ins["L0_ffn_sWg"], dtype=np.float32)], 0)
    uu = np.concatenate([np.asarray(ins["L0_ffn_Wu"], dtype=np.float32),
                         np.asarray(ins["L0_ffn_sWu"], dtype=np.float32)], 0)
    dd = np.concatenate([np.asarray(ins["L0_ffn_Wd"], dtype=np.float32),
                         np.asarray(ins["L0_ffn_sWd"], dtype=np.float32)], 0)
    w["wgu"] = gg.reshape(9, NCT, 128, 512).transpose(0, 2, 1, 3).reshape(9, 128, NCT * 512)
    w["wuu"] = uu.reshape(9, NCT, 128, 512).transpose(0, 2, 1, 3).reshape(9, 128, NCT * 512)
    w["wdd"] = dd.reshape(9, 4, 128, D).transpose(0, 2, 1, 3).reshape(9, 128, 4 * D)

    w["wln1"] = np.asarray(ins["L0_ffn_w_ln1"], dtype=np.float32)
    w["wln2"] = np.asarray(ins["L0_ffn_w_ln2"], dtype=np.float32)
    w["ebias"] = np.asarray(ins["L0_ffn_expert_bias"], dtype=np.float32)
    return w



ENGS = ("t", "v", "s", "g", "y")
QLANES = {"y": 4, "g": 4, "s": 2}
QSEMS = [f"q{q}{i}" for q, n in QLANES.items() for i in range(n)]


class Prog:
    def __init__(self):
        self.stream = {e: [] for e in ENGS}
        self.cnt = {k: 0 for k in ("t", "v", "s", "g", *QSEMS)}
        self.lane_rr = {q: 0 for q in QLANES}

    def op(self, eng, fn, waits=(), sig=None, hwaits=()):
        w = tuple(ev for ev in waits if ev is not None)
        hw = tuple(ev for ev in hwaits if ev is not None)
        for ev_ in w + hw:
            assert isinstance(ev_, tuple) and len(ev_) == 2 and isinstance(ev_[1], int), \
                f"bad wait event: {ev_!r}"
        ev = None
        if sig is not None:
            self.cnt[sig] += 16 if sig.startswith("q") else 1
            ev = (sig, self.cnt[sig])
        self.stream[eng].append((w, fn, sig, hw))
        return ev

    def dma(self, queue, dst, src, waits=(), slow=False):
        lane = self.lane_rr[queue]
        self.lane_rr[queue] = (lane + 1) % QLANES[queue]
        sem = f"q{queue}{lane}"
        prev = self.cnt[sem]
        w = tuple(waits) + (((sem, prev),) if prev else ())
        return self.op(queue,
                       (lambda e, d=dst, s_=src, sl=slow:
                        e.dma_start(d, s_, allow_slow_non_contiguous=sl)),
                       waits=w, sig=sem)

    def realize(self, nc, sems):
        waited = {e: {} for e in ENGS}

        def runner(ename):
            def run(eng):
                wd = waited[ename]
                for (w, fn, sig, hw) in self.stream[ename]:
                    for (snm, val) in w:
                        if snm == ename and not (ename == "s" and snm == "s"):
                            continue
                        if wd.get(snm, -1) >= val:
                            continue
                        eng.wait_ge(sems[snm], val)
                        wd[snm] = val
                    for (snm, val) in hw:
                        if wd.get(snm, -1) >= val:
                            continue
                        eng.wait_ge(sems[snm], val)
                        wd[snm] = val
                    ins = fn(eng)
                    if sig is not None:
                        ins.then_inc(sems[sig], 16 if sig.startswith("q") else 1)
            return run

        with nc.Block() as block:
            block.tensor(runner("t"))
            block.vector(runner("v"))
            block.scalar(runner("s"))
            block.gpsimd(runner("g"))
            block.sync(runner("y"))


AR_BYTES = 144832

O_WGATE = 0
O_WQ = 128
O_WKVA = 16512
O_WKVB = 19584
O_CCQ = 20608
O_CSQ = 24704
O_WO = 28800

O_WBD = 36992
O_XBF = 43136
O_MGBUF = 51328
O_XSTAGE = 116864

O_XN = 36992
O_LN_ROWS = 69760
O_LN_REPS = 102528
O_LN_SQ = 127104

O_KP = 36992
O_P = 45184
O_PT = 53376
O_YGRP = 61568
O_SPARE = 65664
O_QH2 = 69760
O_KVFM = 102528
O_KVTM = 118912
O_LAT = 135296
O_KR = 139392

MOE_WSLOT = [128, 8320, 16512, 69760, 77952, 86144]
MOE_H = [36992, 41088, 45184, 49280, 53376, 57472, 61568, 94336, 98432]
XN2_OFF = [24704, 28800, 32896, 102528, 106624, 110720, 114816, 118912]
O_M2SQ = [45184, 53376]
O_M2ROW = 61568
O_MREP = 36992
O_YTM = 65664
O_COMB = 123008
O_LG = 65664
O_PR = 66176
O_MS = 66688
O_MS2 = 67200
O_COMBREP = 67712
O_SG = 135296
O_CSEL = 131200

O_TM = 128
O_SQ2 = 8320
O_X1B = 16512


def build_program(upto="full", debug_dump=()):
    nc = bass.Bass("TRN2", target_bir_lowering=False, detect_race_conditions=False)

    def din(name, shape, dt=F32):
        return nc.dram_tensor(name, list(shape), dt, kind="ExternalInput").ap()

    xin = din("xin", (T, D))
    d_cident_bf = din("cident_bf", (128, 128), BF16)
    d_cident_f = din("cident_f", (128, 128))
    d_ctrimask = din("ctrimask", (128, 128))
    d_ctrimask_bf = din("ctrimask_bf", (128, 128), BF16)
    d_csel = din("csel", (8, 8 * 128))
    d_ccq = din("ccq", (128, T), BF16)
    d_csq = din("csq", (128, T), BF16)
    d_wbd = {k: din("wbd_" + k, (128, NCT * 128)) for k in "fih"}
    d_wq = din("wq", (128, NCT * D))
    d_wkva = din("wkva", (128, NCT * 192))
    d_wkvb = din("wkvb", (128, 512))
    d_wo = din("wo", (128, 4 * D))
    d_wgate = din("wgate", (128, NCT * 8))
    d_wgu = din("wgu", (9, 128, NCT * 512))
    d_wuu = din("wuu", (9, 128, NCT * 512))
    d_wdd = din("wdd", (9, 128, 4 * D))
    d_wln1 = din("wln1", (D,))
    d_wln2 = din("wln2", (D,))
    d_ebias = din("ebias", (8,))

    out = nc.dram_tensor("out", [T, D], F32, kind="ExternalOutput").ap()
    x1sc = nc.dram_tensor("x1sc", [D, T], F32, kind="Internal").ap()
    dbg = {}
    for nm in debug_dump:
        shape = [8, T] if nm == "comb" else [D, T]
        dbg[nm] = nc.dram_tensor("dbg_" + nm, list(shape), F32, kind="ExternalOutput").ap()

    P = Prog()

    with ExitStack() as ctx:
        e = ctx.enter_context

        X = [e(nc.sbuf_tensor(f"X{i}", [128, T], F32)) for i in range(NCT)]
        ident_bf = e(nc.sbuf_tensor("ident_bf", [128, 128], BF16))
        ident_f = e(nc.sbuf_tensor("ident_f", [128, 128], F32))
        trimask = e(nc.sbuf_tensor("trimask", [128, 128], F32))
        trimask_bf = e(nc.sbuf_tensor("trimask_bf", [128, 128], BF16))
        gam1 = e(nc.sbuf_tensor("gam1", [128, NCT], F32))
        gam2 = e(nc.sbuf_tensor("gam2", [128, NCT], F32))
        ebias_sb = e(nc.sbuf_tensor("ebias_sb", [1, 8], F32))
        ebias_rep = e(nc.sbuf_tensor("ebias_rep", [128, 8], F32))
        colw = e(nc.sbuf_tensor("colw", [128, 24], F32))
        ones_c = e(nc.sbuf_tensor("ones_c", [128, 1], F32))
        ones_r = e(nc.sbuf_tensor("ones_r", [1, 128], F32))
        ysc = e(nc.sbuf_tensor("ysc", [128, 64], BF16))
        arena = e(nc.sbuf_tensor("arena", [128, AR_BYTES // 2], BF16))

        pbA = e(nc.psum_tensor("pbA", [128, 2048], F32))
        pbB = e(nc.psum_tensor("pbB", [128, 2048], F32))

        def bk(i):
            t_ = pbA if i < 4 else pbB
            j = i % 4
            return t_[:, j * 512:(j + 1) * 512]

        def bkb(i):
            t_ = pbA if i < 4 else pbB
            j = i % 4
            return t_[:, j * 512:(j + 1) * 512].bitcast(BF16)

        sems = {k: e(nc.semaphore("sem_" + k))
                for k in ("t", "v", "s", "g", *QSEMS)}

        def ab(off, n, p0=0, p1=128):
            return arena[p0:p1, off // 2: off // 2 + n]

        def af(off, n, p0=0, p1=128):
            assert off % 4 == 0
            return arena[p0:p1, off // 2: off // 2 + 2 * n].bitcast(F32)

        ev_id_bf = P.dma("y", ident_bf[:, :], d_cident_bf)
        ev_id_f = P.dma("y", ident_f[:, :], d_cident_f)
        P.dma("y", trimask[:, :], d_ctrimask)
        ev_tri_bf = P.dma("y", trimask_bf[:, :], d_ctrimask_bf)
        P.dma("y", ab(O_CCQ, T), d_ccq)
        ev_tab = P.dma("y", ab(O_CSQ, T), d_csq)
        P.dma("y", gam1[:, :], d_wln1.rearrange("(a b) -> b a", b=128), slow=True)
        ev_gam2 = P.dma("y", gam2[:, :], d_wln2.rearrange("(a b) -> b a", b=128), slow=True)
        ev_eb = P.dma("y", ebias_sb[0:1, :], d_ebias.rearrange("(o a) -> o a", o=1))
        ev_ones = P.op("g", lambda e_: e_.memset(ones_c[:, :], 1.0), sig="g")
        ev_ones2 = P.op("g", lambda e_: e_.memset(ones_r[:, :], 1.0), sig="g")

        ev_wbd = {}
        for i, k in enumerate("fih"):
            ev_wbd[k] = P.dma("g", ab(O_WBD + i * 2048, NCT * 128), d_wbd[k])
        ev_wq = P.dma("g", ab(O_WQ, NCT * D), d_wq)
        ev_wkva = P.dma("g", ab(O_WKVA, NCT * 192), d_wkva)
        ev_wkvb = P.dma("g", ab(O_WKVB, 512), d_wkvb)
        ev_wo = P.dma("g", ab(O_WO, 4 * D), d_wo)
        ev_wgate = P.dma("g", ab(O_WGATE, NCT * 8), d_wgate)

        ev_xready = [None] * NCT
        ev_stage_free = [None] * 4
        bankfree = {j: None for j in range(8)}
        for grp in range(4):
            ev_st = []
            for j in range(4):
                tt = grp * 4 + j
                st = af(O_XSTAGE + j * 4096, D)
                ev_st.append(P.dma("y", st, xin[tt * 128:(tt + 1) * 128, :],
                                   waits=(ev_stage_free[j],)))
            for ct in range(NCT):
                bank = ct % 4
                evt = None
                for j in range(4):
                    st = af(O_XSTAGE + j * 4096, D)
                    evt = P.op(
                        "t",
                        (lambda e_, b=bank, jj=j, s_=st, c=ct:
                         e_.transpose(bk(b)[:, jj * 128:(jj + 1) * 128],
                                      s_[:, c * 128:(c + 1) * 128], ident_f[:, :])),
                        waits=(ev_st[j], bankfree[bank], ev_id_f),
                        sig="t" if j == 3 else None)
                ev_ev = P.op(
                    "v",
                    (lambda e_, b=bank, c=ct, g=grp:
                     e_.tensor_copy(X[c][:, g * 512:(g + 1) * 512], bk(b))),
                    waits=(evt,), sig="v")
                bankfree[bank] = ev_ev
                if ct == NCT - 1:
                    for j in range(4):
                        ev_stage_free[j] = ev_ev
                if grp == 3:
                    ev_xready[ct] = ev_ev

        def mgb(seti, k):
            return af(O_MGBUF + (seti * 4 + k) * 8192, T)

        def xbf_t(seti):
            return ab(O_XBF + seti * 4096, T)

        wbd_bf = {k: ab(O_WBD + i * 2048, NCT * 128) for i, k in enumerate("fih")}

        ev_scan = [None] * NCT
        ev_sig2 = [None] * NCT
        ev_diff = [None] * NCT
        ev_hcopy = [None] * NCT
        ev_ef = [None] * NCT
        ev_ei = [None] * NCT
        ev_lnf = [None] * NCT
        ev_lni = [None] * NCT
        ev_f_mm = [None] * NCT
        ev_i_mm = [None] * NCT
        ev_h_mm = [None] * NCT
        ev_xbf = [None] * NCT

        for ct in range(NCT):
            si = ct % 2
            ev_xbf[ct] = P.op(
                "v", (lambda e_, c=ct, s=si: e_.tensor_copy(xbf_t(s)[:, :], X[c][:, :])),
                waits=(ev_xready[ct], ev_scan[ct - 2] if ct >= 2 else None),
                sig="v")
            for j in range(4):
                ev_f_mm[ct] = P.op(
                    "t",
                    (lambda e_, c=ct, jj=j, s=si:
                     e_.matmul(bk(jj), wbd_bf["f"][:, c * 128:(c + 1) * 128],
                               xbf_t(s)[:, jj * 512:(jj + 1) * 512],
                               start=True, stop=True)),
                    waits=(ev_xbf[ct], ev_wbd["f"],
                           ev_hcopy[ct - 1] if ct >= 1 else bankfree[j]),
                    sig="t" if j == 3 else None)
            for j in range(4):
                ev_i_mm[ct] = P.op(
                    "t",
                    (lambda e_, c=ct, jj=j, s=si:
                     e_.matmul(bk(4 + jj), wbd_bf["i"][:, c * 128:(c + 1) * 128],
                               xbf_t(s)[:, jj * 512:(jj + 1) * 512],
                               start=True, stop=True)),
                    waits=(ev_wbd["i"], ev_ei[ct - 1] if ct >= 1 else None),
                    sig="t" if j == 3 else None)
            Pd, Ph, Pf, Pm = (mgb(si, 0), mgb(si, 1), mgb(si, 2), mgb(si, 3))
            wb = ev_sig2[ct - 2] if ct >= 2 else None
            wb2 = ev_scan[ct - 2] if ct >= 2 else None
            ev_ef[ct] = P.op(
                "s", (lambda e_, p=Pd: e_.activation(p, pbA[:, :], ACT.Exp, scale=-1.0)),
                waits=(ev_f_mm[ct], wb, wb2), sig="s")
            ev_lnf[ct] = P.op(
                "s", (lambda e_, p=Pd: e_.activation(p, p, ACT.Ln, bias=1.0)),
                waits=(ev_ef[ct],), sig="s")
            ev_ei[ct] = P.op(
                "s", (lambda e_, p=Pf: e_.activation(p, pbB[:, :], ACT.Exp, scale=-1.0)),
                waits=(ev_i_mm[ct], wb, wb2), sig="s")
            ev_lni[ct] = P.op(
                "s", (lambda e_, p=Pf: e_.activation(p, p, ACT.Ln, bias=1.0)),
                waits=(ev_ei[ct],), sig="s")
            for j in range(4):
                ev_h_mm[ct] = P.op(
                    "t",
                    (lambda e_, c=ct, jj=j, s=si:
                     e_.matmul(bk(jj), wbd_bf["h"][:, c * 128:(c + 1) * 128],
                               xbf_t(s)[:, jj * 512:(jj + 1) * 512],
                               start=True, stop=True)),
                    waits=(ev_ef[ct], ev_wbd["h"]),
                    sig="t" if j == 3 else None)
            ev_diff[ct] = P.op(
                "v", (lambda e_, a=Pd, b=Pf: e_.tensor_sub(a, a, b)),
                waits=(ev_lnf[ct], ev_lni[ct]), sig="v")
            ev_hcopy[ct] = P.op(
                "v", (lambda e_, p=Ph: e_.tensor_copy(p, pbA[:, :])),
                waits=(ev_h_mm[ct],), sig="v")
            if ct % 2 == 1:
                for c2 in (ct - 1, ct):
                    s2 = c2 % 2
                    Qd, Qh, Qf, Qm = (mgb(s2, 0), mgb(s2, 1), mgb(s2, 2), mgb(s2, 3))
                    ev_fp = P.op(
                        "s", (lambda e_, a=Qd, o=Qf: e_.activation(o, a, ACT.Sigmoid, scale=-1.0)),
                        waits=(ev_diff[c2],), sig="s")
                    ev_sig2[c2] = P.op(
                        "s", (lambda e_, a=Qh, o=Qd: e_.activation(o, a, ACT.Sigmoid)),
                        waits=(ev_hcopy[c2], ev_fp), sig="s")
                    P.op("v", (lambda e_, a=Qh, o=Qm: e_.tensor_single_scalar(o, a, 0.0, ALU.is_ge)),
                         waits=(ev_sig2[c2],))
                    P.op("v", (lambda e_, a=Qh: e_.tensor_scalar(a, a, GEPS, None, ALU.add)))
                    P.op("v", (lambda e_, a=Qh, b=Qd: e_.tensor_sub(a, a, b)))
                    P.op("v", (lambda e_, a=Qh, b=Qm: e_.tensor_mul(a, a, b)))
                    P.op("v", (lambda e_, a=Qh, b=Qd: e_.tensor_add(a, a, b)))
                    P.op("v", (lambda e_, a=Qf, o=Qm:
                               e_.tensor_scalar(o, a, -1.0, 1.0, ALU.mult, ALU.add)))
                    P.op("v", (lambda e_, a=Qm, b=Qh: e_.tensor_mul(a, a, b)))
                    P.op("v", (lambda e_, f=Qf, v_=Qm, o=Qh:
                               e_.tensor_tensor_scan(o, f, v_, 0.0, ALU.mult, ALU.add)))
                    ev_scan[c2] = P.op(
                        "v", (lambda e_, c=c2, hh=Qh: e_.tensor_add(X[c][:, :], X[c][:, :], hh)),
                        sig="v")

        def dump_X(tag, waits):
            if tag in dbg:
                for ct in range(NCT):
                    P.dma("y", dbg[tag][ct * 128:(ct + 1) * 128, :], X[ct][:, :],
                          waits=waits)

        if upto == "mingru":
            dump_X("X", tuple(ev_scan))
            _finish(P, sems)
            P.realize(nc, sems)
            return nc

        def xn_t(ct, which):
            off = (O_XN + ct * 4096) if which == 1 else XN2_OFF[ct]
            return ab(off, T)

        rowsl = [af(O_LN_ROWS + i * 8192, T, 0, 1) for i in range(4)]
        repr_ = af(O_LN_REPS, T)
        repmr = af(O_LN_REPS + 8192, T)
        repr2 = af(O_LN_REPS + 16384, T)

        ev_sq = [None, None]
        ev_sums = None
        ev_sumsq = None
        ev_sumsq_ct = [None] * NCT
        for ct in range(NCT):
            sqb = af(O_LN_SQ + (ct % 2) * 8192, T)
            ev_sq[ct % 2] = P.op(
                "s", (lambda e_, c=ct, o=sqb: e_.activation(o, X[c][:, :], ACT.Square)),
                waits=(ev_scan[ct],
                       ev_sumsq_ct[ct - 2] if ct >= 2 else None), sig="s")
            for j in range(4):
                ev_sums = P.op(
                    "t",
                    (lambda e_, c=ct, jj=j:
                     e_.matmul(bk(jj)[0:1, :], ones_c[:, 0:1],
                               X[c][:, jj * 512:(jj + 1) * 512],
                               start=(c == 0), stop=(c == NCT - 1))),
                    waits=(ev_scan[ct], ev_ones,
                           ev_hcopy[NCT - 1] if ct == 0 else None),
                    sig="t" if j == 3 else None)
            for j in range(4):
                ev_sumsq = P.op(
                    "t",
                    (lambda e_, o=sqb, jj=j, c=ct:
                     e_.matmul(bk(4 + jj)[0:1, :], ones_c[:, 0:1],
                               o[:, jj * 512:(jj + 1) * 512],
                               start=(c == 0), stop=(c == NCT - 1))),
                    waits=(ev_sq[ct % 2],
                           ev_ei[NCT - 1] if ct == 0 else None),
                    sig="t" if j == 3 else None)
            ev_sumsq_ct[ct] = ev_sumsq

        ev_m = P.op("v", (lambda e_: e_.tensor_scalar(rowsl[0], pbA[0:1, :], 1.0 / D, None, ALU.mult)),
                    waits=(ev_sums,), sig="v")
        ev_q = P.op("v", (lambda e_: e_.tensor_scalar(rowsl[1], pbB[0:1, :], 1.0 / D, None, ALU.mult)),
                    waits=(ev_sumsq,), sig="v")
        P.op("v", (lambda e_: e_.tensor_mul(rowsl[2], rowsl[0], rowsl[0])))
        ev_var = P.op("v", (lambda e_: e_.tensor_sub(rowsl[1], rowsl[1], rowsl[2])), sig="v")
        ev_vare = P.op("v", (lambda e_: e_.tensor_scalar(rowsl[1], rowsl[1], LN_EPS, None, ALU.add)),
                       waits=(ev_var,), sig="v")
        ev_lnv = P.op("s", (lambda e_: e_.activation(rowsl[2], rowsl[1], ACT.Ln)),
                      waits=(ev_vare,), sig="s")
        ev_r = P.op("s", (lambda e_: e_.activation(rowsl[3], rowsl[2], ACT.Exp, scale=-0.5)),
                    waits=(ev_lnv,), sig="s")
        ev_rr = P.op("s", (lambda e_: e_.activation(rowsl[2], rowsl[2], ACT.Exp, scale=-1.0)),
                     waits=(ev_lnv, ev_r), sig="s")
        ev_msu = P.op("v", (lambda e_: e_.scalar_tensor_tensor(rowsl[2], rowsl[1], -LN_EPS,
                                                               rowsl[2], ALU.add, ALU.mult)),
                      waits=(ev_rr,), sig="v")
        ev_ms2 = P.op("v", (lambda e_: e_.tensor_scalar(rowsl[2], rowsl[2], LN_EPS, None, ALU.add)),
                      sig="v")
        ev_ln2 = P.op("s", (lambda e_: e_.activation(rowsl[2], rowsl[2], ACT.Ln)),
                      waits=(ev_ms2,), sig="s")
        ev_r2 = P.op("s", (lambda e_: e_.activation(rowsl[2], rowsl[2], ACT.Exp, scale=-0.5)),
                     waits=(ev_ln2,), sig="s")
        ev_mr = P.op("v", (lambda e_: e_.tensor_mul(rowsl[0], rowsl[0], rowsl[3])),
                     waits=(ev_r,), sig="v")

        def replicate(dst, src_row, waits, bank_wait):
            last = None
            for j in range(4):
                last = P.op(
                    "t",
                    (lambda e_, jj=j, s_=src_row:
                     e_.matmul(bk(jj), ones_r[0:1, :],
                               s_[0:1, jj * 512:(jj + 1) * 512], start=True, stop=True)),
                    waits=tuple(waits) + (ev_ones2, bank_wait),
                    sig="t" if j == 3 else None)
            return P.op("v", (lambda e_, d=dst: e_.tensor_copy(d, pbA[:, :])),
                        waits=(last,), sig="v")

        ev_repr = replicate(repr_, rowsl[3], (ev_r, ev_m), ev_m)
        ev_repmr = replicate(repmr, rowsl[0], (ev_mr,), ev_repr)
        ev_repr2 = replicate(repr2, rowsl[2], (ev_r2,), ev_repmr)

        ev_xn1 = [None] * NCT
        ev_x1 = [None] * NCT
        for ct in range(NCT):
            sqb = af(O_LN_SQ + (ct % 2) * 8192, T)
            P.op("v", (lambda e_, c=ct, o=sqb: e_.scalar_tensor_tensor(
                o, X[c][:, :], 1.0, repr_, ALU.mult, ALU.mult)),
                waits=(ev_repr, ev_sumsq))
            ev_x1[ct] = P.op("v", (lambda e_, c=ct, a=sqb: e_.tensor_sub(X[c][:, :], a, repmr)),
                             waits=(ev_repmr,), sig="v")
            P.op("v", (lambda e_, c=ct, o=sqb: e_.tensor_mul(o, X[c][:, :], repr2)),
                 waits=(ev_repr2,))
            ev_xn1[ct] = P.op(
                "v", (lambda e_, c=ct, a=sqb: e_.tensor_scalar(
                    xn_t(c, 1), a, gam1[:, c:c + 1], None, ALU.mult)),
                sig="v")
        ev_x1dma = [None] * NCT
        for ct in range(NCT):
            ev_x1dma[ct] = P.dma("y", x1sc[ct * 128:(ct + 1) * 128, :], X[ct][:, :],
                                 waits=(ev_x1[ct],))
        ev_x1dma_all = ev_x1dma[NCT - 1]

        if upto == "ln1":
            dump_X("X", tuple(ev_x1))
            if "xn1" in dbg:
                for ct in range(NCT):
                    evc = P.op("v", (lambda e_, c=ct: e_.tensor_copy(
                        af(O_LN_SQ, T), xn_t(c, 1))), waits=(ev_xn1[ct],), sig="v")
                    P.dma("y", dbg["xn1"][ct * 128:(ct + 1) * 128, :], af(O_LN_SQ, T),
                          waits=(evc,))
            _finish(P, sems)
            P.realize(nc, sems)
            return nc

        lat_bf = ab(O_LAT, T)
        kr_b = ab(O_KR, T, 0, 64)
        kvtm_all = ab(O_KVTM, NT * 512)
        wkva_bf = ab(O_WKVA, NCT * 192)
        wkvb_bf = ab(O_WKVB, 512)
        wq_bf = ab(O_WQ, NCT * D)
        wo_bf = ab(O_WO, 4 * D)
        ccq = ab(O_CCQ, T)
        csq = ab(O_CSQ, T)
        smask = list(range(16, 32)) + list(range(0, 16))

        ev_lat_mm = None
        for tc in range(4):
            for cc in range(NCT):
                ev_lat_mm = P.op(
                    "t",
                    (lambda e_, c=cc, j=tc:
                     e_.matmul(bk(j), wkva_bf[:, c * 192: c * 192 + 128],
                               xn_t(c, 1)[:, j * 512:(j + 1) * 512],
                               start=(c == 0), stop=(c == NCT - 1))),
                    waits=(ev_xn1[cc], ev_wkva),
                    sig="t" if cc == NCT - 1 else None)
        ev_lat = P.op("v", (lambda e_: e_.tensor_copy(lat_bf[:, :], pbA[:, :])),
                      waits=(ev_lat_mm,), sig="v")
        ev_kr_mm = None
        for tc in range(4):
            for cc in range(NCT):
                ev_kr_mm = P.op(
                    "t",
                    (lambda e_, c=cc, j=tc:
                     e_.matmul(bk(4 + j)[0:64, :], wkva_bf[:, c * 192 + 128: c * 192 + 192],
                               xn_t(c, 1)[:, j * 512:(j + 1) * 512],
                               start=(c == 0), stop=(c == NCT - 1))),
                    waits=(ev_xn1[cc], ev_wkva),
                    sig="t" if cc == NCT - 1 else None)
        ev_kr_raw = P.op("v", (lambda e_: e_.tensor_copy(kr_b[32:64, :], pbB[32:64, :])),
                         waits=(ev_kr_mm,), sig="v")
        krs = ab(O_SPARE, T, 0, 64)
        P.op("v", (lambda e_: e_.stream_shuffle(krs[32:64, :], kr_b[32:64, :], smask)),
             waits=(ev_kr_raw, ev_tab))
        P.op("v", (lambda e_: e_.tensor_mul(kr_b[32:64, :], kr_b[32:64, :], ccq[32:64, :])))
        P.op("v", (lambda e_: e_.tensor_mul(krs[32:64, :], krs[32:64, :], csq[32:64, :])))
        ev_kr = P.op("v", (lambda e_: e_.tensor_add(kr_b[32:64, :], kr_b[32:64, :], krs[32:64, :])),
                     sig="v")

        ev_kvtm_mm = [None] * NT
        ev_kvtm = [None] * NT
        for kt in range(NT):
            b = kt % 4
            ev_kvtm_mm[kt] = P.op(
                "t",
                (lambda e_, k=kt, bb=b:
                 e_.matmul(bk(bb), lat_bf[:, k * 128:(k + 1) * 128], wkvb_bf[:, :],
                           start=True, stop=True)),
                waits=(ev_lat, ev_wkvb, ev_kvtm[kt - 4] if kt >= 4 else None),
                sig="t")
            ev_kvtm[kt] = P.op(
                "v", (lambda e_, k=kt, bb=b: e_.tensor_copy(kvtm_all[:, k * 512:(k + 1) * 512], bk(bb))),
                waits=(ev_kvtm_mm[kt],), sig="v")
        ev_kvfm = [None] * 4
        for g in range(4):
            ev_mm = None
            for tc in range(4):
                ev_mm = P.op(
                    "t",
                    (lambda e_, gg=g, j=tc:
                     e_.matmul(bk(4 + j), wkvb_bf[:, gg * 128:(gg + 1) * 128],
                               lat_bf[:, j * 512:(j + 1) * 512], start=True, stop=True)),
                    waits=(ev_lat, ev_kvfm[g - 1] if g >= 1 else ev_kr_raw),
                    sig="t" if tc == 3 else None)
            ev_kvfm[g] = P.op(
                "v", (lambda e_, gg=g: e_.tensor_copy(ab(O_KVFM + gg * 4096, T), pbB[:, :])),
                waits=(ev_mm,), sig="v")

        qsh = ab(O_SPARE, T)
        ev_qh2 = [None] * 8
        ev_qh2cp = [None] * 8
        ev_qproj_last = None
        for jp in range(8):
            ev_mm = None
            for tc in range(4):
                b = tc if jp % 2 == 0 else 4 + tc
                for cc in range(NCT):
                    ev_mm = P.op(
                        "t",
                        (lambda e_, c=cc, j=tc, p_=jp, bb=b:
                         e_.matmul(bk(bb), wq_bf[:, c * D + p_ * 128: c * D + p_ * 128 + 128],
                                   xn_t(c, 1)[:, j * 512:(j + 1) * 512],
                                   start=(c == 0), stop=(c == NCT - 1))),
                        waits=(ev_xn1[cc], ev_wq,
                               ev_qh2cp[jp - 2] if jp >= 2 else
                               (ev_kvfm[3] if jp % 2 == 1 else ev_kvtm[NT - 1])),
                        sig="t" if cc == NCT - 1 else None)
            psrc = pbA if jp % 2 == 0 else pbB
            ev_qh2cp[jp] = P.op(
                "v", (lambda e_, p_=jp, ps=psrc: e_.tensor_scalar(
                    ab(O_QH2 + p_ * 4096, T), ps[:, :], 0.125, None, ALU.mult)),
                waits=(ev_mm, ev_kr if jp == 0 else None), sig="v")
            ev_qproj_last = ev_mm
            P.op("v", (lambda e_, p_=jp: e_.stream_shuffle(qsh[:, :], ab(O_QH2 + p_ * 4096, T), smask)),
                 waits=(ev_tab,))
            P.op("v", (lambda e_, p_=jp: e_.tensor_mul(ab(O_QH2 + p_ * 4096, T),
                                                       ab(O_QH2 + p_ * 4096, T), ccq[:, :])))
            P.op("v", (lambda e_: e_.tensor_mul(qsh[:, :], qsh[:, :], csq[:, :])))
            ev_qh2[jp] = P.op(
                "v", (lambda e_, p_=jp: e_.tensor_add(ab(O_QH2 + p_ * 4096, T),
                                                      ab(O_QH2 + p_ * 4096, T), qsh[:, :])),
                sig="v")

        ev_ebr_mm = P.op(
            "t", (lambda e_: e_.matmul(bk(0)[:, 0:8], ones_r[0:1, :], ebias_sb[0:1, :],
                                       start=True, stop=True)),
            waits=(ev_eb, ev_ones2, ev_qh2cp[6]), sig="t")
        ev_ebr = P.op("v", (lambda e_: e_.tensor_copy(ebias_rep[:, :], bk(0)[:, 0:8])),
                      waits=(ev_ebr_mm, ev_qh2cp[7]), sig="v")

        if upto == "pre":
            dump_X("X", tuple(ev_x1))
            _finish(P, sems)
            P.realize(nc, sems)
            return nc

        ev_exp_prev = [ev_ebr, ev_ebr]
        ev_pair_done = [None] * 8
        ev_kp_used = [None, None]
        ev_kp_ready = [None, None]
        ev_kp_last_dma = None
        ev_ygrp_prev = {}
        ev_ytm_free = None
        ev_ynorm_prev = [None, None]
        ev_ptevac_prev = [None, None]
        ev_av_prev = [None, None]
        ev_rl_read_prev = [None, None]
        ev_wo_done = [None] * 4
        ev_wo_evac = {}

        def kp_view(par, p0=0, p1=128):
            return ab(O_KP + par * 4096, T, p0, p1)

        def assemble_kp(jp, extra=()):
            nonlocal ev_kp_last_dma
            par = jp % 2
            last = None
            for (hh, dstlo) in ((2 * jp, 0), (2 * jp + 1, 64)):
                g, a = hh // 4, hh % 4
                src = ab(O_KVFM + g * 4096, T, a * 32, a * 32 + 32)
                last = P.dma("y", kp_view(par, dstlo, dstlo + 32), src,
                             waits=(ev_kvfm[g], ev_kp_used[par]) + tuple(extra))
                last = P.dma("y", kp_view(par, dstlo + 32, dstlo + 64), kr_b[32:64, :],
                             waits=(ev_kr,))
            ev_kp_last_dma = last
            return last

        ev_kp_ready[0] = assemble_kp(0, extra=(ev_qproj_last,))
        ev_kp_ready[1] = assemble_kp(1, extra=(ev_qproj_last,))

        pend_pt = []
        pend_rest = []

        ptev_stash = [None]

        def flush_ptev():
            if ptev_stash[0] is not None:
                ptev_stash[0]()
                ptev_stash[0] = None

        def flush_all():
            nonlocal pend_pt, pend_rest
            for fnp in pend_pt:
                fnp()
            pend_pt = []
            flush_ptev()
            for fnp in pend_rest:
                fnp()
            pend_rest = []

        def PS(p):
            return pbA if p == 0 else pbB

        ev_ptdone_prev = [None, None]
        ev_ptevac_prev2 = [None, None]
        ev_max_prev = [None, None]

        for jp in range(_ATTN_PAIRS):
            par = jp % 2
            for h2 in range(2):
                h = 2 * jp + h2
                base = 64 * h2
                for qt in range(_ATTN_QT):
                    p = qt % 2
                    W = (qt + 1) * 128
                    nch = (W + 511) // 512
                    cs = p * 4
                    ps = PS(p)
                    while len(pend_rest) >= 2:
                        pend_rest.pop(0)()
                    ev_smm = None
                    for j in range(nch):
                        n = min(512, W - j * 512)
                        last_chunk = (j == nch - 1)
                        ev_smm = P.op(
                            "t",
                            (lambda e_, jj=j, nn=n, p_=jp, b_=base, q_=qt, pr=par, ps_=ps, lc=last_chunk:
                             e_.matmul(ps_[:, jj * 512: jj * 512 + nn],
                                       ab(O_QH2 + p_ * 4096, T, b_, b_ + 64)[:, q_ * 128:(q_ + 1) * 128],
                                       kp_view(pr, b_, b_ + 64)[:, jj * 512: jj * 512 + nn],
                                       start=True, stop=not lc)),
                            waits=(ev_qh2[jp], ev_kp_ready[par], ev_exp_prev[p],
                                   ev_ptevac_prev[p], ev_ptevac_prev2[p],
                                   ev_ynorm_prev[p] if (nch >= 3 and j == 2) else None),
                            sig=None)
                    ev_smm = P.op(
                        "t",
                        (lambda e_, q_=qt, ps_=ps:
                         e_.matmul(ps_[:, q_ * 128:(q_ + 1) * 128],
                                   ident_bf[:, :], trimask_bf[:, :],
                                   start=False, stop=True)),
                        waits=(ev_tri_bf, ev_id_bf), sig="t")
                    ev_pair_done[jp] = ev_smm
                    ev_kp_used[par] = ev_smm
                    for fnp in pend_pt:
                        fnp()
                    pend_pt = []
                    ev_max = P.op(
                        "v",
                        (lambda e_, w_=W, c=cs, ps_=ps: e_.reduce_max(colw[:, c:c + 1], ps_[:, 0:w_],
                                                                      axis=AX, negate=True)),
                        waits=(ev_smm,), hwaits=(ev_rl_read_prev[p],), sig="v")
                    ev_max_prev[p] = ev_max
                    flush_ptev()
                    ev_exp = P.op(
                        "s",
                        (lambda e_, w_=W, c=cs, pr=p, ps_=ps:
                         e_.activation(ab(O_P + pr * 4096, T)[:, 0:w_], ps_[:, 0:w_], ACT.Exp,
                                       bias=colw[:, c:c + 1], scale=1.0,
                                       accum_out=colw[:, c + 2:c + 3])),
                        waits=(ev_max, ev_ptdone_prev[p]),
                        sig="s")
                    ev_rl = P.op(
                        "v",
                        (lambda e_, c=cs: e_.reciprocal(colw[:, c + 3:c + 4], colw[:, c + 2:c + 3])),
                        waits=(ev_exp,), sig="v")
                    ev_exp_prev[p] = ev_exp

                    def make_pt(qt_=qt, pr=qt % 2, ev_exp_=ev_exp):
                        def tail_pt():
                            nkt = qt_ + 1
                            ps_ = PS(pr)
                            psb = ps_[:, 0:1024].bitcast(BF16)
                            ev_tr = None
                            for kt in range(nkt):
                                ev_tr = P.op(
                                    "t",
                                    (lambda e_, k=kt, pb=psb, p2=pr:
                                     e_.transpose(pb[:, k * 128:(k + 1) * 128],
                                                  ab(O_P + p2 * 4096, T)[:, k * 128:(k + 1) * 128],
                                                  ident_bf[:, :])),
                                    waits=(ev_exp_, ev_ptevac_prev[pr],
                                           ev_ptevac_prev2[pr], ev_id_bf),
                                    sig="t" if kt == nkt - 1 else None)
                            ev_ptdone_prev[pr] = ev_tr
                            n_all = nkt * 128
                            if _PTEV and n_all > 1024:
                                def emit_ptev(nc_=n_all, pb=psb, p2=pr, et=ev_tr):
                                    ev_ptevac_prev2[p2] = P.op(
                                        "v",
                                        (lambda e_:
                                         e_.tensor_copy(
                                             ab(O_PT + p2 * 4096, T)[:, 1024:nc_],
                                             pb[:, 1024:nc_])),
                                        waits=(et,), sig="v")
                                ptev_stash[0] = emit_ptev
                                n_s = 1024
                            else:
                                n_s = n_all
                            ev_pte = P.op(
                                "s",
                                (lambda e_, nc_=n_s, pb=psb, p2=pr:
                                 e_.copy(ab(O_PT + p2 * 4096, T)[:, 0:nc_], pb[:, 0:nc_])),
                                waits=(ev_tr,), sig="s")
                            ev_ptevac_prev[pr] = ev_pte
                            return ev_pte
                        return tail_pt

                    def make_rest(h_=h, qt_=qt, cs_=cs, h2_=h2, pr=qt % 2, ev_rl_=ev_rl):
                        def tail_rest():
                            nkt = qt_ + 1
                            ps_ = PS(pr)
                            ev_pte = ev_ptevac_prev[pr]
                            ev_pte2 = ev_ptevac_prev2[pr]
                            ev_mx = ev_max_prev[pr] if qt_ >= 8 else None
                            ev_av = None
                            for kt in range(nkt):
                                ev_av = P.op(
                                    "t",
                                    (lambda e_, k=kt, hh=h_, p2=pr, q2=qt_, ps2=ps_:
                                     e_.matmul(ps2[:, 1024:1056],
                                               ab(O_PT + p2 * 4096, T)[:, k * 128:(k + 1) * 128],
                                               kvtm_all[:, k * 512 + 32 * hh: k * 512 + 32 * hh + 32],
                                               start=(k == 0), stop=(k == q2))),
                                    waits=(ev_pte, ev_pte2, ev_kvtm[kt],
                                           ev_ynorm_prev[pr], ev_mx),
                                    sig="t" if kt == nkt - 1 else None)
                            ev_av_prev[pr] = ev_av
                            ev_yn = P.op(
                                "v",
                                (lambda e_, c=cs_, q2=qt_, hh2=h2_, ps2=ps_: e_.tensor_scalar(
                                    ab(O_YTM, 1024)[:, q2 * 64 + hh2 * 32: q2 * 64 + hh2 * 32 + 32],
                                    ps2[:, 1024:1056],
                                    colw[:, c + 3:c + 4], None, ALU.mult)),
                                waits=(ev_av, ev_ytm_free), hwaits=(ev_rl_,), sig="v")
                            ev_ynorm_prev[pr] = ev_yn
                            ev_rl_read_prev[pr] = ev_yn
                        return tail_rest

                    pend_pt.append(make_pt())
                    pend_rest.append(make_rest())
            if jp + 2 <= 7:
                ev_kp_ready[par] = assemble_kp(jp + 2)
            flush_all()
            rbase = 64 * (jp % 2)
            ev_tre = {}
            for qg in range(4):
                last_tr = None
                for qq in range(4):
                    qt_ = qg * 4 + qq
                    last_tr = P.op(
                        "t",
                        (lambda e_, q_=qt_, b_=qg, rb=rbase:
                         e_.transpose(pbA[:, b_ * 512: b_ * 512 + 512]
                                      .bitcast(BF16)[rb:rb + 64, (q_ % 4) * 128:(q_ % 4 + 1) * 128],
                                      ab(O_YTM, 1024)[:, q_ * 64:(q_ + 1) * 64],
                                      ident_bf[:, :], tile_position=(0, rb))),
                        waits=(ev_ynorm_prev[0], ev_ynorm_prev[1],
                               ev_exp_prev[0], ev_ptevac_prev[0],
                               ev_ptevac_prev2[0], ev_ygrp_prev.get(qg)),
                        sig="t" if qq == 3 else None)
                ev_tre[qg] = last_tr
            ev_ytm_free = ev_tre[3]
            for qg in range(4):
                ev_ygrp_prev[qg] = P.op(
                    "v",
                    (lambda e_, b_=qg, rb=rbase:
                     e_.tensor_copy(ab(O_YGRP, T, rb, rb + 64)[:, b_ * 512:(b_ + 1) * 512],
                                    pbA[:, b_ * 512: b_ * 512 + 512].bitcast(BF16)[rb:rb + 64, 0:512])),
                    waits=(ev_tre[qg],), sig="v")
            ev_exp_prev[0] = ev_ygrp_prev[3]
            if jp % 2 == 1 and _K_WO:
                g = jp // 2
                last_mm = None
                for cc in range(NCT):
                    for tc in range(4):
                        b = tc
                        last_mm = P.op(
                            "t",
                            (lambda e_, c=cc, j=tc, gg=g, bb=b:
                             e_.matmul(bk(bb),
                                       wo_bf[:, gg * D + c * 128: gg * D + c * 128 + 128],
                                       ab(O_YGRP, T)[:, j * 512:(j + 1) * 512],
                                       start=True, stop=True)),
                            waits=(ev_ygrp_prev[0], ev_ygrp_prev[1],
                                   ev_ygrp_prev[2], ev_ygrp_prev[3], ev_wo,
                                   ev_exp_prev[0], ev_exp_prev[1],
                                   ev_wo_evac.get((cc - 1, tc))),
                            sig="t")
                        ev_wo_evac[(cc, tc)] = P.op(
                            "v",
                            (lambda e_, c=cc, j=tc, bb=b:
                             e_.scalar_tensor_tensor(X[c][:, j * 512:(j + 1) * 512],
                                                     bk(bb), 1.0,
                                                     X[c][:, j * 512:(j + 1) * 512],
                                                     ALU.mult, ALU.add)),
                            waits=(last_mm,) + ((tuple(ev_x1dma)) if g == 0 else ()),
                            sig="v")
                ev_wo_done[g] = last_mm
                ev_exp_prev[0] = ev_wo_evac[(NCT - 1, 3)]
                ev_exp_prev[1] = ev_wo_evac[(NCT - 1, 3)]
                ev_wo_evac = {}

        flush_all()
        ev_attn_done = ev_wo_done[3]
        ev_attn_vdone = ev_exp_prev[0]

        if upto == "attn":
            dump_X("X", (ev_attn_vdone,))
            _finish(P, sems)
            P.realize(nc, sems)
            return nc

        m2row = af(O_M2ROW, T, 0, 1)
        mrep = af(O_MREP, T)
        ev_sq2p = [None, None]
        ev_sumsq2 = None
        ev_sumsq2_ct = [None] * NCT
        for ct in range(NCT):
            sqb = af(O_M2SQ[ct % 2], T)
            ev_sq2p[ct % 2] = P.op(
                "s", (lambda e_, c=ct, o=sqb: e_.activation(o, X[c][:, :], ACT.Square)),
                waits=(ev_attn_vdone, ev_attn_done,
                       ev_sumsq2_ct[ct - 2] if ct >= 2 else None,
                       ev_ptevac_prev[0] if ct == 0 else None,
                       ev_ptevac_prev[1] if ct == 0 else None),
                sig="s")
            for j in range(4):
                ev_sumsq2 = P.op(
                    "t",
                    (lambda e_, o=sqb, jj=j, c=ct:
                     e_.matmul(bk(jj)[0:1, :], ones_c[:, 0:1],
                               o[:, jj * 512:(jj + 1) * 512],
                               start=(c == 0), stop=(c == NCT - 1))),
                    waits=(ev_sq2p[ct % 2],),
                    sig="t" if j == 3 else None)
            ev_sumsq2_ct[ct] = ev_sumsq2
        ev_q2 = P.op("v", (lambda e_: e_.tensor_scalar(m2row, pbA[0:1, :], 1.0 / D, None, ALU.mult)),
                     waits=(ev_sumsq2, ev_kp_last_dma), sig="v")
        ev_q2e = P.op("v", (lambda e_: e_.tensor_scalar(m2row, m2row, LN_EPS, None, ALU.add)),
                      waits=(ev_q2,), sig="v")
        ev_l2 = P.op("s", (lambda e_: e_.activation(m2row, m2row, ACT.Ln)),
                     waits=(ev_q2e,), sig="s")
        ev_r2b = P.op("s", (lambda e_: e_.activation(m2row, m2row, ACT.Exp, scale=-0.5)),
                      waits=(ev_l2,), sig="s")
        ev_mrep = replicate(mrep, m2row, (ev_r2b,), ev_q2)
        ev_xn2 = [None] * NCT
        for ct in range(NCT):
            sqb = af(O_M2SQ[ct % 2], T)
            P.op("v", (lambda e_, c=ct, o=sqb: e_.tensor_mul(o, X[c][:, :], mrep)),
                 waits=(ev_mrep, ev_sumsq2))
            ev_xn2[ct] = P.op(
                "v", (lambda e_, c=ct, a=sqb: e_.tensor_scalar(
                    xn_t(c, 2), a, gam2[:, c:c + 1], None, ALU.mult)),
                waits=(ev_gam2,), sig="v")

        if upto == "rms2":
            dump_X("X", tuple(ev_xn2))
            _finish(P, sems)
            P.realize(nc, sems)
            return nc

        wgate_bf = ab(O_WGATE, NCT * 8)
        comb_fm = af(O_COMB, T, 0, 8)
        LG = af(O_LG, 128)
        PR = af(O_PR, 128)
        MS = af(O_MS, 128)
        MS2 = af(O_MS2, 128)

        def a3(t_, d1=16, d2=8, bcast_from=None):
            h_ = t_.tensor if hasattr(t_, "tensor") else None
            return t_

        def v3(base_off):
            return bass.AP(arena, base_off // 4 * 0 + 0, [[0, 0]])

        def lg3(off):
            return bass.AP(arena, off // 4, [[AR_BYTES // 4, 128], [8, 16], [1, 8]]).bitcast_hack \
                if False else None

        arf = arena[:, :].bitcast(F32)

        def t3(off):
            return bass.AP(arf.tensor, off // 4, [[AR_BYTES // 4, 128], [8, 16], [1, 8]]) \
                if False else None

        def ap3(off, last_stride=1):
            base = af(off, 128)
            t_ = base.tensor
            return bass.AP(t_, base.offset, [[base.ap[0][0], 128], [8 if last_stride else 0, 16], [last_stride, 8]])

        def apb(off, col):
            base = af(off, 128)
            t_ = base.tensor
            return bass.AP(t_, base.offset + col, [[base.ap[0][0], 128], [1, 16], [0, 8]])

        ev_gate_prev = ev_xn2[NCT - 1]
        ev_gmm = None
        for tt in range(NT):
            for cc in range(NCT):
                ev_gmm = P.op(
                    "t",
                    (lambda e_, c=cc, t_=tt:
                     e_.matmul(bk(6)[:, 8 * t_: 8 * t_ + 8], xn_t(c, 2)[:, t_ * 128:(t_ + 1) * 128],
                               wgate_bf[:, c * 8:(c + 1) * 8],
                               start=(c == 0), stop=(c == NCT - 1))),
                    waits=(ev_xn2[cc], ev_wgate,
                           ev_dn_evac_guard if False else None,
                           ev_attn_done if (tt == 0 and cc == 0) else None),
                    sig="t" if (tt == NT - 1 and cc == NCT - 1) else None)
        eb3 = bass.AP(ebias_rep, 0, [[8, 128], [0, 16], [1, 8]])
        ev_lg = P.op(
            "v", (lambda e_: e_.tensor_tensor(ap3(O_LG), bk(6)[:, 0:128].rearrange(
                "p (a b) -> p a b", b=8), eb3, ALU.add)),
            waits=(ev_gmm, ev_ebr), sig="v")
        ev_m8 = P.op("v", (lambda e_: e_.reduce_max(MS[:, 0:16], ap3(O_LG), axis=AX, negate=True)),
                     hwaits=(ev_lg,), sig="v")
        ev_sub = P.op("v", (lambda e_: e_.tensor_tensor(ap3(O_LG), ap3(O_LG), apb(O_MS, 0), ALU.add)),
                      hwaits=(ev_m8,), sig="v")
        ev_e8 = P.op("s", (lambda e_: e_.activation(LG, LG, ACT.Exp)),
                     waits=(ev_sub,), sig="s")
        ev_l8 = P.op("v", (lambda e_: e_.reduce_sum(MS[:, 16:32], ap3(O_LG), axis=AX)),
                     waits=(ev_e8,), sig="v")
        ev_rs = P.op("v", (lambda e_: e_.reciprocal(MS[:, 32:48], MS[:, 16:32])),
                     hwaits=(ev_l8,), sig="v")
        ev_pr = P.op("v", (lambda e_: e_.tensor_tensor(ap3(O_PR), ap3(O_LG), apb(O_MS, 32), ALU.mult)),
                     hwaits=(ev_rs,), sig="v")
        ev_m1 = P.op("v", (lambda e_: e_.reduce_max(MS[:, 0:16], ap3(O_PR), axis=AX)),
                     hwaits=(ev_pr,), sig="v")
        ev_mk1 = P.op("v", (lambda e_: e_.tensor_tensor(ap3(O_LG), ap3(O_PR), apb(O_MS, 0), ALU.is_ge)),
                      hwaits=(ev_m1,), sig="v")
        ev_p2 = P.op("v", (lambda e_: e_.scalar_tensor_tensor(ap3(O_MS2), ap3(O_LG), -1e9,
                                                              ap3(O_PR), ALU.mult, ALU.add)),
                     hwaits=(ev_mk1,), sig="v")
        ev_m2 = P.op("v", (lambda e_: e_.reduce_max(MS[:, 48:64], ap3(O_MS2), axis=AX)),
                     hwaits=(ev_p2,), sig="v")
        ev_mk2 = P.op("v", (lambda e_: e_.tensor_tensor(ap3(O_MS2), ap3(O_MS2), apb(O_MS, 48), ALU.is_ge)),
                      hwaits=(ev_m2,), sig="v")
        ev_mks = P.op("v", (lambda e_: e_.tensor_tensor(ap3(O_LG), ap3(O_LG), ap3(O_MS2), ALU.add)),
                      hwaits=(ev_mk2,), sig="v")
        ev_ws = P.op("v", (lambda e_: e_.tensor_add(MS[:, 64:80], MS[:, 0:16], MS[:, 48:64])),
                     hwaits=(ev_m2,), sig="v")
        ev_rw = P.op("v", (lambda e_: e_.reciprocal(MS[:, 80:96], MS[:, 64:80])),
                     hwaits=(ev_ws,), sig="v")
        ev_cb1 = P.op("v", (lambda e_: e_.tensor_tensor(ap3(O_PR), ap3(O_PR), ap3(O_LG), ALU.mult)),
                      hwaits=(ev_mks,), sig="v")
        ev_comb = P.op("v", (lambda e_: e_.tensor_tensor(ap3(O_PR), ap3(O_PR), apb(O_MS, 80), ALU.mult)),
                       hwaits=(ev_rw, ev_cb1), sig="v")
        ev_combevac = None
        ev_ctr_prev = None
        for tt in range(NT):
            ev_ctr_prev = P.op(
                "t",
                (lambda e_, t_=tt: e_.transpose(pbB[0:8, 1536 + (t_ % 4) * 128: 1536 + (t_ % 4 + 1) * 128],
                                                PR[:, 8 * t_: 8 * t_ + 8], ident_f[:, :])),
                waits=(ev_comb, ev_combevac if tt % 4 == 0 and tt > 0 else None,
                       ev_attn_done if tt == 0 else None),
                sig="t")
            if tt % 4 == 3:
                ev_combevac = P.op(
                    "v",
                    (lambda e_, t_=tt: e_.tensor_copy(comb_fm[0:8, (t_ - 3) * 128:(t_ + 1) * 128],
                                                      pbB[0:8, 1536:2048])),
                    waits=(ev_ctr_prev,), sig="v")
        ev_comb_done = ev_combevac

        if upto == "gate":
            dump_X("X", (ev_comb_done,))
            if "comb" in dbg:
                P.dma("y", dbg["comb"][:, :], comb_fm[0:8, :], waits=(ev_comb_done,))
            _finish(P, sems)
            P.realize(nc, sems)
            return nc

        def wslot(i, n):
            return ab(MOE_WSLOT[i], n)

        slot_free = {0: ev_qh2[7], 1: ev_qh2[7], 2: ev_qh2[7],
                     3: ev_pair_done[5], 4: ev_pair_done[5], 5: ev_pair_done[5]}
        hfree = {i: ev_attn_done for i in range(9)}
        sgb = af(O_SG, 512)
        combrep = af(O_COMBREP, 512)
        csel_sb = af(O_CSEL, 1024, 0, 8)
        ev_csel = P.dma("y", csel_sb, d_csel, waits=(ev_attn_done,))
        ev_wg_dma = {}
        ev_wu_dma = {}
        ev_wd_dma = {}
        ev_hall = {}
        ev_prev_silu = None
        ev_down_read = None
        ev_hwr = {}
        ev_prev_hwr = None
        ev_sil_done = {}
        ev_umul_done = {}
        ev_dn_evac = {}
        ev_dn_free = {}
        ev_dn_last = {}
        ev_cre_prev = None
        ev_dn_done = None

        for tc in range(_K_MOE_TC):
            for e9 in range(_K_MOE_E):
                k = tc * 9 + e9
                sgu = (2 * k) % 4
                suu = sgu + 1
                ev_wg_dma[k] = P.dma("g", wslot(sgu, NCT * 512), d_wgu[e9],
                                     waits=(slot_free[sgu],))
                ev_wu_dma[k] = P.dma("g", wslot(suu, NCT * 512), d_wuu[e9],
                                     waits=(slot_free[suu],))
                if e9 < 8:
                    ev_crm = P.op(
                        "t",
                        (lambda e_, ee=e9, j=tc:
                         e_.matmul(bk(3), csel_sb[:, ee * 128:(ee + 1) * 128],
                                   comb_fm[0:8, j * 512:(j + 1) * 512],
                                   start=True, stop=True)),
                        waits=(ev_comb_done, ev_cre_prev, ev_csel,
                               ev_dn_free.get((tc, 3))), sig="t")
                    ev_cre = P.op(
                        "v", (lambda e_: e_.tensor_copy(combrep[:, :], bk(3))),
                        waits=(ev_crm,), sig="v")
                    ev_cre_prev = ev_cre
                else:
                    ev_cre = None
                ev_gmm = None
                ev_umm = None
                for ic in range(4):
                    bg = 4 + 2 * (ic % 2)
                    bu = bg + 1
                    for cc in range(NCT):
                        ev_gmm = P.op(
                            "t",
                            (lambda e_, c=cc, i_=ic, j=tc, s_=sgu, b_=bg:
                             e_.matmul(bk(b_),
                                       wslot(s_, NCT * 512)[:, c * 512 + i_ * 128: c * 512 + i_ * 128 + 128],
                                       xn_t(c, 2)[:, j * 512:(j + 1) * 512],
                                       start=(c == 0), stop=(c == NCT - 1))),
                            waits=(ev_wg_dma[k], ev_xn2[cc],
                                   ev_sil_done.get((k, ic - 2)) or ev_sil_done.get((k - 1, ic + 2)),
                                   ev_dn_free.get((tc, bg)) if e9 == 0 else None,
                                   ev_hwr.get((k - 1, ic))),
                            sig="t" if cc == NCT - 1 else None)
                    for cc in range(NCT):
                        ev_umm = P.op(
                            "t",
                            (lambda e_, c=cc, i_=ic, j=tc, s_=suu, b_=bu:
                             e_.matmul(bk(b_),
                                       wslot(s_, NCT * 512)[:, c * 512 + i_ * 128: c * 512 + i_ * 128 + 128],
                                       xn_t(c, 2)[:, j * 512:(j + 1) * 512],
                                       start=(c == 0), stop=(c == NCT - 1))),
                            waits=(ev_wu_dma[k],
                                   ev_umul_done.get((k, ic - 2)) or ev_umul_done.get((k - 1, ic + 2)),
                                   ev_dn_free.get((tc, bu)) if e9 == 0 else None),
                            sig="t" if cc == NCT - 1 else None)
                    ev_sl = P.op(
                        "s", (lambda e_, b_=bg: e_.activation(sgb, bk(b_), ACT.Silu)),
                        waits=(ev_gmm, ev_prev_silu, ev_prev_hwr), sig="s")
                    ev_prev_silu = ev_sl
                    ev_sil_done[(k, ic)] = ev_sl
                    ev_umul_done[(k, ic)] = P.op(
                        "v", (lambda e_, b_=bu: e_.tensor_mul(sgb, bk(b_), sgb)),
                        waits=(ev_sl, ev_umm), sig="v")
                    hdst = ab(MOE_H[e9], T)
                    if e9 < 8:
                        ev_h = P.op(
                            "v",
                            (lambda e_, i_=ic, hd=hdst:
                             e_.tensor_mul(hd[:, i_ * 512:(i_ + 1) * 512], sgb, combrep[:, :])),
                            waits=(ev_cre,
                                   hfree[e9] if ic == 0 else None,
                                   ev_down_read if ic == 0 else None),
                            sig="v")
                    else:
                        ev_h = P.op(
                            "v",
                            (lambda e_, i_=ic, hd=hdst:
                             e_.tensor_copy(hd[:, i_ * 512:(i_ + 1) * 512], sgb)),
                            waits=(hfree[e9] if ic == 0 else None,
                                   ev_down_read if ic == 0 else None),
                            sig="v")
                    ev_hwr[(k, ic)] = ev_h
                    ev_prev_hwr = ev_h
                slot_free[sgu] = ev_gmm
                slot_free[suu] = ev_umm
                ev_hall[(tc, e9)] = ev_hwr[(k, 3)]
            for e9 in range(2):
                k = tc * 9 + e9
                sdd = 4 + e9 % 2
                ev_wd_dma[k] = P.dma("g", wslot(sdd, 4 * D), d_wdd[e9],
                                     waits=(slot_free[sdd],))
            ev_dn_last = {}
            for e9 in range(_K_MOE_E):
                k = tc * 9 + e9
                sdd = 4 + e9 % 2
                ev_mm_e = None
                for cc in range(NCT):
                    for ic in range(4):
                        ev_mm_e = P.op(
                            "t",
                            (lambda e_, c=cc, i_=ic, ee=e9, s_=sdd:
                             e_.matmul(bk(c),
                                       wslot(s_, 4 * D)[:, i_ * D + c * 128: i_ * D + c * 128 + 128],
                                       ab(MOE_H[ee], T)[:, i_ * 512:(i_ + 1) * 512],
                                       start=(ee == 0 and i_ == 0),
                                       stop=(ee == _K_MOE_E - 1 and i_ == 3))),
                            waits=(ev_wd_dma[k], ev_hall[(tc, e9)],
                                   ev_dn_free.get((tc, cc)),
                                   ev_cre_prev if (cc == 3 and e9 == 0) else None,
                                   ev_prev_silu if (cc >= 4 and e9 == 0) else None,
                                   ev_prev_hwr if (cc >= 4 and e9 == 0) else None),
                            sig="t" if ic == 3 else None)
                    ev_dn_last[cc] = ev_mm_e
                slot_free[sdd] = ev_mm_e
                if e9 + 2 < _K_MOE_E:
                    k2 = tc * 9 + e9 + 2
                    ev_wd_dma[k2] = P.dma("g", wslot(sdd, 4 * D), d_wdd[e9 + 2],
                                          waits=(ev_mm_e,))
            ev_down_read = ev_mm_e
            for i in range(9):
                hfree[i] = None
            for cc in range(NCT):
                ev_e = P.op(
                    "v",
                    (lambda e_, c=cc, j=tc:
                     e_.scalar_tensor_tensor(X[c][:, j * 512:(j + 1) * 512], bk(c), 1.0,
                                             X[c][:, j * 512:(j + 1) * 512], ALU.mult, ALU.add)),
                    waits=(ev_dn_last[cc],), sig="v")
                ev_dn_free[(tc + 1, cc)] = ev_e
                ev_dn_evac[cc % 3] = ev_e
            ev_dn_done = ev_dn_evac[(NCT - 1) % 3]

        ev_moe_done = ev_dn_done

        if upto == "moe":
            dump_X("X", (ev_moe_done, ev_dn_evac.get(0), ev_dn_evac.get(1), ev_dn_evac.get(2)))
            _finish(P, sems)
            P.realize(nc, sems)
            return nc

        ev_dn_all = (ev_dn_evac.get(0), ev_dn_evac.get(1), ev_dn_evac.get(2))
        ev_x1b_free = [None, None]
        ev_xadd = [None] * NCT
        for ct in range(NCT):
            sl2 = ct % 2
            x1b = af(O_X1B + sl2 * 8192, T)
            ev_ld = P.dma("y", x1b, x1sc[ct * 128:(ct + 1) * 128, :],
                          waits=(ev_x1b_free[sl2], ev_moe_done,
                                 ev_dn_evac.get(0), ev_dn_evac.get(1), ev_dn_evac.get(2)))
            ev_xadd[ct] = P.op(
                "v", (lambda e_, c=ct, b_=x1b: e_.tensor_add(X[c][:, :], X[c][:, :], b_)),
                waits=(ev_ld,), sig="v")
            ev_x1b_free[sl2] = ev_xadd[ct]
        ev_tm_free = [None, None]
        ev_tr_prev = {0: None, 1: None}
        ev_ln2_first = tuple(x for x in ev_dn_all if x is not None)
        for tt in range(NT):
            sl = tt % 2
            tmb = af(O_TM + sl * 4096, D)
            sqb = af(O_SQ2 + sl * 4096, D)
            ev_tr = {}
            for cc in range(NCT):
                b = cc // 4
                ev_tr[b] = P.op(
                    "t",
                    (lambda e_, c=cc, t_=tt, b_=b:
                     e_.transpose(bk(b_)[:, (c % 4) * 128:(c % 4 + 1) * 128],
                                  X[c][:, t_ * 128:(t_ + 1) * 128], ident_f[:, :])),
                    waits=ev_ln2_first + (ev_xadd[cc],) + ((ev_tr_prev[b],) if cc % 4 == 0 else ()),
                    sig="t" if cc % 4 == 3 else None)
            ev_ln2_first = ()
            ev_cpa = P.op(
                "v", (lambda e_, o=tmb: e_.tensor_copy(o[:, 0:512], bk(0))),
                waits=(ev_tr[0], ev_tm_free[sl]), sig="v")
            ev_cpb = P.op(
                "v", (lambda e_, o=tmb: e_.tensor_copy(o[:, 512:1024], bk(1))),
                waits=(ev_tr[1],), sig="v")
            ev_tr_prev[0] = ev_cpa
            ev_tr_prev[1] = ev_cpb
            ev_sqq = P.op(
                "s", (lambda e_, a=tmb, o=sqb: e_.activation(o, a, ACT.Square)),
                waits=(ev_cpa, ev_cpb, ev_tm_free[sl]), sig="s")
            cb = 8 + 6 * sl
            ev_s1 = P.op("v", (lambda e_, a=tmb, c=cb: e_.reduce_sum(colw[:, c:c + 1], a, axis=AX)),
                         waits=(ev_cpb,), sig="v")
            ev_s2 = P.op("v", (lambda e_, o=sqb, c=cb: e_.reduce_sum(colw[:, c + 1:c + 2], o, axis=AX)),
                         waits=(ev_sqq,), sig="v")
            ev_c1 = P.op("v", (lambda e_, c=cb: e_.tensor_scalar(colw[:, c + 2:c + 3], colw[:, c:c + 1],
                                                                 1.0 / D, None, ALU.mult)),
                         hwaits=(ev_s1,), sig="v")
            ev_c2 = P.op("v", (lambda e_, c=cb: e_.tensor_scalar(colw[:, c + 3:c + 4], colw[:, c + 1:c + 2],
                                                                 1.0 / D, None, ALU.mult)),
                         hwaits=(ev_s2,), sig="v")
            ev_c3 = P.op("v", (lambda e_, c=cb: e_.tensor_mul(colw[:, c + 4:c + 5], colw[:, c + 2:c + 3],
                                                              colw[:, c + 2:c + 3])),
                         hwaits=(ev_c1,), sig="v")
            ev_c4 = P.op("v", (lambda e_, c=cb: e_.tensor_sub(colw[:, c + 3:c + 4], colw[:, c + 3:c + 4],
                                                              colw[:, c + 4:c + 5])),
                         hwaits=(ev_c2, ev_c3), sig="v")
            ev_c5 = P.op("v", (lambda e_, c=cb: e_.tensor_scalar(colw[:, c + 3:c + 4], colw[:, c + 3:c + 4],
                                                                 LN_EPS, None, ALU.add)),
                         hwaits=(ev_c4,), sig="v")
            ev_lnx = P.op("s", (lambda e_, c=cb: e_.activation(colw[:, c + 4:c + 5], colw[:, c + 3:c + 4],
                                                               ACT.Ln)),
                          waits=(ev_c5,), sig="s")
            ev_rx = P.op("s", (lambda e_, c=cb: e_.activation(colw[:, c + 4:c + 5], colw[:, c + 4:c + 5],
                                                              ACT.Exp, scale=-0.5)),
                         waits=(ev_lnx,), sig="s")
            ev_app = P.op(
                "v",
                (lambda e_, a=tmb, o=sqb, c=cb: e_.tensor_scalar(o, a, colw[:, c + 2:c + 3],
                                                                 colw[:, c + 4:c + 5],
                                                                 ALU.subtract, ALU.mult)),
                waits=(ev_rx,), hwaits=(ev_c1,), sig="v")
            ev_out = P.dma("y", out[tt * 128:(tt + 1) * 128, :], sqb, waits=(ev_app,))
            ev_tm_free[sl] = ev_out

        _finish(P, sems)
        P.realize(nc, sems)

    return nc


def _finish(P, sems):
    for sem in QSEMS:
        val = P.cnt[sem]
        if val:
            P.op("y", (lambda e_, s_=sem, v_=val: e_.wait_ge(sems[s_], v_)))
            P.op("t", (lambda e_, s_=sem, v_=val: e_.wait_ge(sems[s_], v_)))



_DEBUG_DUMP = ()


def kernel(**inputs):
    consts = _host_consts()
    weights = _host_weights(inputs)
    x_full = np.ascontiguousarray(np.asarray(inputs["L0_attn_x_t"], dtype=np.float32))

    nc = build_program(upto=_BUILD_UPTO, debug_dump=_DEBUG_DUMP)

    base = {}
    base.update({k: np.ascontiguousarray(v) for k, v in consts.items()})
    base.update({k: np.ascontiguousarray(v) for k, v in weights.items()})
    in_maps = []
    for core in range(8):
        m = dict(base)
        m["xin"] = np.ascontiguousarray(x_full[core])
        in_maps.append(m)

    res = run_bass_kernel_spmd(nc, in_maps, core_ids=list(range(8)))
    if _DEBUG_DUMP:
        kernel._last_results = res.results
    outp = np.stack([res.results[i]["out"] for i in range(8)], axis=0)
    return outp


# version 17
# speedup vs baseline: 1.1806x; 1.1394x over previous
import numpy as np
import ml_dtypes
from contextlib import ExitStack

import concourse.bass as bass
import concourse.mybir as mybir
from concourse.bass_utils import run_bass_kernel_spmd

F32 = mybir.dt.float32
BF16 = mybir.dt.bfloat16
AX = mybir.AxisListType.X
ALU = mybir.AluOpType
ACT = mybir.ActivationFunctionType
BFNP = ml_dtypes.bfloat16

T = 2048
D = 1024
NT = T // 128
NCT = D // 128
THETA = 1e5
LN_EPS = 1e-5
GEPS = 0.5 + 1e-8

_BUILD_UPTO = "full"
import os as _os
_ATTN_PAIRS = int(_os.environ.get("K_ATTN_PAIRS", "8"))
_ATTN_QT = int(_os.environ.get("K_ATTN_QT", "16"))
_K_WO = int(_os.environ.get("K_WO", "1"))
_K_MOE_TC = int(_os.environ.get("K_MOE_TC", "4"))
_K_MOE_E = int(_os.environ.get("K_MOE_E", "9"))
_PTEV = int(_os.environ.get("K_PTEV", "1"))
_STAGES = ("mingru", "ln1", "pre", "attn", "rms2", "gate", "moe", "full")


def _stage_ge(a, b):
    return _STAGES.index(a) >= _STAGES.index(b)



def _host_consts():
    c = {}
    c["cident_bf"] = np.eye(128, dtype=BFNP)
    c["cident_f"] = np.eye(128, dtype=np.float32)
    tri = np.where(np.arange(128)[None, :] <= np.arange(128)[:, None], 0.0, -1e9)
    c["ctrimask"] = tri.astype(np.float32)
    c["ctrimask_bf"] = tri.astype(BFNP)
    csel = np.zeros((8, 8 * 128), np.float32)
    for e in range(8):
        csel[e, e * 128:(e + 1) * 128] = 1.0
    c["csel"] = csel
    t = np.arange(T, dtype=np.float64)
    freqs = THETA ** (-np.arange(16, dtype=np.float64) / 16.0)
    ang = np.outer(freqs, t)
    cosv, sinv = np.cos(ang), np.sin(ang)
    C = np.zeros((128, T), np.float64)
    S = np.zeros((128, T), np.float64)
    for base in (32, 96):
        C[base:base + 16] = cosv
        C[base + 16:base + 32] = cosv
        S[base:base + 16] = -sinv
        S[base + 16:base + 32] = sinv
    C[0:32] = 1.0
    C[64:96] = 1.0
    c["ccq"] = C.astype(BFNP)
    c["csq"] = S.astype(BFNP)
    return c


def _q_col_perm():
    perm = np.empty(D, np.int64)
    for j in range(D):
        jp, r = divmod(j, 128)
        h = 2 * jp + r // 64
        rr = r % 64
        if rr < 32:
            o = 64 * h + rr
        elif rr < 48:
            o = 64 * h + 32 + 2 * (rr - 32)
        else:
            o = 64 * h + 32 + 2 * (rr - 48) + 1
        perm[j] = o
    return perm


def _host_weights(ins):
    w = {}
    for nm, key in (("wbd_f", "L0_attn_Wf"), ("wbd_i", "L0_attn_Wi"),
                    ("wbd_h", "L0_attn_Wh")):
        W = np.asarray(ins[key], dtype=np.float32)
        outw = np.zeros((128, NCT * 128), np.float32)
        for ct in range(NCT):
            for gl in range(16):
                g = 16 * ct + gl
                outw[8 * gl:8 * gl + 8, ct * 128 + 8 * gl: ct * 128 + 8 * gl + 8] = W[g].T
        w[nm] = outw

    Wq = np.asarray(ins["L0_ffn_Wq"], dtype=np.float32)[:, _q_col_perm()]
    w["wq"] = Wq.reshape(NCT, 128, D).transpose(1, 0, 2).reshape(128, NCT * D)

    Wkva = np.asarray(ins["L0_ffn_Wkva"], dtype=np.float32)
    kva = np.zeros((D, 192), np.float32)
    kva[:, 0:128] = Wkva[:, 0:128]
    kva[:, 160:176] = Wkva[:, 128:160][:, 0::2]
    kva[:, 176:192] = Wkva[:, 128:160][:, 1::2]
    w["wkva"] = kva.reshape(NCT, 128, 192).transpose(1, 0, 2).reshape(128, NCT * 192)

    w["wkvb"] = np.asarray(ins["L0_ffn_Wkvb"], dtype=np.float32)

    Wo = np.asarray(ins["L0_ffn_Wo"], dtype=np.float32)
    rows = np.empty(512, np.int64)
    for g in range(4):
        for a in range(4):
            h = 4 * g + a
            rows[g * 128 + a * 32: g * 128 + a * 32 + 32] = 64 * h + np.arange(32)
    w["wo"] = Wo[rows].reshape(4, 128, D).transpose(1, 0, 2).reshape(128, 4 * D)

    w["wgate"] = np.asarray(ins["L0_ffn_Wgate"], dtype=np.float32) \
        .reshape(NCT, 128, 8).transpose(1, 0, 2).reshape(128, NCT * 8)

    gg = np.concatenate([np.asarray(ins["L0_ffn_Wg"], dtype=np.float32),
                         np.asarray(ins["L0_ffn_sWg"], dtype=np.float32)], 0)
    uu = np.concatenate([np.asarray(ins["L0_ffn_Wu"], dtype=np.float32),
                         np.asarray(ins["L0_ffn_sWu"], dtype=np.float32)], 0)
    dd = np.concatenate([np.asarray(ins["L0_ffn_Wd"], dtype=np.float32),
                         np.asarray(ins["L0_ffn_sWd"], dtype=np.float32)], 0)
    w["wgu"] = gg.reshape(9, NCT, 128, 512).transpose(0, 2, 1, 3).reshape(9, 128, NCT * 512)
    w["wuu"] = uu.reshape(9, NCT, 128, 512).transpose(0, 2, 1, 3).reshape(9, 128, NCT * 512)
    w["wdd"] = dd.reshape(9, 4, 128, D).transpose(0, 2, 1, 3).reshape(9, 128, 4 * D)

    w["wln1"] = np.asarray(ins["L0_ffn_w_ln1"], dtype=np.float32)
    w["wln2"] = np.asarray(ins["L0_ffn_w_ln2"], dtype=np.float32)
    w["ebias"] = np.asarray(ins["L0_ffn_expert_bias"], dtype=np.float32)
    return w



ENGS = ("t", "v", "s", "g", "y")
QLANES = {"y": 4, "g": 4, "s": 2}
QSEMS = [f"q{q}{i}" for q, n in QLANES.items() for i in range(n)]


class Prog:
    def __init__(self):
        self.stream = {e: [] for e in ENGS}
        self.cnt = {k: 0 for k in ("t", "v", "s", "g", *QSEMS)}
        self.lane_rr = {q: 0 for q in QLANES}

    def op(self, eng, fn, waits=(), sig=None, hwaits=()):
        w = tuple(ev for ev in waits if ev is not None)
        hw = tuple(ev for ev in hwaits if ev is not None)
        for ev_ in w + hw:
            assert isinstance(ev_, tuple) and len(ev_) == 2 and isinstance(ev_[1], int), \
                f"bad wait event: {ev_!r}"
        ev = None
        if sig is not None:
            self.cnt[sig] += 16 if sig.startswith("q") else 1
            ev = (sig, self.cnt[sig])
        self.stream[eng].append((w, fn, sig, hw))
        return ev

    def dma(self, queue, dst, src, waits=(), slow=False):
        lane = self.lane_rr[queue]
        self.lane_rr[queue] = (lane + 1) % QLANES[queue]
        sem = f"q{queue}{lane}"
        prev = self.cnt[sem]
        w = tuple(waits) + (((sem, prev),) if prev else ())
        return self.op(queue,
                       (lambda e, d=dst, s_=src, sl=slow:
                        e.dma_start(d, s_, allow_slow_non_contiguous=sl)),
                       waits=w, sig=sem)

    def realize(self, nc, sems):
        waited = {e: {} for e in ENGS}

        def runner(ename):
            def run(eng):
                wd = waited[ename]
                for (w, fn, sig, hw) in self.stream[ename]:
                    for (snm, val) in w:
                        if snm == ename and not (ename == "s" and snm == "s"):
                            continue
                        if wd.get(snm, -1) >= val:
                            continue
                        eng.wait_ge(sems[snm], val)
                        wd[snm] = val
                    for (snm, val) in hw:
                        if wd.get(snm, -1) >= val:
                            continue
                        eng.wait_ge(sems[snm], val)
                        wd[snm] = val
                    ins = fn(eng)
                    if sig is not None:
                        ins.then_inc(sems[sig], 16 if sig.startswith("q") else 1)
            return run

        with nc.Block() as block:
            block.tensor(runner("t"))
            block.vector(runner("v"))
            block.scalar(runner("s"))
            block.gpsimd(runner("g"))
            block.sync(runner("y"))


AR_BYTES = 144832

O_WGATE = 0
O_WQ = 128
O_WKVA = 16512
O_WKVB = 19584
O_CCQ = 20608
O_CSQ = 24704
O_WO = 28800

O_WBD = 36992
O_XBF = 43136
O_MGBUF = 51328
O_XSTAGE = 116864

O_XN = 36992
O_LN_ROWS = 69760
O_LN_REPS = 102528
O_LN_SQ = 127104

O_KP = 36992
O_P = 45184
O_PT = 53376
O_YGRP = 61568
O_SPARE = 65664
O_QH2 = 69760
O_KVFM = 102528
O_KVTM = 118912
O_LAT = 135296
O_KR = 139392

MOE_WSLOT = [128, 8320, 16512, 69760, 77952, 86144]
MOE_H = [36992, 41088, 45184, 49280, 53376, 57472, 61568, 94336, 98432]
XN2_OFF = [24704, 28800, 32896, 102528, 106624, 110720, 114816, 118912]
O_M2SQ = [45184, 53376]
O_M2ROW = 61568
O_MREP = 36992
O_YTM = 65664
O_COMB = 123008
O_LG = 65664
O_PR = 66176
O_MS = 66688
O_MS2 = 67200
O_COMBREP = 67712
O_SG = 135296
O_CSEL = 131200

O_TM = 128
O_SQ2 = 8320
O_X1B = 16512


def build_program(upto="full", debug_dump=()):
    nc = bass.Bass("TRN2", target_bir_lowering=False, detect_race_conditions=False)

    def din(name, shape, dt=F32):
        return nc.dram_tensor(name, list(shape), dt, kind="ExternalInput").ap()

    xin = din("xin", (T, D))
    d_cident_bf = din("cident_bf", (128, 128), BF16)
    d_cident_f = din("cident_f", (128, 128))
    d_ctrimask = din("ctrimask", (128, 128))
    d_ctrimask_bf = din("ctrimask_bf", (128, 128), BF16)
    d_csel = din("csel", (8, 8 * 128))
    d_ccq = din("ccq", (128, T), BF16)
    d_csq = din("csq", (128, T), BF16)
    d_wbd = {k: din("wbd_" + k, (128, NCT * 128)) for k in "fih"}
    d_wq = din("wq", (128, NCT * D))
    d_wkva = din("wkva", (128, NCT * 192))
    d_wkvb = din("wkvb", (128, 512))
    d_wo = din("wo", (128, 4 * D))
    d_wgate = din("wgate", (128, NCT * 8))
    d_wgu = din("wgu", (9, 128, NCT * 512))
    d_wuu = din("wuu", (9, 128, NCT * 512))
    d_wdd = din("wdd", (9, 128, 4 * D))
    d_wln1 = din("wln1", (D,))
    d_wln2 = din("wln2", (D,))
    d_ebias = din("ebias", (8,))

    out = nc.dram_tensor("out", [T, D], F32, kind="ExternalOutput").ap()
    x1sc = nc.dram_tensor("x1sc", [D, T], F32, kind="Internal").ap()
    dbg = {}
    for nm in debug_dump:
        shape = [8, T] if nm == "comb" else [D, T]
        dbg[nm] = nc.dram_tensor("dbg_" + nm, list(shape), F32, kind="ExternalOutput").ap()

    P = Prog()

    with ExitStack() as ctx:
        e = ctx.enter_context

        X = [e(nc.sbuf_tensor(f"X{i}", [128, T], F32)) for i in range(NCT)]
        ident_bf = e(nc.sbuf_tensor("ident_bf", [128, 128], BF16))
        ident_f = e(nc.sbuf_tensor("ident_f", [128, 128], F32))
        trimask = e(nc.sbuf_tensor("trimask", [128, 128], F32))
        trimask_bf = e(nc.sbuf_tensor("trimask_bf", [128, 128], BF16))
        gam1 = e(nc.sbuf_tensor("gam1", [128, NCT], F32))
        gam2 = e(nc.sbuf_tensor("gam2", [128, NCT], F32))
        ebias_sb = e(nc.sbuf_tensor("ebias_sb", [1, 8], F32))
        ebias_rep = e(nc.sbuf_tensor("ebias_rep", [128, 8], F32))
        colw = e(nc.sbuf_tensor("colw", [128, 24], F32))
        ones_c = e(nc.sbuf_tensor("ones_c", [128, 1], F32))
        ones_r = e(nc.sbuf_tensor("ones_r", [1, 128], F32))
        ysc = e(nc.sbuf_tensor("ysc", [128, 64], BF16))
        arena = e(nc.sbuf_tensor("arena", [128, AR_BYTES // 2], BF16))

        pbA = e(nc.psum_tensor("pbA", [128, 2048], F32))
        pbB = e(nc.psum_tensor("pbB", [128, 2048], F32))

        def bk(i):
            t_ = pbA if i < 4 else pbB
            j = i % 4
            return t_[:, j * 512:(j + 1) * 512]

        def bkb(i):
            t_ = pbA if i < 4 else pbB
            j = i % 4
            return t_[:, j * 512:(j + 1) * 512].bitcast(BF16)

        sems = {k: e(nc.semaphore("sem_" + k))
                for k in ("t", "v", "s", "g", *QSEMS)}

        def ab(off, n, p0=0, p1=128):
            return arena[p0:p1, off // 2: off // 2 + n]

        def af(off, n, p0=0, p1=128):
            assert off % 4 == 0
            return arena[p0:p1, off // 2: off // 2 + 2 * n].bitcast(F32)

        ev_id_bf = P.dma("y", ident_bf[:, :], d_cident_bf)
        ev_id_f = P.dma("y", ident_f[:, :], d_cident_f)
        P.dma("y", trimask[:, :], d_ctrimask)
        ev_tri_bf = P.dma("y", trimask_bf[:, :], d_ctrimask_bf)
        P.dma("y", ab(O_CCQ, T), d_ccq)
        ev_tab = P.dma("y", ab(O_CSQ, T), d_csq)
        P.dma("y", gam1[:, :], d_wln1.rearrange("(a b) -> b a", b=128), slow=True)
        ev_gam2 = P.dma("y", gam2[:, :], d_wln2.rearrange("(a b) -> b a", b=128), slow=True)
        ev_eb = P.dma("y", ebias_sb[0:1, :], d_ebias.rearrange("(o a) -> o a", o=1))
        ev_ones = P.op("g", lambda e_: e_.memset(ones_c[:, :], 1.0), sig="g")
        ev_ones2 = P.op("g", lambda e_: e_.memset(ones_r[:, :], 1.0), sig="g")

        ev_wbd = {}
        for i, k in enumerate("fih"):
            ev_wbd[k] = P.dma("g", ab(O_WBD + i * 2048, NCT * 128), d_wbd[k])
        ev_wq = P.dma("g", ab(O_WQ, NCT * D), d_wq)
        ev_wkva = P.dma("g", ab(O_WKVA, NCT * 192), d_wkva)
        ev_wkvb = P.dma("g", ab(O_WKVB, 512), d_wkvb)
        ev_wo = P.dma("g", ab(O_WO, 4 * D), d_wo)
        ev_wgate = P.dma("g", ab(O_WGATE, NCT * 8), d_wgate)

        ev_xready = [None] * NCT
        ev_stage_free = [None] * 4
        bankfree = {j: None for j in range(8)}
        for grp in range(4):
            ev_st = []
            for j in range(4):
                tt = grp * 4 + j
                st = af(O_XSTAGE + j * 4096, D)
                ev_st.append(P.dma("y", st, xin[tt * 128:(tt + 1) * 128, :],
                                   waits=(ev_stage_free[j],)))
            for ct in range(NCT):
                bank = ct % 4
                evt = None
                for j in range(4):
                    st = af(O_XSTAGE + j * 4096, D)
                    evt = P.op(
                        "t",
                        (lambda e_, b=bank, jj=j, s_=st, c=ct:
                         e_.transpose(bk(b)[:, jj * 128:(jj + 1) * 128],
                                      s_[:, c * 128:(c + 1) * 128], ident_f[:, :])),
                        waits=(ev_st[j], bankfree[bank], ev_id_f),
                        sig="t" if j == 3 else None)
                ev_ev = P.op(
                    "v",
                    (lambda e_, b=bank, c=ct, g=grp:
                     e_.tensor_copy(X[c][:, g * 512:(g + 1) * 512], bk(b))),
                    waits=(evt,), sig="v")
                bankfree[bank] = ev_ev
                if ct == NCT - 1:
                    for j in range(4):
                        ev_stage_free[j] = ev_ev
                if grp == 3:
                    ev_xready[ct] = ev_ev

        def mgb(seti, k):
            return af(O_MGBUF + (seti * 4 + k) * 8192, T)

        def xbf_t(seti):
            return ab(O_XBF + seti * 4096, T)

        wbd_bf = {k: ab(O_WBD + i * 2048, NCT * 128) for i, k in enumerate("fih")}

        ev_scan = [None] * NCT
        ev_sig2 = [None] * NCT
        ev_diff = [None] * NCT
        ev_hcopy = [None] * NCT
        ev_ef = [None] * NCT
        ev_ei = [None] * NCT
        ev_lnf = [None] * NCT
        ev_lni = [None] * NCT
        ev_f_mm = [None] * NCT
        ev_i_mm = [None] * NCT
        ev_h_mm = [None] * NCT
        ev_xbf = [None] * NCT

        for ct in range(NCT):
            si = ct % 2
            ev_xbf[ct] = P.op(
                "v", (lambda e_, c=ct, s=si: e_.tensor_copy(xbf_t(s)[:, :], X[c][:, :])),
                waits=(ev_xready[ct], ev_scan[ct - 2] if ct >= 2 else None),
                sig="v")
            for j in range(4):
                ev_f_mm[ct] = P.op(
                    "t",
                    (lambda e_, c=ct, jj=j, s=si:
                     e_.matmul(bk(jj), wbd_bf["f"][:, c * 128:(c + 1) * 128],
                               xbf_t(s)[:, jj * 512:(jj + 1) * 512],
                               start=True, stop=True)),
                    waits=(ev_xbf[ct], ev_wbd["f"],
                           ev_hcopy[ct - 1] if ct >= 1 else bankfree[j]),
                    sig="t" if j == 3 else None)
            for j in range(4):
                ev_i_mm[ct] = P.op(
                    "t",
                    (lambda e_, c=ct, jj=j, s=si:
                     e_.matmul(bk(4 + jj), wbd_bf["i"][:, c * 128:(c + 1) * 128],
                               xbf_t(s)[:, jj * 512:(jj + 1) * 512],
                               start=True, stop=True)),
                    waits=(ev_wbd["i"], ev_ei[ct - 1] if ct >= 1 else None),
                    sig="t" if j == 3 else None)
            Pd, Ph, Pf, Pm = (mgb(si, 0), mgb(si, 1), mgb(si, 2), mgb(si, 3))
            wb = ev_sig2[ct - 2] if ct >= 2 else None
            wb2 = ev_scan[ct - 2] if ct >= 2 else None
            ev_ef[ct] = P.op(
                "s", (lambda e_, p=Pd: e_.activation(p, pbA[:, :], ACT.Exp, scale=-1.0)),
                waits=(ev_f_mm[ct], wb, wb2), sig="s")
            ev_lnf[ct] = P.op(
                "s", (lambda e_, p=Pd: e_.activation(p, p, ACT.Ln, bias=1.0)),
                waits=(ev_ef[ct],), sig="s")
            ev_ei[ct] = P.op(
                "s", (lambda e_, p=Pf: e_.activation(p, pbB[:, :], ACT.Exp, scale=-1.0)),
                waits=(ev_i_mm[ct], wb, wb2), sig="s")
            ev_lni[ct] = P.op(
                "s", (lambda e_, p=Pf: e_.activation(p, p, ACT.Ln, bias=1.0)),
                waits=(ev_ei[ct],), sig="s")
            for j in range(4):
                ev_h_mm[ct] = P.op(
                    "t",
                    (lambda e_, c=ct, jj=j, s=si:
                     e_.matmul(bk(jj), wbd_bf["h"][:, c * 128:(c + 1) * 128],
                               xbf_t(s)[:, jj * 512:(jj + 1) * 512],
                               start=True, stop=True)),
                    waits=(ev_ef[ct], ev_wbd["h"]),
                    sig="t" if j == 3 else None)
            ev_diff[ct] = P.op(
                "v", (lambda e_, a=Pd, b=Pf: e_.tensor_sub(a, a, b)),
                waits=(ev_lnf[ct], ev_lni[ct]), sig="v")
            ev_hcopy[ct] = P.op(
                "v", (lambda e_, p=Ph: e_.tensor_copy(p, pbA[:, :])),
                waits=(ev_h_mm[ct],), sig="v")
            if ct % 2 == 1:
                for c2 in (ct - 1, ct):
                    s2 = c2 % 2
                    Qd, Qh, Qf, Qm = (mgb(s2, 0), mgb(s2, 1), mgb(s2, 2), mgb(s2, 3))
                    ev_fp = P.op(
                        "s", (lambda e_, a=Qd, o=Qf: e_.activation(o, a, ACT.Sigmoid, scale=-1.0)),
                        waits=(ev_diff[c2],), sig="s")
                    ev_sig2[c2] = P.op(
                        "s", (lambda e_, a=Qh, o=Qd: e_.activation(o, a, ACT.Sigmoid)),
                        waits=(ev_hcopy[c2], ev_fp), sig="s")
                    P.op("v", (lambda e_, a=Qh, o=Qm: e_.tensor_single_scalar(o, a, 0.0, ALU.is_ge)),
                         waits=(ev_sig2[c2],))
                    P.op("v", (lambda e_, a=Qh: e_.tensor_scalar(a, a, GEPS, None, ALU.add)))
                    P.op("v", (lambda e_, a=Qh, b=Qd: e_.tensor_sub(a, a, b)))
                    P.op("v", (lambda e_, a=Qh, b=Qm: e_.tensor_mul(a, a, b)))
                    P.op("v", (lambda e_, a=Qh, b=Qd: e_.tensor_add(a, a, b)))
                    P.op("v", (lambda e_, a=Qf, o=Qm:
                               e_.tensor_scalar(o, a, -1.0, 1.0, ALU.mult, ALU.add)))
                    P.op("v", (lambda e_, a=Qm, b=Qh: e_.tensor_mul(a, a, b)))
                    P.op("v", (lambda e_, f=Qf, v_=Qm, o=Qh:
                               e_.tensor_tensor_scan(o, f, v_, 0.0, ALU.mult, ALU.add)))
                    ev_scan[c2] = P.op(
                        "v", (lambda e_, c=c2, hh=Qh: e_.tensor_add(X[c][:, :], X[c][:, :], hh)),
                        sig="v")

        def dump_X(tag, waits):
            if tag in dbg:
                for ct in range(NCT):
                    P.dma("y", dbg[tag][ct * 128:(ct + 1) * 128, :], X[ct][:, :],
                          waits=waits)

        if upto == "mingru":
            dump_X("X", tuple(ev_scan))
            _finish(P, sems)
            P.realize(nc, sems)
            return nc

        def xn_t(ct, which):
            off = (O_XN + ct * 4096) if which == 1 else XN2_OFF[ct]
            return ab(off, T)

        rowsl = [af(O_LN_ROWS + i * 8192, T, 0, 1) for i in range(4)]
        repr_ = af(O_LN_REPS, T)
        repmr = af(O_LN_REPS + 8192, T)
        repr2 = af(O_LN_REPS + 16384, T)

        ev_sq = [None, None]
        ev_sums = None
        ev_sumsq = None
        ev_sumsq_ct = [None] * NCT
        for ct in range(NCT):
            sqb = af(O_LN_SQ + (ct % 2) * 8192, T)
            ev_sq[ct % 2] = P.op(
                "s", (lambda e_, c=ct, o=sqb: e_.activation(o, X[c][:, :], ACT.Square)),
                waits=(ev_scan[ct],
                       ev_sumsq_ct[ct - 2] if ct >= 2 else None), sig="s")
            for j in range(4):
                ev_sums = P.op(
                    "t",
                    (lambda e_, c=ct, jj=j:
                     e_.matmul(bk(jj)[0:1, :], ones_c[:, 0:1],
                               X[c][:, jj * 512:(jj + 1) * 512],
                               start=(c == 0), stop=(c == NCT - 1))),
                    waits=(ev_scan[ct], ev_ones,
                           ev_hcopy[NCT - 1] if ct == 0 else None),
                    sig="t" if j == 3 else None)
            for j in range(4):
                ev_sumsq = P.op(
                    "t",
                    (lambda e_, o=sqb, jj=j, c=ct:
                     e_.matmul(bk(4 + jj)[0:1, :], ones_c[:, 0:1],
                               o[:, jj * 512:(jj + 1) * 512],
                               start=(c == 0), stop=(c == NCT - 1))),
                    waits=(ev_sq[ct % 2],
                           ev_ei[NCT - 1] if ct == 0 else None),
                    sig="t" if j == 3 else None)
            ev_sumsq_ct[ct] = ev_sumsq

        ev_m = P.op("v", (lambda e_: e_.tensor_scalar(rowsl[0], pbA[0:1, :], 1.0 / D, None, ALU.mult)),
                    waits=(ev_sums,), sig="v")
        ev_q = P.op("v", (lambda e_: e_.tensor_scalar(rowsl[1], pbB[0:1, :], 1.0 / D, None, ALU.mult)),
                    waits=(ev_sumsq,), sig="v")
        P.op("v", (lambda e_: e_.tensor_mul(rowsl[2], rowsl[0], rowsl[0])))
        ev_var = P.op("v", (lambda e_: e_.tensor_sub(rowsl[1], rowsl[1], rowsl[2])), sig="v")
        ev_vare = P.op("v", (lambda e_: e_.tensor_scalar(rowsl[1], rowsl[1], LN_EPS, None, ALU.add)),
                       waits=(ev_var,), sig="v")
        ev_lnv = P.op("s", (lambda e_: e_.activation(rowsl[2], rowsl[1], ACT.Ln)),
                      waits=(ev_vare,), sig="s")
        ev_r = P.op("s", (lambda e_: e_.activation(rowsl[3], rowsl[2], ACT.Exp, scale=-0.5)),
                    waits=(ev_lnv,), sig="s")
        ev_rr = P.op("s", (lambda e_: e_.activation(rowsl[2], rowsl[2], ACT.Exp, scale=-1.0)),
                     waits=(ev_lnv, ev_r), sig="s")
        ev_msu = P.op("v", (lambda e_: e_.scalar_tensor_tensor(rowsl[2], rowsl[1], -LN_EPS,
                                                               rowsl[2], ALU.add, ALU.mult)),
                      waits=(ev_rr,), sig="v")
        ev_ms2 = P.op("v", (lambda e_: e_.tensor_scalar(rowsl[2], rowsl[2], LN_EPS, None, ALU.add)),
                      sig="v")
        ev_ln2 = P.op("s", (lambda e_: e_.activation(rowsl[2], rowsl[2], ACT.Ln)),
                      waits=(ev_ms2,), sig="s")
        ev_r2 = P.op("s", (lambda e_: e_.activation(rowsl[2], rowsl[2], ACT.Exp, scale=-0.5)),
                     waits=(ev_ln2,), sig="s")
        ev_mr = P.op("v", (lambda e_: e_.tensor_mul(rowsl[0], rowsl[0], rowsl[3])),
                     waits=(ev_r,), sig="v")

        def replicate(dst, src_row, waits, bank_wait):
            last = None
            for j in range(4):
                last = P.op(
                    "t",
                    (lambda e_, jj=j, s_=src_row:
                     e_.matmul(bk(jj), ones_r[0:1, :],
                               s_[0:1, jj * 512:(jj + 1) * 512], start=True, stop=True)),
                    waits=tuple(waits) + (ev_ones2, bank_wait),
                    sig="t" if j == 3 else None)
            return P.op("v", (lambda e_, d=dst: e_.tensor_copy(d, pbA[:, :])),
                        waits=(last,), sig="v")

        ev_repr = replicate(repr_, rowsl[3], (ev_r, ev_m), ev_m)
        ev_repmr = replicate(repmr, rowsl[0], (ev_mr,), ev_repr)
        ev_repr2 = replicate(repr2, rowsl[2], (ev_r2,), ev_repmr)

        ev_xn1 = [None] * NCT
        ev_x1 = [None] * NCT
        for ct in range(NCT):
            sqb = af(O_LN_SQ + (ct % 2) * 8192, T)
            P.op("v", (lambda e_, c=ct, o=sqb: e_.scalar_tensor_tensor(
                o, X[c][:, :], 1.0, repr_, ALU.mult, ALU.mult)),
                waits=(ev_repr, ev_sumsq))
            ev_x1[ct] = P.op("v", (lambda e_, c=ct, a=sqb: e_.tensor_sub(X[c][:, :], a, repmr)),
                             waits=(ev_repmr,), sig="v")
            P.op("v", (lambda e_, c=ct, o=sqb: e_.tensor_mul(o, X[c][:, :], repr2)),
                 waits=(ev_repr2,))
            ev_xn1[ct] = P.op(
                "v", (lambda e_, c=ct, a=sqb: e_.tensor_scalar(
                    xn_t(c, 1), a, gam1[:, c:c + 1], None, ALU.mult)),
                sig="v")
        ev_x1dma = [None] * NCT
        for ct in range(NCT):
            ev_x1dma[ct] = P.dma("y", x1sc[ct * 128:(ct + 1) * 128, :], X[ct][:, :],
                                 waits=(ev_x1[ct],))
        ev_x1dma_all = ev_x1dma[NCT - 1]

        if upto == "ln1":
            dump_X("X", tuple(ev_x1))
            if "xn1" in dbg:
                for ct in range(NCT):
                    evc = P.op("v", (lambda e_, c=ct: e_.tensor_copy(
                        af(O_LN_SQ, T), xn_t(c, 1))), waits=(ev_xn1[ct],), sig="v")
                    P.dma("y", dbg["xn1"][ct * 128:(ct + 1) * 128, :], af(O_LN_SQ, T),
                          waits=(evc,))
            _finish(P, sems)
            P.realize(nc, sems)
            return nc

        lat_bf = ab(O_LAT, T)
        kr_b = ab(O_KR, T, 0, 64)
        kvtm_all = ab(O_KVTM, NT * 512)
        wkva_bf = ab(O_WKVA, NCT * 192)
        wkvb_bf = ab(O_WKVB, 512)
        wq_bf = ab(O_WQ, NCT * D)
        wo_bf = ab(O_WO, 4 * D)
        ccq = ab(O_CCQ, T)
        csq = ab(O_CSQ, T)
        smask = list(range(16, 32)) + list(range(0, 16))

        ev_lat_mm = None
        for tc in range(4):
            for cc in range(NCT):
                ev_lat_mm = P.op(
                    "t",
                    (lambda e_, c=cc, j=tc:
                     e_.matmul(bk(j), wkva_bf[:, c * 192: c * 192 + 128],
                               xn_t(c, 1)[:, j * 512:(j + 1) * 512],
                               start=(c == 0), stop=(c == NCT - 1))),
                    waits=(ev_xn1[cc], ev_wkva),
                    sig="t" if cc == NCT - 1 else None)
        ev_lat = P.op("v", (lambda e_: e_.tensor_copy(lat_bf[:, :], pbA[:, :])),
                      waits=(ev_lat_mm,), sig="v")
        ev_kr_mm = None
        for tc in range(4):
            for cc in range(NCT):
                ev_kr_mm = P.op(
                    "t",
                    (lambda e_, c=cc, j=tc:
                     e_.matmul(bk(4 + j)[0:64, :], wkva_bf[:, c * 192 + 128: c * 192 + 192],
                               xn_t(c, 1)[:, j * 512:(j + 1) * 512],
                               start=(c == 0), stop=(c == NCT - 1))),
                    waits=(ev_xn1[cc], ev_wkva),
                    sig="t" if cc == NCT - 1 else None)
        ev_kr_raw = P.op("v", (lambda e_: e_.tensor_copy(kr_b[32:64, :], pbB[32:64, :])),
                         waits=(ev_kr_mm,), sig="v")
        krs = ab(O_SPARE, T, 0, 64)
        P.op("v", (lambda e_: e_.stream_shuffle(krs[32:64, :], kr_b[32:64, :], smask)),
             waits=(ev_kr_raw, ev_tab))
        P.op("v", (lambda e_: e_.tensor_mul(kr_b[32:64, :], kr_b[32:64, :], ccq[32:64, :])))
        P.op("v", (lambda e_: e_.tensor_mul(krs[32:64, :], krs[32:64, :], csq[32:64, :])))
        ev_kr = P.op("v", (lambda e_: e_.tensor_add(kr_b[32:64, :], kr_b[32:64, :], krs[32:64, :])),
                     sig="v")

        ev_kvtm_mm = [None] * NT
        ev_kvtm = [None] * NT
        for kt in range(NT):
            b = kt % 4
            ev_kvtm_mm[kt] = P.op(
                "t",
                (lambda e_, k=kt, bb=b:
                 e_.matmul(bk(bb), lat_bf[:, k * 128:(k + 1) * 128], wkvb_bf[:, :],
                           start=True, stop=True)),
                waits=(ev_lat, ev_wkvb, ev_kvtm[kt - 4] if kt >= 4 else None),
                sig="t")
            ev_kvtm[kt] = P.op(
                "v", (lambda e_, k=kt, bb=b: e_.tensor_copy(kvtm_all[:, k * 512:(k + 1) * 512], bk(bb))),
                waits=(ev_kvtm_mm[kt],), sig="v")
        ev_kvfm = [None] * 4
        for g in range(4):
            ev_mm = None
            for tc in range(4):
                ev_mm = P.op(
                    "t",
                    (lambda e_, gg=g, j=tc:
                     e_.matmul(bk(4 + j), wkvb_bf[:, gg * 128:(gg + 1) * 128],
                               lat_bf[:, j * 512:(j + 1) * 512], start=True, stop=True)),
                    waits=(ev_lat, ev_kvfm[g - 1] if g >= 1 else ev_kr_raw),
                    sig="t" if tc == 3 else None)
            ev_kvfm[g] = P.op(
                "v", (lambda e_, gg=g: e_.tensor_copy(ab(O_KVFM + gg * 4096, T), pbB[:, :])),
                waits=(ev_mm,), sig="v")

        qsh = ab(O_SPARE, T)
        ev_qh2 = [None] * 8
        ev_qh2cp = [None] * 8
        ev_qproj_last = None
        for jp in range(8):
            ev_mm = None
            for tc in range(4):
                b = tc if jp % 2 == 0 else 4 + tc
                for cc in range(NCT):
                    ev_mm = P.op(
                        "t",
                        (lambda e_, c=cc, j=tc, p_=jp, bb=b:
                         e_.matmul(bk(bb), wq_bf[:, c * D + p_ * 128: c * D + p_ * 128 + 128],
                                   xn_t(c, 1)[:, j * 512:(j + 1) * 512],
                                   start=(c == 0), stop=(c == NCT - 1))),
                        waits=(ev_xn1[cc], ev_wq,
                               ev_qh2cp[jp - 2] if jp >= 2 else
                               (ev_kvfm[3] if jp % 2 == 1 else ev_kvtm[NT - 1])),
                        sig="t" if cc == NCT - 1 else None)
            psrc = pbA if jp % 2 == 0 else pbB
            ev_qh2cp[jp] = P.op(
                "v", (lambda e_, p_=jp, ps=psrc: e_.tensor_scalar(
                    ab(O_QH2 + p_ * 4096, T), ps[:, :], 0.125, None, ALU.mult)),
                waits=(ev_mm, ev_kr if jp == 0 else None), sig="v")
            ev_qproj_last = ev_mm
            P.op("v", (lambda e_, p_=jp: e_.stream_shuffle(qsh[:, :], ab(O_QH2 + p_ * 4096, T), smask)),
                 waits=(ev_tab,))
            P.op("v", (lambda e_, p_=jp: e_.tensor_mul(ab(O_QH2 + p_ * 4096, T),
                                                       ab(O_QH2 + p_ * 4096, T), ccq[:, :])))
            P.op("v", (lambda e_: e_.tensor_mul(qsh[:, :], qsh[:, :], csq[:, :])))
            ev_qh2[jp] = P.op(
                "v", (lambda e_, p_=jp: e_.tensor_add(ab(O_QH2 + p_ * 4096, T),
                                                      ab(O_QH2 + p_ * 4096, T), qsh[:, :])),
                sig="v")

        ev_ebr_mm = P.op(
            "t", (lambda e_: e_.matmul(bk(0)[:, 0:8], ones_r[0:1, :], ebias_sb[0:1, :],
                                       start=True, stop=True)),
            waits=(ev_eb, ev_ones2, ev_qh2cp[6]), sig="t")
        ev_ebr = P.op("v", (lambda e_: e_.tensor_copy(ebias_rep[:, :], bk(0)[:, 0:8])),
                      waits=(ev_ebr_mm, ev_qh2cp[7]), sig="v")

        if upto == "pre":
            dump_X("X", tuple(ev_x1))
            _finish(P, sems)
            P.realize(nc, sems)
            return nc

        ev_exp_prev = [ev_ebr, ev_ebr]
        ev_pair_done = [None] * 8
        ev_kp_used = [None, None]
        ev_kp_ready = [None, None]
        ev_kp_last_dma = None
        ev_ygrp_prev = {}
        ev_ytm_free = None
        ev_ynorm_prev = [None, None]
        ev_ptevac_prev = [None, None]
        ev_av_prev = [None, None]
        ev_rl_read_prev = [None, None]
        ev_wo_done = [None] * 4
        ev_wo_evac = {}

        def kp_view(par, p0=0, p1=128):
            return ab(O_KP + par * 4096, T, p0, p1)

        def assemble_kp(jp, extra=()):
            nonlocal ev_kp_last_dma
            par = jp % 2
            last = None
            for (hh, dstlo) in ((2 * jp, 0), (2 * jp + 1, 64)):
                g, a = hh // 4, hh % 4
                src = ab(O_KVFM + g * 4096, T, a * 32, a * 32 + 32)
                last = P.dma("y", kp_view(par, dstlo, dstlo + 32), src,
                             waits=(ev_kvfm[g], ev_kp_used[par]) + tuple(extra))
                last = P.dma("y", kp_view(par, dstlo + 32, dstlo + 64), kr_b[32:64, :],
                             waits=(ev_kr,))
            ev_kp_last_dma = last
            return last

        ev_kp_ready[0] = assemble_kp(0, extra=(ev_qproj_last,))
        ev_kp_ready[1] = assemble_kp(1, extra=(ev_qproj_last,))

        pend_pt = []
        pend_rest = []

        ptev_stash = [None]

        def flush_ptev():
            if ptev_stash[0] is not None:
                ptev_stash[0]()
                ptev_stash[0] = None

        def flush_all():
            nonlocal pend_pt, pend_rest
            for fnp in pend_pt:
                fnp()
            pend_pt = []
            flush_ptev()
            for fnp in pend_rest:
                fnp()
            pend_rest = []

        def PS(p):
            return pbA if p == 0 else pbB

        ev_ptdone_prev = [None, None]
        ev_ptevac_prev2 = [None, None]
        ev_max_prev = [None, None]

        for jp in range(_ATTN_PAIRS):
            par = jp % 2
            for h2 in range(2):
                h = 2 * jp + h2
                base = 64 * h2
                for qt in range(_ATTN_QT):
                    p = qt % 2
                    W = (qt + 1) * 128
                    nch = (W + 511) // 512
                    cs = p * 4
                    ps = PS(p)
                    while len(pend_rest) >= 2:
                        pend_rest.pop(0)()
                    ev_smm = None
                    for j in range(nch):
                        n = min(512, W - j * 512)
                        last_chunk = (j == nch - 1)
                        ev_smm = P.op(
                            "t",
                            (lambda e_, jj=j, nn=n, p_=jp, b_=base, q_=qt, pr=par, ps_=ps, lc=last_chunk:
                             e_.matmul(ps_[:, jj * 512: jj * 512 + nn],
                                       ab(O_QH2 + p_ * 4096, T, b_, b_ + 64)[:, q_ * 128:(q_ + 1) * 128],
                                       kp_view(pr, b_, b_ + 64)[:, jj * 512: jj * 512 + nn],
                                       start=True, stop=not lc)),
                            waits=(ev_qh2[jp], ev_kp_ready[par], ev_exp_prev[p],
                                   ev_ptevac_prev[p], ev_ptevac_prev2[p],
                                   ev_ynorm_prev[p] if nch >= 3 else None),
                            sig=None)
                    ev_smm = P.op(
                        "t",
                        (lambda e_, q_=qt, ps_=ps:
                         e_.matmul(ps_[:, q_ * 128:(q_ + 1) * 128],
                                   ident_bf[:, :], trimask_bf[:, :],
                                   start=False, stop=True)),
                        waits=(ev_tri_bf, ev_id_bf), sig="t")
                    ev_pair_done[jp] = ev_smm
                    ev_kp_used[par] = ev_smm
                    for fnp in pend_pt:
                        fnp()
                    pend_pt = []
                    ev_max = P.op(
                        "v",
                        (lambda e_, w_=W, c=cs, ps_=ps: e_.reduce_max(colw[:, c:c + 1], ps_[:, 0:w_],
                                                                      axis=AX, negate=True)),
                        waits=(ev_smm,), hwaits=(ev_rl_read_prev[p],), sig="v")
                    ev_max_prev[p] = ev_max
                    flush_ptev()
                    ev_exp = P.op(
                        "s",
                        (lambda e_, w_=W, c=cs, pr=p, ps_=ps:
                         e_.activation(ab(O_P + pr * 4096, T)[:, 0:w_], ps_[:, 0:w_], ACT.Exp,
                                       bias=colw[:, c:c + 1], scale=1.0,
                                       accum_out=colw[:, c + 2:c + 3])),
                        waits=(ev_max, ev_ptdone_prev[p]),
                        sig="s")
                    ev_rl = P.op(
                        "v",
                        (lambda e_, c=cs: e_.reciprocal(colw[:, c + 3:c + 4], colw[:, c + 2:c + 3])),
                        waits=(ev_exp,), sig="v")
                    ev_exp_prev[p] = ev_exp

                    def make_pt(qt_=qt, pr=qt % 2, ev_exp_=ev_exp):
                        def tail_pt():
                            nkt = qt_ + 1
                            ps_ = PS(pr)
                            psb = ps_[:, 0:1024].bitcast(BF16)
                            ev_tr = None
                            for kt in range(nkt):
                                ev_tr = P.op(
                                    "t",
                                    (lambda e_, k=kt, pb=psb, p2=pr:
                                     e_.transpose(pb[:, k * 128:(k + 1) * 128],
                                                  ab(O_P + p2 * 4096, T)[:, k * 128:(k + 1) * 128],
                                                  ident_bf[:, :])),
                                    waits=(ev_exp_, ev_ptevac_prev[pr],
                                           ev_ptevac_prev2[pr], ev_id_bf),
                                    sig="t" if kt == nkt - 1 else None)
                            ev_ptdone_prev[pr] = ev_tr
                            n_all = nkt * 128
                            if _PTEV and n_all > 1024:
                                def emit_ptev(nc_=n_all, pb=psb, p2=pr, et=ev_tr):
                                    ev_ptevac_prev2[p2] = P.op(
                                        "v",
                                        (lambda e_:
                                         e_.tensor_copy(
                                             ab(O_PT + p2 * 4096, T)[:, 1024:nc_],
                                             pb[:, 1024:nc_])),
                                        waits=(et,), sig="v")
                                ptev_stash[0] = emit_ptev
                                n_s = 1024
                            else:
                                n_s = n_all
                            ev_pte = P.op(
                                "s",
                                (lambda e_, nc_=n_s, pb=psb, p2=pr:
                                 e_.copy(ab(O_PT + p2 * 4096, T)[:, 0:nc_], pb[:, 0:nc_])),
                                waits=(ev_tr,), sig="s")
                            ev_ptevac_prev[pr] = ev_pte
                            return ev_pte
                        return tail_pt

                    def make_rest(h_=h, qt_=qt, cs_=cs, h2_=h2, pr=qt % 2, ev_rl_=ev_rl):
                        def tail_rest():
                            nkt = qt_ + 1
                            ps_ = PS(pr)
                            ev_pte = ev_ptevac_prev[pr]
                            ev_pte2 = ev_ptevac_prev2[pr]
                            ev_mx = ev_max_prev[pr] if qt_ >= 8 else None
                            ev_av = None
                            for kt in range(nkt):
                                ev_av = P.op(
                                    "t",
                                    (lambda e_, k=kt, hh=h_, p2=pr, q2=qt_, ps2=ps_:
                                     e_.matmul(ps2[:, 1024:1056],
                                               ab(O_PT + p2 * 4096, T)[:, k * 128:(k + 1) * 128],
                                               kvtm_all[:, k * 512 + 32 * hh: k * 512 + 32 * hh + 32],
                                               start=(k == 0), stop=(k == q2))),
                                    waits=(ev_pte, ev_pte2, ev_kvtm[kt],
                                           ev_ynorm_prev[pr], ev_mx),
                                    sig="t" if kt == nkt - 1 else None)
                            ev_av_prev[pr] = ev_av
                            ev_yn = P.op(
                                "v",
                                (lambda e_, c=cs_, q2=qt_, hh2=h2_, ps2=ps_: e_.tensor_scalar(
                                    ab(O_YTM, 1024)[:, q2 * 64 + hh2 * 32: q2 * 64 + hh2 * 32 + 32],
                                    ps2[:, 1024:1056],
                                    colw[:, c + 3:c + 4], None, ALU.mult)),
                                waits=(ev_av, ev_ytm_free), hwaits=(ev_rl_,), sig="v")
                            ev_ynorm_prev[pr] = ev_yn
                            ev_rl_read_prev[pr] = ev_yn
                        return tail_rest

                    pend_pt.append(make_pt())
                    pend_rest.append(make_rest())
            if jp + 2 <= 7:
                ev_kp_ready[par] = assemble_kp(jp + 2)
            flush_all()
            rbase = 64 * (jp % 2)
            ev_tre = {}
            for qg in range(4):
                last_tr = None
                for qq in range(4):
                    qt_ = qg * 4 + qq
                    last_tr = P.op(
                        "t",
                        (lambda e_, q_=qt_, b_=qg, rb=rbase:
                         e_.transpose(pbA[:, b_ * 512: b_ * 512 + 512]
                                      .bitcast(BF16)[rb:rb + 64, (q_ % 4) * 128:(q_ % 4 + 1) * 128],
                                      ab(O_YTM, 1024)[:, q_ * 64:(q_ + 1) * 64],
                                      ident_bf[:, :], tile_position=(0, rb))),
                        waits=(ev_ynorm_prev[0], ev_ynorm_prev[1],
                               ev_exp_prev[0], ev_ptevac_prev[0],
                               ev_ptevac_prev2[0], ev_ygrp_prev.get(qg)),
                        sig="t" if qq == 3 else None)
                ev_tre[qg] = last_tr
            ev_ytm_free = ev_tre[3]
            for qg in range(4):
                ev_ygrp_prev[qg] = P.op(
                    "v",
                    (lambda e_, b_=qg, rb=rbase:
                     e_.tensor_copy(ab(O_YGRP, T, rb, rb + 64)[:, b_ * 512:(b_ + 1) * 512],
                                    pbA[:, b_ * 512: b_ * 512 + 512].bitcast(BF16)[rb:rb + 64, 0:512])),
                    waits=(ev_tre[qg],), sig="v")
            ev_exp_prev[0] = ev_ygrp_prev[3]
            if jp % 2 == 1 and _K_WO:
                g = jp // 2
                last_mm = None
                for cc in range(NCT):
                    for tc in range(4):
                        b = tc
                        last_mm = P.op(
                            "t",
                            (lambda e_, c=cc, j=tc, gg=g, bb=b:
                             e_.matmul(bk(bb),
                                       wo_bf[:, gg * D + c * 128: gg * D + c * 128 + 128],
                                       ab(O_YGRP, T)[:, j * 512:(j + 1) * 512],
                                       start=True, stop=True)),
                            waits=(ev_ygrp_prev[0], ev_ygrp_prev[1],
                                   ev_ygrp_prev[2], ev_ygrp_prev[3], ev_wo,
                                   ev_exp_prev[0], ev_exp_prev[1],
                                   ev_wo_evac.get((cc - 1, tc))),
                            sig="t")
                        ev_wo_evac[(cc, tc)] = P.op(
                            "v",
                            (lambda e_, c=cc, j=tc, bb=b:
                             e_.scalar_tensor_tensor(X[c][:, j * 512:(j + 1) * 512],
                                                     bk(bb), 1.0,
                                                     X[c][:, j * 512:(j + 1) * 512],
                                                     ALU.mult, ALU.add)),
                            waits=(last_mm,) + ((tuple(ev_x1dma)) if g == 0 else ()),
                            sig="v")
                ev_wo_done[g] = last_mm
                ev_exp_prev[0] = ev_wo_evac[(NCT - 1, 3)]
                ev_exp_prev[1] = ev_wo_evac[(NCT - 1, 3)]
                ev_wo_evac = {}

        flush_all()
        ev_attn_done = ev_wo_done[3]
        ev_attn_vdone = ev_exp_prev[0]

        if upto == "attn":
            dump_X("X", (ev_attn_vdone,))
            _finish(P, sems)
            P.realize(nc, sems)
            return nc

        m2row = af(O_M2ROW, T, 0, 1)
        mrep = af(O_MREP, T)
        ev_sq2p = [None, None]
        ev_sumsq2 = None
        ev_sumsq2_ct = [None] * NCT
        for ct in range(NCT):
            sqb = af(O_M2SQ[ct % 2], T)
            ev_sq2p[ct % 2] = P.op(
                "s", (lambda e_, c=ct, o=sqb: e_.activation(o, X[c][:, :], ACT.Square)),
                waits=(ev_attn_vdone, ev_attn_done,
                       ev_sumsq2_ct[ct - 2] if ct >= 2 else None,
                       ev_ptevac_prev[0] if ct == 0 else None,
                       ev_ptevac_prev[1] if ct == 0 else None),
                sig="s")
            for j in range(4):
                ev_sumsq2 = P.op(
                    "t",
                    (lambda e_, o=sqb, jj=j, c=ct:
                     e_.matmul(bk(jj)[0:1, :], ones_c[:, 0:1],
                               o[:, jj * 512:(jj + 1) * 512],
                               start=(c == 0), stop=(c == NCT - 1))),
                    waits=(ev_sq2p[ct % 2],),
                    sig="t" if j == 3 else None)
            ev_sumsq2_ct[ct] = ev_sumsq2
        ev_q2 = P.op("v", (lambda e_: e_.tensor_scalar(m2row, pbA[0:1, :], 1.0 / D, None, ALU.mult)),
                     waits=(ev_sumsq2, ev_kp_last_dma), sig="v")
        ev_q2e = P.op("v", (lambda e_: e_.tensor_scalar(m2row, m2row, LN_EPS, None, ALU.add)),
                      waits=(ev_q2,), sig="v")
        ev_l2 = P.op("s", (lambda e_: e_.activation(m2row, m2row, ACT.Ln)),
                     waits=(ev_q2e,), sig="s")
        ev_r2b = P.op("s", (lambda e_: e_.activation(m2row, m2row, ACT.Exp, scale=-0.5)),
                      waits=(ev_l2,), sig="s")
        ev_mrep = replicate(mrep, m2row, (ev_r2b,), ev_q2)
        ev_xn2 = [None] * NCT
        for ct in range(NCT):
            sqb = af(O_M2SQ[ct % 2], T)
            P.op("v", (lambda e_, c=ct, o=sqb: e_.tensor_mul(o, X[c][:, :], mrep)),
                 waits=(ev_mrep, ev_sumsq2))
            ev_xn2[ct] = P.op(
                "v", (lambda e_, c=ct, a=sqb: e_.tensor_scalar(
                    xn_t(c, 2), a, gam2[:, c:c + 1], None, ALU.mult)),
                waits=(ev_gam2,), sig="v")

        if upto == "rms2":
            dump_X("X", tuple(ev_xn2))
            _finish(P, sems)
            P.realize(nc, sems)
            return nc

        wgate_bf = ab(O_WGATE, NCT * 8)
        comb_fm = af(O_COMB, T, 0, 8)
        LG = af(O_LG, 128)
        PR = af(O_PR, 128)
        MS = af(O_MS, 128)
        MS2 = af(O_MS2, 128)

        def a3(t_, d1=16, d2=8, bcast_from=None):
            h_ = t_.tensor if hasattr(t_, "tensor") else None
            return t_

        def v3(base_off):
            return bass.AP(arena, base_off // 4 * 0 + 0, [[0, 0]])

        def lg3(off):
            return bass.AP(arena, off // 4, [[AR_BYTES // 4, 128], [8, 16], [1, 8]]).bitcast_hack \
                if False else None

        arf = arena[:, :].bitcast(F32)

        def t3(off):
            return bass.AP(arf.tensor, off // 4, [[AR_BYTES // 4, 128], [8, 16], [1, 8]]) \
                if False else None

        def ap3(off, last_stride=1):
            base = af(off, 128)
            t_ = base.tensor
            return bass.AP(t_, base.offset, [[base.ap[0][0], 128], [8 if last_stride else 0, 16], [last_stride, 8]])

        def apb(off, col):
            base = af(off, 128)
            t_ = base.tensor
            return bass.AP(t_, base.offset + col, [[base.ap[0][0], 128], [1, 16], [0, 8]])

        ev_gate_prev = ev_xn2[NCT - 1]
        ev_gmm = None
        for tt in range(NT):
            for cc in range(NCT):
                ev_gmm = P.op(
                    "t",
                    (lambda e_, c=cc, t_=tt:
                     e_.matmul(bk(6)[:, 8 * t_: 8 * t_ + 8], xn_t(c, 2)[:, t_ * 128:(t_ + 1) * 128],
                               wgate_bf[:, c * 8:(c + 1) * 8],
                               start=(c == 0), stop=(c == NCT - 1))),
                    waits=(ev_xn2[cc], ev_wgate,
                           ev_dn_evac_guard if False else None,
                           ev_attn_done if (tt == 0 and cc == 0) else None),
                    sig="t" if (tt == NT - 1 and cc == NCT - 1) else None)
        eb3 = bass.AP(ebias_rep, 0, [[8, 128], [0, 16], [1, 8]])
        ev_lg = P.op(
            "v", (lambda e_: e_.tensor_tensor(ap3(O_LG), bk(6)[:, 0:128].rearrange(
                "p (a b) -> p a b", b=8), eb3, ALU.add)),
            waits=(ev_gmm, ev_ebr), sig="v")
        ev_m8 = P.op("v", (lambda e_: e_.reduce_max(MS[:, 0:16], ap3(O_LG), axis=AX, negate=True)),
                     hwaits=(ev_lg,), sig="v")
        ev_sub = P.op("v", (lambda e_: e_.tensor_tensor(ap3(O_LG), ap3(O_LG), apb(O_MS, 0), ALU.add)),
                      hwaits=(ev_m8,), sig="v")
        ev_e8 = P.op("s", (lambda e_: e_.activation(LG, LG, ACT.Exp)),
                     waits=(ev_sub,), sig="s")
        ev_l8 = P.op("v", (lambda e_: e_.reduce_sum(MS[:, 16:32], ap3(O_LG), axis=AX)),
                     waits=(ev_e8,), sig="v")
        ev_rs = P.op("v", (lambda e_: e_.reciprocal(MS[:, 32:48], MS[:, 16:32])),
                     hwaits=(ev_l8,), sig="v")
        ev_pr = P.op("v", (lambda e_: e_.tensor_tensor(ap3(O_PR), ap3(O_LG), apb(O_MS, 32), ALU.mult)),
                     hwaits=(ev_rs,), sig="v")
        ev_m1 = P.op("v", (lambda e_: e_.reduce_max(MS[:, 0:16], ap3(O_PR), axis=AX)),
                     hwaits=(ev_pr,), sig="v")
        ev_mk1 = P.op("v", (lambda e_: e_.tensor_tensor(ap3(O_LG), ap3(O_PR), apb(O_MS, 0), ALU.is_ge)),
                      hwaits=(ev_m1,), sig="v")
        ev_p2 = P.op("v", (lambda e_: e_.scalar_tensor_tensor(ap3(O_MS2), ap3(O_LG), -1e9,
                                                              ap3(O_PR), ALU.mult, ALU.add)),
                     hwaits=(ev_mk1,), sig="v")
        ev_m2 = P.op("v", (lambda e_: e_.reduce_max(MS[:, 48:64], ap3(O_MS2), axis=AX)),
                     hwaits=(ev_p2,), sig="v")
        ev_mk2 = P.op("v", (lambda e_: e_.tensor_tensor(ap3(O_MS2), ap3(O_MS2), apb(O_MS, 48), ALU.is_ge)),
                      hwaits=(ev_m2,), sig="v")
        ev_mks = P.op("v", (lambda e_: e_.tensor_tensor(ap3(O_LG), ap3(O_LG), ap3(O_MS2), ALU.add)),
                      hwaits=(ev_mk2,), sig="v")
        ev_ws = P.op("v", (lambda e_: e_.tensor_add(MS[:, 64:80], MS[:, 0:16], MS[:, 48:64])),
                     hwaits=(ev_m2,), sig="v")
        ev_rw = P.op("v", (lambda e_: e_.reciprocal(MS[:, 80:96], MS[:, 64:80])),
                     hwaits=(ev_ws,), sig="v")
        ev_cb1 = P.op("v", (lambda e_: e_.tensor_tensor(ap3(O_PR), ap3(O_PR), ap3(O_LG), ALU.mult)),
                      hwaits=(ev_mks,), sig="v")
        ev_comb = P.op("v", (lambda e_: e_.tensor_tensor(ap3(O_PR), ap3(O_PR), apb(O_MS, 80), ALU.mult)),
                       hwaits=(ev_rw, ev_cb1), sig="v")
        ev_combevac = None
        ev_ctr_prev = None
        for tt in range(NT):
            ev_ctr_prev = P.op(
                "t",
                (lambda e_, t_=tt: e_.transpose(pbB[0:8, 1536 + (t_ % 4) * 128: 1536 + (t_ % 4 + 1) * 128],
                                                PR[:, 8 * t_: 8 * t_ + 8], ident_f[:, :])),
                waits=(ev_comb, ev_combevac if tt % 4 == 0 and tt > 0 else None,
                       ev_attn_done if tt == 0 else None),
                sig="t")
            if tt % 4 == 3:
                ev_combevac = P.op(
                    "v",
                    (lambda e_, t_=tt: e_.tensor_copy(comb_fm[0:8, (t_ - 3) * 128:(t_ + 1) * 128],
                                                      pbB[0:8, 1536:2048])),
                    waits=(ev_ctr_prev,), sig="v")
        ev_comb_done = ev_combevac

        if upto == "gate":
            dump_X("X", (ev_comb_done,))
            if "comb" in dbg:
                P.dma("y", dbg["comb"][:, :], comb_fm[0:8, :], waits=(ev_comb_done,))
            _finish(P, sems)
            P.realize(nc, sems)
            return nc

        def wslot(i, n):
            return ab(MOE_WSLOT[i], n)

        slot_free = {0: ev_qh2[7], 1: ev_qh2[7], 2: ev_qh2[7],
                     3: ev_pair_done[5], 4: ev_pair_done[5], 5: ev_pair_done[5]}
        hfree = {i: ev_attn_done for i in range(9)}
        sgb = af(O_SG, 512)
        combrep = af(O_COMBREP, 512)
        csel_sb = af(O_CSEL, 1024, 0, 8)
        ev_csel = P.dma("y", csel_sb, d_csel, waits=(ev_attn_done,))
        ev_wg_dma = {}
        ev_wu_dma = {}
        ev_wd_dma = {}
        ev_hall = {}
        ev_prev_silu = None
        ev_down_read = None
        ev_hwr = {}
        ev_prev_hwr = None
        ev_sil_done = {}
        ev_umul_done = {}
        ev_dn_evac = {}
        ev_dn_free = {}
        ev_dn_last = {}
        ev_cre_prev = None
        ev_dn_done = None

        for tc in range(_K_MOE_TC):
            for e9 in range(_K_MOE_E):
                k = tc * 9 + e9
                sgu = (2 * k) % 4
                suu = sgu + 1
                ev_wg_dma[k] = P.dma("g", wslot(sgu, NCT * 512), d_wgu[e9],
                                     waits=(slot_free[sgu],))
                ev_wu_dma[k] = P.dma("g", wslot(suu, NCT * 512), d_wuu[e9],
                                     waits=(slot_free[suu],))
                if e9 < 8:
                    ev_crm = P.op(
                        "t",
                        (lambda e_, ee=e9, j=tc:
                         e_.matmul(bk(3), csel_sb[:, ee * 128:(ee + 1) * 128],
                                   comb_fm[0:8, j * 512:(j + 1) * 512],
                                   start=True, stop=True)),
                        waits=(ev_comb_done, ev_cre_prev, ev_csel,
                               ev_dn_free.get((tc, 3))), sig="t")
                    ev_cre = P.op(
                        "v", (lambda e_: e_.tensor_copy(combrep[:, :], bk(3))),
                        waits=(ev_crm,), sig="v")
                    ev_cre_prev = ev_cre
                else:
                    ev_cre = None
                ev_gmm = None
                ev_umm = None
                for ic in range(4):
                    bg = 4 + 2 * (ic % 2)
                    bu = bg + 1
                    for cc in range(NCT):
                        ev_gmm = P.op(
                            "t",
                            (lambda e_, c=cc, i_=ic, j=tc, s_=sgu, b_=bg:
                             e_.matmul(bk(b_),
                                       wslot(s_, NCT * 512)[:, c * 512 + i_ * 128: c * 512 + i_ * 128 + 128],
                                       xn_t(c, 2)[:, j * 512:(j + 1) * 512],
                                       start=(c == 0), stop=(c == NCT - 1))),
                            waits=(ev_wg_dma[k], ev_xn2[cc],
                                   ev_sil_done.get((k, ic - 2)) or ev_sil_done.get((k - 1, ic + 2)),
                                   ev_dn_free.get((tc, bg)) if e9 == 0 else None,
                                   ev_hwr.get((k - 1, ic))),
                            sig="t" if cc == NCT - 1 else None)
                    for cc in range(NCT):
                        ev_umm = P.op(
                            "t",
                            (lambda e_, c=cc, i_=ic, j=tc, s_=suu, b_=bu:
                             e_.matmul(bk(b_),
                                       wslot(s_, NCT * 512)[:, c * 512 + i_ * 128: c * 512 + i_ * 128 + 128],
                                       xn_t(c, 2)[:, j * 512:(j + 1) * 512],
                                       start=(c == 0), stop=(c == NCT - 1))),
                            waits=(ev_wu_dma[k],
                                   ev_umul_done.get((k, ic - 2)) or ev_umul_done.get((k - 1, ic + 2)),
                                   ev_dn_free.get((tc, bu)) if e9 == 0 else None),
                            sig="t" if cc == NCT - 1 else None)
                    ev_sl = P.op(
                        "s", (lambda e_, b_=bg: e_.activation(sgb, bk(b_), ACT.Silu)),
                        waits=(ev_gmm, ev_prev_silu, ev_prev_hwr), sig="s")
                    ev_prev_silu = ev_sl
                    ev_sil_done[(k, ic)] = ev_sl
                    ev_umul_done[(k, ic)] = P.op(
                        "v", (lambda e_, b_=bu: e_.tensor_mul(sgb, bk(b_), sgb)),
                        waits=(ev_sl, ev_umm), sig="v")
                    hdst = ab(MOE_H[e9], T)
                    if e9 < 8:
                        ev_h = P.op(
                            "v",
                            (lambda e_, i_=ic, hd=hdst:
                             e_.tensor_mul(hd[:, i_ * 512:(i_ + 1) * 512], sgb, combrep[:, :])),
                            waits=(ev_cre,
                                   hfree[e9] if ic == 0 else None,
                                   ev_down_read if ic == 0 else None),
                            sig="v")
                    else:
                        ev_h = P.op(
                            "v",
                            (lambda e_, i_=ic, hd=hdst:
                             e_.tensor_copy(hd[:, i_ * 512:(i_ + 1) * 512], sgb)),
                            waits=(hfree[e9] if ic == 0 else None,
                                   ev_down_read if ic == 0 else None),
                            sig="v")
                    ev_hwr[(k, ic)] = ev_h
                    ev_prev_hwr = ev_h
                slot_free[sgu] = ev_gmm
                slot_free[suu] = ev_umm
                ev_hall[(tc, e9)] = ev_hwr[(k, 3)]
            for e9 in range(2):
                k = tc * 9 + e9
                sdd = 4 + e9 % 2
                ev_wd_dma[k] = P.dma("g", wslot(sdd, 4 * D), d_wdd[e9],
                                     waits=(slot_free[sdd],))
            ev_dn_last = {}
            for e9 in range(_K_MOE_E):
                k = tc * 9 + e9
                sdd = 4 + e9 % 2
                ev_mm_e = None
                for cc in range(NCT):
                    for ic in range(4):
                        ev_mm_e = P.op(
                            "t",
                            (lambda e_, c=cc, i_=ic, ee=e9, s_=sdd:
                             e_.matmul(bk(c),
                                       wslot(s_, 4 * D)[:, i_ * D + c * 128: i_ * D + c * 128 + 128],
                                       ab(MOE_H[ee], T)[:, i_ * 512:(i_ + 1) * 512],
                                       start=(ee == 0 and i_ == 0),
                                       stop=(ee == _K_MOE_E - 1 and i_ == 3))),
                            waits=(ev_wd_dma[k], ev_hall[(tc, e9)],
                                   ev_dn_free.get((tc, cc)),
                                   ev_cre_prev if (cc == 3 and e9 == 0) else None,
                                   ev_prev_silu if (cc >= 4 and e9 == 0) else None,
                                   ev_prev_hwr if (cc >= 4 and e9 == 0) else None),
                            sig="t" if ic == 3 else None)
                    ev_dn_last[cc] = ev_mm_e
                slot_free[sdd] = ev_mm_e
                if e9 + 2 < _K_MOE_E:
                    k2 = tc * 9 + e9 + 2
                    ev_wd_dma[k2] = P.dma("g", wslot(sdd, 4 * D), d_wdd[e9 + 2],
                                          waits=(ev_mm_e,))
            ev_down_read = ev_mm_e
            for i in range(9):
                hfree[i] = None
            for cc in range(NCT):
                ev_e = P.op(
                    "v",
                    (lambda e_, c=cc, j=tc:
                     e_.scalar_tensor_tensor(X[c][:, j * 512:(j + 1) * 512], bk(c), 1.0,
                                             X[c][:, j * 512:(j + 1) * 512], ALU.mult, ALU.add)),
                    waits=(ev_dn_last[cc],), sig="v")
                ev_dn_free[(tc + 1, cc)] = ev_e
                ev_dn_evac[cc % 3] = ev_e
            ev_dn_done = ev_dn_evac[(NCT - 1) % 3]

        ev_moe_done = ev_dn_done

        if upto == "moe":
            dump_X("X", (ev_moe_done, ev_dn_evac.get(0), ev_dn_evac.get(1), ev_dn_evac.get(2)))
            _finish(P, sems)
            P.realize(nc, sems)
            return nc

        ev_dn_all = (ev_dn_evac.get(0), ev_dn_evac.get(1), ev_dn_evac.get(2))
        ev_x1b_free = [None, None]
        ev_xadd = [None] * NCT
        for ct in range(NCT):
            sl2 = ct % 2
            x1b = af(O_X1B + sl2 * 8192, T)
            ev_ld = P.dma("y", x1b, x1sc[ct * 128:(ct + 1) * 128, :],
                          waits=(ev_x1b_free[sl2], ev_moe_done,
                                 ev_dn_evac.get(0), ev_dn_evac.get(1), ev_dn_evac.get(2)))
            ev_xadd[ct] = P.op(
                "v", (lambda e_, c=ct, b_=x1b: e_.tensor_add(X[c][:, :], X[c][:, :], b_)),
                waits=(ev_ld,), sig="v")
            ev_x1b_free[sl2] = ev_xadd[ct]
        ev_tm_free = [None, None]
        ev_tr_prev = {0: None, 1: None}
        ev_ln2_first = tuple(x for x in ev_dn_all if x is not None)
        for tt in range(NT):
            sl = tt % 2
            tmb = af(O_TM + sl * 4096, D)
            sqb = af(O_SQ2 + sl * 4096, D)
            ev_tr = {}
            for cc in range(NCT):
                b = cc // 4
                ev_tr[b] = P.op(
                    "t",
                    (lambda e_, c=cc, t_=tt, b_=b:
                     e_.transpose(bk(b_)[:, (c % 4) * 128:(c % 4 + 1) * 128],
                                  X[c][:, t_ * 128:(t_ + 1) * 128], ident_f[:, :])),
                    waits=ev_ln2_first + (ev_xadd[cc],) + ((ev_tr_prev[b],) if cc % 4 == 0 else ()),
                    sig="t" if cc % 4 == 3 else None)
            ev_ln2_first = ()
            ev_cpa = P.op(
                "v", (lambda e_, o=tmb: e_.tensor_copy(o[:, 0:512], bk(0))),
                waits=(ev_tr[0], ev_tm_free[sl]), sig="v")
            ev_cpb = P.op(
                "v", (lambda e_, o=tmb: e_.tensor_copy(o[:, 512:1024], bk(1))),
                waits=(ev_tr[1],), sig="v")
            ev_tr_prev[0] = ev_cpa
            ev_tr_prev[1] = ev_cpb
            ev_sqq = P.op(
                "s", (lambda e_, a=tmb, o=sqb: e_.activation(o, a, ACT.Square)),
                waits=(ev_cpa, ev_cpb, ev_tm_free[sl]), sig="s")
            cb = 8 + 6 * sl
            ev_s1 = P.op("v", (lambda e_, a=tmb, c=cb: e_.reduce_sum(colw[:, c:c + 1], a, axis=AX)),
                         waits=(ev_cpb,), sig="v")
            ev_s2 = P.op("v", (lambda e_, o=sqb, c=cb: e_.reduce_sum(colw[:, c + 1:c + 2], o, axis=AX)),
                         waits=(ev_sqq,), sig="v")
            ev_c1 = P.op("v", (lambda e_, c=cb: e_.tensor_scalar(colw[:, c + 2:c + 3], colw[:, c:c + 1],
                                                                 1.0 / D, None, ALU.mult)),
                         hwaits=(ev_s1,), sig="v")
            ev_c2 = P.op("v", (lambda e_, c=cb: e_.tensor_scalar(colw[:, c + 3:c + 4], colw[:, c + 1:c + 2],
                                                                 1.0 / D, None, ALU.mult)),
                         hwaits=(ev_s2,), sig="v")
            ev_c3 = P.op("v", (lambda e_, c=cb: e_.tensor_mul(colw[:, c + 4:c + 5], colw[:, c + 2:c + 3],
                                                              colw[:, c + 2:c + 3])),
                         hwaits=(ev_c1,), sig="v")
            ev_c4 = P.op("v", (lambda e_, c=cb: e_.tensor_sub(colw[:, c + 3:c + 4], colw[:, c + 3:c + 4],
                                                              colw[:, c + 4:c + 5])),
                         hwaits=(ev_c2, ev_c3), sig="v")
            ev_c5 = P.op("v", (lambda e_, c=cb: e_.tensor_scalar(colw[:, c + 3:c + 4], colw[:, c + 3:c + 4],
                                                                 LN_EPS, None, ALU.add)),
                         hwaits=(ev_c4,), sig="v")
            ev_lnx = P.op("s", (lambda e_, c=cb: e_.activation(colw[:, c + 4:c + 5], colw[:, c + 3:c + 4],
                                                               ACT.Ln)),
                          waits=(ev_c5,), sig="s")
            ev_rx = P.op("s", (lambda e_, c=cb: e_.activation(colw[:, c + 4:c + 5], colw[:, c + 4:c + 5],
                                                              ACT.Exp, scale=-0.5)),
                         waits=(ev_lnx,), sig="s")
            ev_app = P.op(
                "v",
                (lambda e_, a=tmb, o=sqb, c=cb: e_.tensor_scalar(o, a, colw[:, c + 2:c + 3],
                                                                 colw[:, c + 4:c + 5],
                                                                 ALU.subtract, ALU.mult)),
                waits=(ev_rx,), hwaits=(ev_c1,), sig="v")
            ev_out = P.dma("y", out[tt * 128:(tt + 1) * 128, :], sqb, waits=(ev_app,))
            ev_tm_free[sl] = ev_out

        _finish(P, sems)
        P.realize(nc, sems)

    return nc


def _finish(P, sems):
    for sem in QSEMS:
        val = P.cnt[sem]
        if val:
            P.op("y", (lambda e_, s_=sem, v_=val: e_.wait_ge(sems[s_], v_)))
            P.op("t", (lambda e_, s_=sem, v_=val: e_.wait_ge(sems[s_], v_)))



_DEBUG_DUMP = ()


def kernel(**inputs):
    consts = _host_consts()
    weights = _host_weights(inputs)
    x_full = np.ascontiguousarray(np.asarray(inputs["L0_attn_x_t"], dtype=np.float32))

    nc = build_program(upto=_BUILD_UPTO, debug_dump=_DEBUG_DUMP)

    base = {}
    base.update({k: np.ascontiguousarray(v) for k, v in consts.items()})
    base.update({k: np.ascontiguousarray(v) for k, v in weights.items()})
    in_maps = []
    for core in range(8):
        m = dict(base)
        m["xin"] = np.ascontiguousarray(x_full[core])
        in_maps.append(m)

    res = run_bass_kernel_spmd(nc, in_maps, core_ids=list(range(8)))
    if _DEBUG_DUMP:
        kernel._last_results = res.results
    outp = np.stack([res.results[i]["out"] for i in range(8)], axis=0)
    return outp


# version 18 (final)
# speedup vs baseline: 1.1809x; 1.0003x over previous
import numpy as np
import ml_dtypes
from contextlib import ExitStack

import concourse.bass as bass
import concourse.mybir as mybir
from concourse.bass_utils import run_bass_kernel_spmd

F32 = mybir.dt.float32
BF16 = mybir.dt.bfloat16
AX = mybir.AxisListType.X
ALU = mybir.AluOpType
ACT = mybir.ActivationFunctionType
BFNP = ml_dtypes.bfloat16

T = 2048
D = 1024
NT = T // 128
NCT = D // 128
THETA = 1e5
LN_EPS = 1e-5
GEPS = 0.5 + 1e-8

_BUILD_UPTO = "full"
import os as _os
_ATTN_PAIRS = int(_os.environ.get("K_ATTN_PAIRS", "8"))
_ATTN_QT = int(_os.environ.get("K_ATTN_QT", "16"))
_K_WO = int(_os.environ.get("K_WO", "1"))
_K_MOE_TC = int(_os.environ.get("K_MOE_TC", "4"))
_K_MOE_E = int(_os.environ.get("K_MOE_E", "9"))
_PTEV = int(_os.environ.get("K_PTEV", "1"))
_STAGES = ("mingru", "ln1", "pre", "attn", "rms2", "gate", "moe", "full")


def _stage_ge(a, b):
    return _STAGES.index(a) >= _STAGES.index(b)



def _host_consts():
    c = {}
    c["cident_bf"] = np.eye(128, dtype=BFNP)
    c["cident_f"] = np.eye(128, dtype=np.float32)
    tri = np.where(np.arange(128)[None, :] <= np.arange(128)[:, None], 0.0, -1e9)
    c["ctrimask"] = tri.astype(np.float32)
    c["ctrimask_bf"] = tri.astype(BFNP)
    csel = np.zeros((8, 8 * 128), np.float32)
    for e in range(8):
        csel[e, e * 128:(e + 1) * 128] = 1.0
    c["csel"] = csel
    t = np.arange(T, dtype=np.float64)
    freqs = THETA ** (-np.arange(16, dtype=np.float64) / 16.0)
    ang = np.outer(freqs, t)
    cosv, sinv = np.cos(ang), np.sin(ang)
    C = np.zeros((128, T), np.float64)
    S = np.zeros((128, T), np.float64)
    for base in (32, 96):
        C[base:base + 16] = cosv
        C[base + 16:base + 32] = cosv
        S[base:base + 16] = -sinv
        S[base + 16:base + 32] = sinv
    C[0:32] = 1.0
    C[64:96] = 1.0
    c["ccq"] = C.astype(BFNP)
    c["csq"] = S.astype(BFNP)
    return c


def _q_col_perm():
    perm = np.empty(D, np.int64)
    for j in range(D):
        jp, r = divmod(j, 128)
        h = 2 * jp + r // 64
        rr = r % 64
        if rr < 32:
            o = 64 * h + rr
        elif rr < 48:
            o = 64 * h + 32 + 2 * (rr - 32)
        else:
            o = 64 * h + 32 + 2 * (rr - 48) + 1
        perm[j] = o
    return perm


def _host_weights(ins):
    w = {}
    for nm, key in (("wbd_f", "L0_attn_Wf"), ("wbd_i", "L0_attn_Wi"),
                    ("wbd_h", "L0_attn_Wh")):
        W = np.asarray(ins[key], dtype=np.float32)
        outw = np.zeros((128, NCT * 128), np.float32)
        for ct in range(NCT):
            for gl in range(16):
                g = 16 * ct + gl
                outw[8 * gl:8 * gl + 8, ct * 128 + 8 * gl: ct * 128 + 8 * gl + 8] = W[g].T
        w[nm] = outw

    Wq = np.asarray(ins["L0_ffn_Wq"], dtype=np.float32)[:, _q_col_perm()]
    w["wq"] = Wq.reshape(NCT, 128, D).transpose(1, 0, 2).reshape(128, NCT * D)

    Wkva = np.asarray(ins["L0_ffn_Wkva"], dtype=np.float32)
    kva = np.zeros((D, 192), np.float32)
    kva[:, 0:128] = Wkva[:, 0:128]
    kva[:, 160:176] = Wkva[:, 128:160][:, 0::2]
    kva[:, 176:192] = Wkva[:, 128:160][:, 1::2]
    w["wkva"] = kva.reshape(NCT, 128, 192).transpose(1, 0, 2).reshape(128, NCT * 192)

    w["wkvb"] = np.asarray(ins["L0_ffn_Wkvb"], dtype=np.float32)

    Wo = np.asarray(ins["L0_ffn_Wo"], dtype=np.float32)
    rows = np.empty(512, np.int64)
    for g in range(4):
        for a in range(4):
            h = 4 * g + a
            rows[g * 128 + a * 32: g * 128 + a * 32 + 32] = 64 * h + np.arange(32)
    w["wo"] = Wo[rows].reshape(4, 128, D).transpose(1, 0, 2).reshape(128, 4 * D)

    w["wgate"] = np.asarray(ins["L0_ffn_Wgate"], dtype=np.float32) \
        .reshape(NCT, 128, 8).transpose(1, 0, 2).reshape(128, NCT * 8)

    gg = np.concatenate([np.asarray(ins["L0_ffn_Wg"], dtype=np.float32),
                         np.asarray(ins["L0_ffn_sWg"], dtype=np.float32)], 0)
    uu = np.concatenate([np.asarray(ins["L0_ffn_Wu"], dtype=np.float32),
                         np.asarray(ins["L0_ffn_sWu"], dtype=np.float32)], 0)
    dd = np.concatenate([np.asarray(ins["L0_ffn_Wd"], dtype=np.float32),
                         np.asarray(ins["L0_ffn_sWd"], dtype=np.float32)], 0)
    w["wgu"] = gg.reshape(9, NCT, 128, 512).transpose(0, 2, 1, 3).reshape(9, 128, NCT * 512)
    w["wuu"] = uu.reshape(9, NCT, 128, 512).transpose(0, 2, 1, 3).reshape(9, 128, NCT * 512)
    w["wdd"] = dd.reshape(9, 4, 128, D).transpose(0, 2, 1, 3).reshape(9, 128, 4 * D)

    w["wln1"] = np.asarray(ins["L0_ffn_w_ln1"], dtype=np.float32)
    w["wln2"] = np.asarray(ins["L0_ffn_w_ln2"], dtype=np.float32)
    w["ebias"] = np.asarray(ins["L0_ffn_expert_bias"], dtype=np.float32)
    return w



ENGS = ("t", "v", "s", "g", "y")
QLANES = {"y": 4, "g": 4, "s": 2}
QSEMS = [f"q{q}{i}" for q, n in QLANES.items() for i in range(n)]


class Prog:
    def __init__(self):
        self.stream = {e: [] for e in ENGS}
        self.cnt = {k: 0 for k in ("t", "v", "s", "g", *QSEMS)}
        self.lane_rr = {q: 0 for q in QLANES}

    def op(self, eng, fn, waits=(), sig=None, hwaits=()):
        w = tuple(ev for ev in waits if ev is not None)
        hw = tuple(ev for ev in hwaits if ev is not None)
        for ev_ in w + hw:
            assert isinstance(ev_, tuple) and len(ev_) == 2 and isinstance(ev_[1], int), \
                f"bad wait event: {ev_!r}"
        ev = None
        if sig is not None:
            self.cnt[sig] += 16 if sig.startswith("q") else 1
            ev = (sig, self.cnt[sig])
        self.stream[eng].append((w, fn, sig, hw))
        return ev

    def dma(self, queue, dst, src, waits=(), slow=False):
        lane = self.lane_rr[queue]
        self.lane_rr[queue] = (lane + 1) % QLANES[queue]
        sem = f"q{queue}{lane}"
        prev = self.cnt[sem]
        w = tuple(waits) + (((sem, prev),) if prev else ())
        return self.op(queue,
                       (lambda e, d=dst, s_=src, sl=slow:
                        e.dma_start(d, s_, allow_slow_non_contiguous=sl)),
                       waits=w, sig=sem)

    def realize(self, nc, sems):
        waited = {e: {} for e in ENGS}

        def runner(ename):
            def run(eng):
                wd = waited[ename]
                for (w, fn, sig, hw) in self.stream[ename]:
                    for (snm, val) in w:
                        if snm == ename and not (ename == "s" and snm == "s"):
                            continue
                        if wd.get(snm, -1) >= val:
                            continue
                        eng.wait_ge(sems[snm], val)
                        wd[snm] = val
                    for (snm, val) in hw:
                        if wd.get(snm, -1) >= val:
                            continue
                        eng.wait_ge(sems[snm], val)
                        wd[snm] = val
                    ins = fn(eng)
                    if sig is not None:
                        ins.then_inc(sems[sig], 16 if sig.startswith("q") else 1)
            return run

        with nc.Block() as block:
            block.tensor(runner("t"))
            block.vector(runner("v"))
            block.scalar(runner("s"))
            block.gpsimd(runner("g"))
            block.sync(runner("y"))


AR_BYTES = 144832

O_WGATE = 0
O_WQ = 128
O_WKVA = 16512
O_WKVB = 19584
O_CCQ = 20608
O_CSQ = 24704
O_WO = 28800

O_WBD = 36992
O_XBF = 43136
O_MGBUF = 51328
O_XSTAGE = 116864

O_XN = 36992
O_LN_ROWS = 69760
O_LN_REPS = 102528
O_LN_SQ = 127104

O_KP = 36992
O_P = 45184
O_PT = 53376
O_YGRP = 61568
O_SPARE = 65664
O_QH2 = 69760
O_KVFM = 102528
O_KVTM = 118912
O_LAT = 135296
O_KR = 139392

MOE_WSLOT = [128, 8320, 16512, 69760, 77952, 86144]
MOE_H = [36992, 41088, 45184, 49280, 53376, 57472, 61568, 94336, 98432]
XN2_OFF = [24704, 28800, 32896, 102528, 106624, 110720, 114816, 118912]
O_M2SQ = [45184, 53376]
O_M2ROW = 61568
O_MREP = 36992
O_YTM = 65664
O_COMB = 123008
O_LG = 65664
O_PR = 66176
O_MS = 66688
O_MS2 = 67200
O_COMBREP = 67712
O_SG = 135296
O_CSEL = 131200

O_TM = 128
O_SQ2 = 8320
O_X1B = 16512


def build_program(upto="full", debug_dump=()):
    nc = bass.Bass("TRN2", target_bir_lowering=False, detect_race_conditions=False)

    def din(name, shape, dt=F32):
        return nc.dram_tensor(name, list(shape), dt, kind="ExternalInput").ap()

    xin = din("xin", (T, D))
    d_cident_bf = din("cident_bf", (128, 128), BF16)
    d_cident_f = din("cident_f", (128, 128))
    d_ctrimask = din("ctrimask", (128, 128))
    d_ctrimask_bf = din("ctrimask_bf", (128, 128), BF16)
    d_csel = din("csel", (8, 8 * 128))
    d_ccq = din("ccq", (128, T), BF16)
    d_csq = din("csq", (128, T), BF16)
    d_wbd = {k: din("wbd_" + k, (128, NCT * 128)) for k in "fih"}
    d_wq = din("wq", (128, NCT * D))
    d_wkva = din("wkva", (128, NCT * 192))
    d_wkvb = din("wkvb", (128, 512))
    d_wo = din("wo", (128, 4 * D))
    d_wgate = din("wgate", (128, NCT * 8))
    d_wgu = din("wgu", (9, 128, NCT * 512))
    d_wuu = din("wuu", (9, 128, NCT * 512))
    d_wdd = din("wdd", (9, 128, 4 * D))
    d_wln1 = din("wln1", (D,))
    d_wln2 = din("wln2", (D,))
    d_ebias = din("ebias", (8,))

    out = nc.dram_tensor("out", [T, D], F32, kind="ExternalOutput").ap()
    x1sc = nc.dram_tensor("x1sc", [D, T], F32, kind="Internal").ap()
    dbg = {}
    for nm in debug_dump:
        shape = [8, T] if nm == "comb" else [D, T]
        dbg[nm] = nc.dram_tensor("dbg_" + nm, list(shape), F32, kind="ExternalOutput").ap()

    P = Prog()

    with ExitStack() as ctx:
        e = ctx.enter_context

        X = [e(nc.sbuf_tensor(f"X{i}", [128, T], F32)) for i in range(NCT)]
        ident_bf = e(nc.sbuf_tensor("ident_bf", [128, 128], BF16))
        ident_f = e(nc.sbuf_tensor("ident_f", [128, 128], F32))
        trimask = e(nc.sbuf_tensor("trimask", [128, 128], F32))
        trimask_bf = e(nc.sbuf_tensor("trimask_bf", [128, 128], BF16))
        gam1 = e(nc.sbuf_tensor("gam1", [128, NCT], F32))
        gam2 = e(nc.sbuf_tensor("gam2", [128, NCT], F32))
        ebias_sb = e(nc.sbuf_tensor("ebias_sb", [1, 8], F32))
        ebias_rep = e(nc.sbuf_tensor("ebias_rep", [128, 8], F32))
        colw = e(nc.sbuf_tensor("colw", [128, 24], F32))
        ones_c = e(nc.sbuf_tensor("ones_c", [128, 1], F32))
        ones_r = e(nc.sbuf_tensor("ones_r", [1, 128], F32))
        ysc = e(nc.sbuf_tensor("ysc", [128, 64], BF16))
        arena = e(nc.sbuf_tensor("arena", [128, AR_BYTES // 2], BF16))

        pbA = e(nc.psum_tensor("pbA", [128, 2048], F32))
        pbB = e(nc.psum_tensor("pbB", [128, 2048], F32))

        def bk(i):
            t_ = pbA if i < 4 else pbB
            j = i % 4
            return t_[:, j * 512:(j + 1) * 512]

        def bkb(i):
            t_ = pbA if i < 4 else pbB
            j = i % 4
            return t_[:, j * 512:(j + 1) * 512].bitcast(BF16)

        sems = {k: e(nc.semaphore("sem_" + k))
                for k in ("t", "v", "s", "g", *QSEMS)}

        def ab(off, n, p0=0, p1=128):
            return arena[p0:p1, off // 2: off // 2 + n]

        def af(off, n, p0=0, p1=128):
            assert off % 4 == 0
            return arena[p0:p1, off // 2: off // 2 + 2 * n].bitcast(F32)

        ev_id_bf = P.dma("y", ident_bf[:, :], d_cident_bf)
        ev_id_f = P.dma("y", ident_f[:, :], d_cident_f)
        P.dma("y", trimask[:, :], d_ctrimask)
        ev_tri_bf = P.dma("y", trimask_bf[:, :], d_ctrimask_bf)
        P.dma("y", ab(O_CCQ, T), d_ccq)
        ev_tab = P.dma("y", ab(O_CSQ, T), d_csq)
        P.dma("y", gam1[:, :], d_wln1.rearrange("(a b) -> b a", b=128), slow=True)
        ev_gam2 = P.dma("y", gam2[:, :], d_wln2.rearrange("(a b) -> b a", b=128), slow=True)
        ev_eb = P.dma("y", ebias_sb[0:1, :], d_ebias.rearrange("(o a) -> o a", o=1))
        ev_ones = P.op("g", lambda e_: e_.memset(ones_c[:, :], 1.0), sig="g")
        ev_ones2 = P.op("g", lambda e_: e_.memset(ones_r[:, :], 1.0), sig="g")

        ev_wbd = {}
        for i, k in enumerate("fih"):
            ev_wbd[k] = P.dma("g", ab(O_WBD + i * 2048, NCT * 128), d_wbd[k])
        ev_wq = P.dma("g", ab(O_WQ, NCT * D), d_wq)
        ev_wkva = P.dma("g", ab(O_WKVA, NCT * 192), d_wkva)
        ev_wkvb = P.dma("g", ab(O_WKVB, 512), d_wkvb)
        ev_wo = P.dma("g", ab(O_WO, 4 * D), d_wo)
        ev_wgate = P.dma("g", ab(O_WGATE, NCT * 8), d_wgate)

        ev_xready = [None] * NCT
        ev_stage_free = [None] * 4
        bankfree = {j: None for j in range(8)}
        for grp in range(4):
            ev_st = []
            for j in range(4):
                tt = grp * 4 + j
                st = af(O_XSTAGE + j * 4096, D)
                ev_st.append(P.dma("y", st, xin[tt * 128:(tt + 1) * 128, :],
                                   waits=(ev_stage_free[j],)))
            for ct in range(NCT):
                bank = ct % 4
                evt = None
                for j in range(4):
                    st = af(O_XSTAGE + j * 4096, D)
                    evt = P.op(
                        "t",
                        (lambda e_, b=bank, jj=j, s_=st, c=ct:
                         e_.transpose(bk(b)[:, jj * 128:(jj + 1) * 128],
                                      s_[:, c * 128:(c + 1) * 128], ident_f[:, :])),
                        waits=(ev_st[j], bankfree[bank], ev_id_f),
                        sig="t" if j == 3 else None)
                ev_ev = P.op(
                    "v",
                    (lambda e_, b=bank, c=ct, g=grp:
                     e_.tensor_copy(X[c][:, g * 512:(g + 1) * 512], bk(b))),
                    waits=(evt,), sig="v")
                bankfree[bank] = ev_ev
                if ct == NCT - 1:
                    for j in range(4):
                        ev_stage_free[j] = ev_ev
                if grp == 3:
                    ev_xready[ct] = ev_ev

        def mgb(seti, k):
            return af(O_MGBUF + (seti * 4 + k) * 8192, T)

        def xbf_t(seti):
            return ab(O_XBF + seti * 4096, T)

        wbd_bf = {k: ab(O_WBD + i * 2048, NCT * 128) for i, k in enumerate("fih")}

        ev_scan = [None] * NCT
        ev_sig2 = [None] * NCT
        ev_diff = [None] * NCT
        ev_hcopy = [None] * NCT
        ev_ef = [None] * NCT
        ev_ei = [None] * NCT
        ev_lnf = [None] * NCT
        ev_lni = [None] * NCT
        ev_f_mm = [None] * NCT
        ev_i_mm = [None] * NCT
        ev_h_mm = [None] * NCT
        ev_xbf = [None] * NCT

        for ct in range(NCT):
            si = ct % 2
            ev_xbf[ct] = P.op(
                "v", (lambda e_, c=ct, s=si: e_.tensor_copy(xbf_t(s)[:, :], X[c][:, :])),
                waits=(ev_xready[ct], ev_scan[ct - 2] if ct >= 2 else None),
                sig="v")
            for j in range(4):
                ev_f_mm[ct] = P.op(
                    "t",
                    (lambda e_, c=ct, jj=j, s=si:
                     e_.matmul(bk(jj), wbd_bf["f"][:, c * 128:(c + 1) * 128],
                               xbf_t(s)[:, jj * 512:(jj + 1) * 512],
                               start=True, stop=True)),
                    waits=(ev_xbf[ct], ev_wbd["f"],
                           ev_hcopy[ct - 1] if ct >= 1 else bankfree[j]),
                    sig="t" if j == 3 else None)
            for j in range(4):
                ev_i_mm[ct] = P.op(
                    "t",
                    (lambda e_, c=ct, jj=j, s=si:
                     e_.matmul(bk(4 + jj), wbd_bf["i"][:, c * 128:(c + 1) * 128],
                               xbf_t(s)[:, jj * 512:(jj + 1) * 512],
                               start=True, stop=True)),
                    waits=(ev_wbd["i"], ev_ei[ct - 1] if ct >= 1 else None),
                    sig="t" if j == 3 else None)
            Pd, Ph, Pf, Pm = (mgb(si, 0), mgb(si, 1), mgb(si, 2), mgb(si, 3))
            wb = ev_sig2[ct - 2] if ct >= 2 else None
            wb2 = ev_scan[ct - 2] if ct >= 2 else None
            ev_ef[ct] = P.op(
                "s", (lambda e_, p=Pd: e_.activation(p, pbA[:, :], ACT.Exp, scale=-1.0)),
                waits=(ev_f_mm[ct], wb, wb2), sig="s")
            ev_lnf[ct] = P.op(
                "s", (lambda e_, p=Pd: e_.activation(p, p, ACT.Ln, bias=1.0)),
                waits=(ev_ef[ct],), sig="s")
            ev_ei[ct] = P.op(
                "s", (lambda e_, p=Pf: e_.activation(p, pbB[:, :], ACT.Exp, scale=-1.0)),
                waits=(ev_i_mm[ct], wb, wb2), sig="s")
            ev_lni[ct] = P.op(
                "s", (lambda e_, p=Pf: e_.activation(p, p, ACT.Ln, bias=1.0)),
                waits=(ev_ei[ct],), sig="s")
            for j in range(4):
                ev_h_mm[ct] = P.op(
                    "t",
                    (lambda e_, c=ct, jj=j, s=si:
                     e_.matmul(bk(jj), wbd_bf["h"][:, c * 128:(c + 1) * 128],
                               xbf_t(s)[:, jj * 512:(jj + 1) * 512],
                               start=True, stop=True)),
                    waits=(ev_ef[ct], ev_wbd["h"]),
                    sig="t" if j == 3 else None)
            ev_diff[ct] = P.op(
                "v", (lambda e_, a=Pd, b=Pf: e_.tensor_sub(a, a, b)),
                waits=(ev_lnf[ct], ev_lni[ct]), sig="v")
            ev_hcopy[ct] = P.op(
                "v", (lambda e_, p=Ph: e_.tensor_copy(p, pbA[:, :])),
                waits=(ev_h_mm[ct],), sig="v")
            if ct % 2 == 1:
                for c2 in (ct - 1, ct):
                    s2 = c2 % 2
                    Qd, Qh, Qf, Qm = (mgb(s2, 0), mgb(s2, 1), mgb(s2, 2), mgb(s2, 3))
                    ev_fp = P.op(
                        "s", (lambda e_, a=Qd, o=Qf: e_.activation(o, a, ACT.Sigmoid, scale=-1.0)),
                        waits=(ev_diff[c2],), sig="s")
                    ev_sig2[c2] = P.op(
                        "s", (lambda e_, a=Qh, o=Qd: e_.activation(o, a, ACT.Sigmoid)),
                        waits=(ev_hcopy[c2], ev_fp), sig="s")
                    P.op("v", (lambda e_, a=Qh, o=Qm: e_.tensor_single_scalar(o, a, 0.0, ALU.is_ge)),
                         waits=(ev_sig2[c2],))
                    P.op("v", (lambda e_, a=Qh: e_.tensor_scalar(a, a, GEPS, None, ALU.add)))
                    P.op("v", (lambda e_, a=Qh, b=Qd: e_.tensor_sub(a, a, b)))
                    P.op("v", (lambda e_, a=Qh, b=Qm: e_.tensor_mul(a, a, b)))
                    P.op("v", (lambda e_, a=Qh, b=Qd: e_.tensor_add(a, a, b)))
                    P.op("v", (lambda e_, a=Qf, o=Qm:
                               e_.tensor_scalar(o, a, -1.0, 1.0, ALU.mult, ALU.add)))
                    P.op("v", (lambda e_, a=Qm, b=Qh: e_.tensor_mul(a, a, b)))
                    P.op("v", (lambda e_, f=Qf, v_=Qm, o=Qh:
                               e_.tensor_tensor_scan(o, f, v_, 0.0, ALU.mult, ALU.add)))
                    ev_scan[c2] = P.op(
                        "v", (lambda e_, c=c2, hh=Qh: e_.tensor_add(X[c][:, :], X[c][:, :], hh)),
                        sig="v")

        def dump_X(tag, waits):
            if tag in dbg:
                for ct in range(NCT):
                    P.dma("y", dbg[tag][ct * 128:(ct + 1) * 128, :], X[ct][:, :],
                          waits=waits)

        if upto == "mingru":
            dump_X("X", tuple(ev_scan))
            _finish(P, sems)
            P.realize(nc, sems)
            return nc

        def xn_t(ct, which):
            off = (O_XN + ct * 4096) if which == 1 else XN2_OFF[ct]
            return ab(off, T)

        rowsl = [af(O_LN_ROWS + i * 8192, T, 0, 1) for i in range(4)]
        repr_ = af(O_LN_REPS, T)
        repmr = af(O_LN_REPS + 8192, T)
        repr2 = af(O_LN_REPS + 16384, T)

        ev_sq = [None, None]
        ev_sums = None
        ev_sumsq = None
        ev_sumsq_ct = [None] * NCT
        for ct in range(NCT):
            sqb = af(O_LN_SQ + (ct % 2) * 8192, T)
            ev_sq[ct % 2] = P.op(
                "s", (lambda e_, c=ct, o=sqb: e_.activation(o, X[c][:, :], ACT.Square)),
                waits=(ev_scan[ct],
                       ev_sumsq_ct[ct - 2] if ct >= 2 else None), sig="s")
            for j in range(4):
                ev_sums = P.op(
                    "t",
                    (lambda e_, c=ct, jj=j:
                     e_.matmul(bk(jj)[0:1, :], ones_c[:, 0:1],
                               X[c][:, jj * 512:(jj + 1) * 512],
                               start=(c == 0), stop=(c == NCT - 1))),
                    waits=(ev_scan[ct], ev_ones,
                           ev_hcopy[NCT - 1] if ct == 0 else None),
                    sig="t" if j == 3 else None)
            for j in range(4):
                ev_sumsq = P.op(
                    "t",
                    (lambda e_, o=sqb, jj=j, c=ct:
                     e_.matmul(bk(4 + jj)[0:1, :], ones_c[:, 0:1],
                               o[:, jj * 512:(jj + 1) * 512],
                               start=(c == 0), stop=(c == NCT - 1))),
                    waits=(ev_sq[ct % 2],
                           ev_ei[NCT - 1] if ct == 0 else None),
                    sig="t" if j == 3 else None)
            ev_sumsq_ct[ct] = ev_sumsq

        ev_m = P.op("v", (lambda e_: e_.tensor_scalar(rowsl[0], pbA[0:1, :], 1.0 / D, None, ALU.mult)),
                    waits=(ev_sums,), sig="v")
        ev_q = P.op("v", (lambda e_: e_.tensor_scalar(rowsl[1], pbB[0:1, :], 1.0 / D, None, ALU.mult)),
                    waits=(ev_sumsq,), sig="v")
        P.op("v", (lambda e_: e_.tensor_mul(rowsl[2], rowsl[0], rowsl[0])))
        ev_var = P.op("v", (lambda e_: e_.tensor_sub(rowsl[1], rowsl[1], rowsl[2])), sig="v")
        ev_vare = P.op("v", (lambda e_: e_.tensor_scalar(rowsl[1], rowsl[1], LN_EPS, None, ALU.add)),
                       waits=(ev_var,), sig="v")
        ev_lnv = P.op("s", (lambda e_: e_.activation(rowsl[2], rowsl[1], ACT.Ln)),
                      waits=(ev_vare,), sig="s")
        ev_r = P.op("s", (lambda e_: e_.activation(rowsl[3], rowsl[2], ACT.Exp, scale=-0.5)),
                    waits=(ev_lnv,), sig="s")
        ev_rr = P.op("s", (lambda e_: e_.activation(rowsl[2], rowsl[2], ACT.Exp, scale=-1.0)),
                     waits=(ev_lnv, ev_r), sig="s")
        ev_msu = P.op("v", (lambda e_: e_.scalar_tensor_tensor(rowsl[2], rowsl[1], -LN_EPS,
                                                               rowsl[2], ALU.add, ALU.mult)),
                      waits=(ev_rr,), sig="v")
        ev_ms2 = P.op("v", (lambda e_: e_.tensor_scalar(rowsl[2], rowsl[2], LN_EPS, None, ALU.add)),
                      sig="v")
        ev_ln2 = P.op("s", (lambda e_: e_.activation(rowsl[2], rowsl[2], ACT.Ln)),
                      waits=(ev_ms2,), sig="s")
        ev_r2 = P.op("s", (lambda e_: e_.activation(rowsl[2], rowsl[2], ACT.Exp, scale=-0.5)),
                     waits=(ev_ln2,), sig="s")
        ev_mr = P.op("v", (lambda e_: e_.tensor_mul(rowsl[0], rowsl[0], rowsl[3])),
                     waits=(ev_r,), sig="v")

        def replicate(dst, src_row, waits, bank_wait):
            last = None
            for j in range(4):
                last = P.op(
                    "t",
                    (lambda e_, jj=j, s_=src_row:
                     e_.matmul(bk(jj), ones_r[0:1, :],
                               s_[0:1, jj * 512:(jj + 1) * 512], start=True, stop=True)),
                    waits=tuple(waits) + (ev_ones2, bank_wait),
                    sig="t" if j == 3 else None)
            return P.op("v", (lambda e_, d=dst: e_.tensor_copy(d, pbA[:, :])),
                        waits=(last,), sig="v")

        ev_repr = replicate(repr_, rowsl[3], (ev_r, ev_m), ev_m)
        ev_repmr = replicate(repmr, rowsl[0], (ev_mr,), ev_repr)
        ev_repr2 = replicate(repr2, rowsl[2], (ev_r2,), ev_repmr)

        ev_xn1 = [None] * NCT
        ev_x1 = [None] * NCT
        for ct in range(NCT):
            sqb = af(O_LN_SQ + (ct % 2) * 8192, T)
            P.op("v", (lambda e_, c=ct, o=sqb: e_.scalar_tensor_tensor(
                o, X[c][:, :], 1.0, repr_, ALU.mult, ALU.mult)),
                waits=(ev_repr, ev_sumsq))
            ev_x1[ct] = P.op("v", (lambda e_, c=ct, a=sqb: e_.tensor_sub(X[c][:, :], a, repmr)),
                             waits=(ev_repmr,), sig="v")
            P.op("v", (lambda e_, c=ct, o=sqb: e_.tensor_mul(o, X[c][:, :], repr2)),
                 waits=(ev_repr2,))
            ev_xn1[ct] = P.op(
                "v", (lambda e_, c=ct, a=sqb: e_.tensor_scalar(
                    xn_t(c, 1), a, gam1[:, c:c + 1], None, ALU.mult)),
                sig="v")
        ev_x1dma = [None] * NCT
        for ct in range(NCT):
            ev_x1dma[ct] = P.dma("y", x1sc[ct * 128:(ct + 1) * 128, :], X[ct][:, :],
                                 waits=(ev_x1[ct],))
        ev_x1dma_all = ev_x1dma[NCT - 1]

        if upto == "ln1":
            dump_X("X", tuple(ev_x1))
            if "xn1" in dbg:
                for ct in range(NCT):
                    evc = P.op("v", (lambda e_, c=ct: e_.tensor_copy(
                        af(O_LN_SQ, T), xn_t(c, 1))), waits=(ev_xn1[ct],), sig="v")
                    P.dma("y", dbg["xn1"][ct * 128:(ct + 1) * 128, :], af(O_LN_SQ, T),
                          waits=(evc,))
            _finish(P, sems)
            P.realize(nc, sems)
            return nc

        lat_bf = ab(O_LAT, T)
        kr_b = ab(O_KR, T, 0, 64)
        kvtm_all = ab(O_KVTM, NT * 512)
        wkva_bf = ab(O_WKVA, NCT * 192)
        wkvb_bf = ab(O_WKVB, 512)
        wq_bf = ab(O_WQ, NCT * D)
        wo_bf = ab(O_WO, 4 * D)
        ccq = ab(O_CCQ, T)
        csq = ab(O_CSQ, T)
        smask = list(range(16, 32)) + list(range(0, 16))

        ev_lat_mm = None
        for tc in range(4):
            for cc in range(NCT):
                ev_lat_mm = P.op(
                    "t",
                    (lambda e_, c=cc, j=tc:
                     e_.matmul(bk(j), wkva_bf[:, c * 192: c * 192 + 128],
                               xn_t(c, 1)[:, j * 512:(j + 1) * 512],
                               start=(c == 0), stop=(c == NCT - 1))),
                    waits=(ev_xn1[cc], ev_wkva),
                    sig="t" if cc == NCT - 1 else None)
        ev_lat = P.op("v", (lambda e_: e_.tensor_copy(lat_bf[:, :], pbA[:, :])),
                      waits=(ev_lat_mm,), sig="v")
        ev_kr_mm = None
        for tc in range(4):
            for cc in range(NCT):
                ev_kr_mm = P.op(
                    "t",
                    (lambda e_, c=cc, j=tc:
                     e_.matmul(bk(4 + j)[0:64, :], wkva_bf[:, c * 192 + 128: c * 192 + 192],
                               xn_t(c, 1)[:, j * 512:(j + 1) * 512],
                               start=(c == 0), stop=(c == NCT - 1))),
                    waits=(ev_xn1[cc], ev_wkva),
                    sig="t" if cc == NCT - 1 else None)
        ev_kr_raw = P.op("v", (lambda e_: e_.tensor_copy(kr_b[32:64, :], pbB[32:64, :])),
                         waits=(ev_kr_mm,), sig="v")
        krs = ab(O_SPARE, T, 0, 64)
        P.op("v", (lambda e_: e_.stream_shuffle(krs[32:64, :], kr_b[32:64, :], smask)),
             waits=(ev_kr_raw, ev_tab))
        P.op("v", (lambda e_: e_.tensor_mul(kr_b[32:64, :], kr_b[32:64, :], ccq[32:64, :])))
        P.op("v", (lambda e_: e_.tensor_mul(krs[32:64, :], krs[32:64, :], csq[32:64, :])))
        ev_kr = P.op("v", (lambda e_: e_.tensor_add(kr_b[32:64, :], kr_b[32:64, :], krs[32:64, :])),
                     sig="v")

        ev_kvtm_mm = [None] * NT
        ev_kvtm = [None] * NT
        for kt in range(NT):
            b = kt % 4
            ev_kvtm_mm[kt] = P.op(
                "t",
                (lambda e_, k=kt, bb=b:
                 e_.matmul(bk(bb), lat_bf[:, k * 128:(k + 1) * 128], wkvb_bf[:, :],
                           start=True, stop=True)),
                waits=(ev_lat, ev_wkvb, ev_kvtm[kt - 4] if kt >= 4 else None),
                sig="t")
            ev_kvtm[kt] = P.op(
                "v", (lambda e_, k=kt, bb=b: e_.tensor_copy(kvtm_all[:, k * 512:(k + 1) * 512], bk(bb))),
                waits=(ev_kvtm_mm[kt],), sig="v")
        ev_kvfm = [None] * 4
        for g in range(4):
            ev_mm = None
            for tc in range(4):
                ev_mm = P.op(
                    "t",
                    (lambda e_, gg=g, j=tc:
                     e_.matmul(bk(4 + j), wkvb_bf[:, gg * 128:(gg + 1) * 128],
                               lat_bf[:, j * 512:(j + 1) * 512], start=True, stop=True)),
                    waits=(ev_lat, ev_kvfm[g - 1] if g >= 1 else ev_kr_raw),
                    sig="t" if tc == 3 else None)
            ev_kvfm[g] = P.op(
                "v", (lambda e_, gg=g: e_.tensor_copy(ab(O_KVFM + gg * 4096, T), pbB[:, :])),
                waits=(ev_mm,), sig="v")

        qsh = ab(O_SPARE, T)
        ev_qh2 = [None] * 8
        ev_qh2cp = [None] * 8
        ev_qproj_last = None
        for jp in range(8):
            ev_mm = None
            for tc in range(4):
                b = tc if jp % 2 == 0 else 4 + tc
                for cc in range(NCT):
                    ev_mm = P.op(
                        "t",
                        (lambda e_, c=cc, j=tc, p_=jp, bb=b:
                         e_.matmul(bk(bb), wq_bf[:, c * D + p_ * 128: c * D + p_ * 128 + 128],
                                   xn_t(c, 1)[:, j * 512:(j + 1) * 512],
                                   start=(c == 0), stop=(c == NCT - 1))),
                        waits=(ev_xn1[cc], ev_wq,
                               ev_qh2cp[jp - 2] if jp >= 2 else
                               (ev_kvfm[3] if jp % 2 == 1 else ev_kvtm[NT - 1])),
                        sig="t" if cc == NCT - 1 else None)
            psrc = pbA if jp % 2 == 0 else pbB
            ev_qh2cp[jp] = P.op(
                "v", (lambda e_, p_=jp, ps=psrc: e_.tensor_scalar(
                    ab(O_QH2 + p_ * 4096, T), ps[:, :], 0.125, None, ALU.mult)),
                waits=(ev_mm, ev_kr if jp == 0 else None), sig="v")
            ev_qproj_last = ev_mm
            P.op("v", (lambda e_, p_=jp: e_.stream_shuffle(qsh[:, :], ab(O_QH2 + p_ * 4096, T), smask)),
                 waits=(ev_tab,))
            P.op("v", (lambda e_, p_=jp: e_.tensor_mul(ab(O_QH2 + p_ * 4096, T),
                                                       ab(O_QH2 + p_ * 4096, T), ccq[:, :])))
            P.op("v", (lambda e_: e_.tensor_mul(qsh[:, :], qsh[:, :], csq[:, :])))
            ev_qh2[jp] = P.op(
                "v", (lambda e_, p_=jp: e_.tensor_add(ab(O_QH2 + p_ * 4096, T),
                                                      ab(O_QH2 + p_ * 4096, T), qsh[:, :])),
                sig="v")

        ev_ebr_mm = P.op(
            "t", (lambda e_: e_.matmul(bk(0)[:, 0:8], ones_r[0:1, :], ebias_sb[0:1, :],
                                       start=True, stop=True)),
            waits=(ev_eb, ev_ones2, ev_qh2cp[6]), sig="t")
        ev_ebr = P.op("v", (lambda e_: e_.tensor_copy(ebias_rep[:, :], bk(0)[:, 0:8])),
                      waits=(ev_ebr_mm, ev_qh2cp[7]), sig="v")

        if upto == "pre":
            dump_X("X", tuple(ev_x1))
            _finish(P, sems)
            P.realize(nc, sems)
            return nc

        ev_exp_prev = [ev_ebr, ev_ebr]
        ev_pair_done = [None] * 8
        ev_kp_used = [None, None]
        ev_kp_ready = [None, None]
        ev_kp_last_dma = None
        ev_ygrp_prev = {}
        ev_ytm_free = None
        ev_ynorm_prev = [None, None]
        ev_ptevac_prev = [None, None]
        ev_av_prev = [None, None]
        ev_rl_read_prev = [None, None]
        ev_wo_done = [None] * 4
        ev_wo_evac = {}

        def kp_view(par, p0=0, p1=128):
            return ab(O_KP + par * 4096, T, p0, p1)

        def assemble_kp(jp, extra=()):
            nonlocal ev_kp_last_dma
            par = jp % 2
            last = None
            for (hh, dstlo) in ((2 * jp, 0), (2 * jp + 1, 64)):
                g, a = hh // 4, hh % 4
                src = ab(O_KVFM + g * 4096, T, a * 32, a * 32 + 32)
                last = P.dma("y", kp_view(par, dstlo, dstlo + 32), src,
                             waits=(ev_kvfm[g], ev_kp_used[par]) + tuple(extra))
                last = P.dma("y", kp_view(par, dstlo + 32, dstlo + 64), kr_b[32:64, :],
                             waits=(ev_kr,))
            ev_kp_last_dma = last
            return last

        ev_kp_ready[0] = assemble_kp(0, extra=(ev_qproj_last,))
        ev_kp_ready[1] = assemble_kp(1, extra=(ev_qproj_last,))

        pend_pt = []
        pend_rest = []

        ptev_stash = [None]

        def flush_ptev():
            if ptev_stash[0] is not None:
                ptev_stash[0]()
                ptev_stash[0] = None

        def flush_all():
            nonlocal pend_pt, pend_rest
            for fnp in pend_pt:
                fnp()
            pend_pt = []
            flush_ptev()
            for fnp in pend_rest:
                fnp()
            pend_rest = []

        def PS(p):
            return pbA if p == 0 else pbB

        ev_ptdone_prev = [None, None]
        ev_ptevac_prev2 = [None, None]
        ev_max_prev = [None, None]

        for jp in range(_ATTN_PAIRS):
            par = jp % 2
            for h2 in range(2):
                h = 2 * jp + h2
                base = 64 * h2
                for qt in range(_ATTN_QT):
                    p = qt % 2
                    W = (qt + 1) * 128
                    nch = (W + 511) // 512
                    cs = p * 4
                    ps = PS(p)
                    while len(pend_rest) >= 2:
                        pend_rest.pop(0)()
                    ev_smm = None
                    for j in range(nch):
                        n = min(512, W - j * 512)
                        last_chunk = (j == nch - 1)
                        ev_smm = P.op(
                            "t",
                            (lambda e_, jj=j, nn=n, p_=jp, b_=base, q_=qt, pr=par, ps_=ps, lc=last_chunk:
                             e_.matmul(ps_[:, jj * 512: jj * 512 + nn],
                                       ab(O_QH2 + p_ * 4096, T, b_, b_ + 64)[:, q_ * 128:(q_ + 1) * 128],
                                       kp_view(pr, b_, b_ + 64)[:, jj * 512: jj * 512 + nn],
                                       start=True, stop=not lc)),
                            waits=(ev_qh2[jp], ev_kp_ready[par], ev_exp_prev[p],
                                   ev_ptevac_prev[p],
                                   ev_ptevac_prev2[p] if nch >= 2 else None,
                                   ev_ynorm_prev[p] if nch >= 3 else None),
                            sig=None)
                    ev_smm = P.op(
                        "t",
                        (lambda e_, q_=qt, ps_=ps:
                         e_.matmul(ps_[:, q_ * 128:(q_ + 1) * 128],
                                   ident_bf[:, :], trimask_bf[:, :],
                                   start=False, stop=True)),
                        waits=(ev_tri_bf, ev_id_bf), sig="t")
                    ev_pair_done[jp] = ev_smm
                    ev_kp_used[par] = ev_smm
                    for fnp in pend_pt:
                        fnp()
                    pend_pt = []
                    ev_max = P.op(
                        "v",
                        (lambda e_, w_=W, c=cs, ps_=ps: e_.reduce_max(colw[:, c:c + 1], ps_[:, 0:w_],
                                                                      axis=AX, negate=True)),
                        waits=(ev_smm,), hwaits=(ev_rl_read_prev[p],), sig="v")
                    ev_max_prev[p] = ev_max
                    flush_ptev()
                    ev_exp = P.op(
                        "s",
                        (lambda e_, w_=W, c=cs, pr=p, ps_=ps:
                         e_.activation(ab(O_P + pr * 4096, T)[:, 0:w_], ps_[:, 0:w_], ACT.Exp,
                                       bias=colw[:, c:c + 1], scale=1.0,
                                       accum_out=colw[:, c + 2:c + 3])),
                        waits=(ev_max, ev_ptdone_prev[p]),
                        sig="s")
                    ev_rl = P.op(
                        "v",
                        (lambda e_, c=cs: e_.reciprocal(colw[:, c + 3:c + 4], colw[:, c + 2:c + 3])),
                        waits=(ev_exp,), sig="v")
                    ev_exp_prev[p] = ev_exp

                    def make_pt(qt_=qt, pr=qt % 2, ev_exp_=ev_exp):
                        def tail_pt():
                            nkt = qt_ + 1
                            ps_ = PS(pr)
                            psb = ps_[:, 0:1024].bitcast(BF16)
                            ev_tr = None
                            for kt in range(nkt):
                                ev_tr = P.op(
                                    "t",
                                    (lambda e_, k=kt, pb=psb, p2=pr:
                                     e_.transpose(pb[:, k * 128:(k + 1) * 128],
                                                  ab(O_P + p2 * 4096, T)[:, k * 128:(k + 1) * 128],
                                                  ident_bf[:, :])),
                                    waits=(ev_exp_, ev_ptevac_prev[pr],
                                           ev_ptevac_prev2[pr] if nkt > 8 else None,
                                           ev_id_bf),
                                    sig="t" if kt == nkt - 1 else None)
                            ev_ptdone_prev[pr] = ev_tr
                            n_all = nkt * 128
                            if _PTEV and n_all > 1024:
                                def emit_ptev(nc_=n_all, pb=psb, p2=pr, et=ev_tr):
                                    ev_ptevac_prev2[p2] = P.op(
                                        "v",
                                        (lambda e_:
                                         e_.tensor_copy(
                                             ab(O_PT + p2 * 4096, T)[:, 1024:nc_],
                                             pb[:, 1024:nc_])),
                                        waits=(et,), sig="v")
                                ptev_stash[0] = emit_ptev
                                n_s = 1024
                            else:
                                n_s = n_all
                            ev_pte = P.op(
                                "s",
                                (lambda e_, nc_=n_s, pb=psb, p2=pr:
                                 e_.copy(ab(O_PT + p2 * 4096, T)[:, 0:nc_], pb[:, 0:nc_])),
                                waits=(ev_tr,), sig="s")
                            ev_ptevac_prev[pr] = ev_pte
                            return ev_pte
                        return tail_pt

                    def make_rest(h_=h, qt_=qt, cs_=cs, h2_=h2, pr=qt % 2, ev_rl_=ev_rl):
                        def tail_rest():
                            nkt = qt_ + 1
                            ps_ = PS(pr)
                            ev_pte = ev_ptevac_prev[pr]
                            ev_pte2 = ev_ptevac_prev2[pr] if qt_ >= 8 else None
                            ev_mx = ev_max_prev[pr] if qt_ >= 8 else None
                            ev_av = None
                            for kt in range(nkt):
                                ev_av = P.op(
                                    "t",
                                    (lambda e_, k=kt, hh=h_, p2=pr, q2=qt_, ps2=ps_:
                                     e_.matmul(ps2[:, 1024:1056],
                                               ab(O_PT + p2 * 4096, T)[:, k * 128:(k + 1) * 128],
                                               kvtm_all[:, k * 512 + 32 * hh: k * 512 + 32 * hh + 32],
                                               start=(k == 0), stop=(k == q2))),
                                    waits=(ev_pte, ev_pte2, ev_kvtm[kt],
                                           ev_ynorm_prev[pr], ev_mx),
                                    sig="t" if kt == nkt - 1 else None)
                            ev_av_prev[pr] = ev_av
                            ev_yn = P.op(
                                "v",
                                (lambda e_, c=cs_, q2=qt_, hh2=h2_, ps2=ps_: e_.tensor_scalar(
                                    ab(O_YTM, 1024)[:, q2 * 64 + hh2 * 32: q2 * 64 + hh2 * 32 + 32],
                                    ps2[:, 1024:1056],
                                    colw[:, c + 3:c + 4], None, ALU.mult)),
                                waits=(ev_av, ev_ytm_free), hwaits=(ev_rl_,), sig="v")
                            ev_ynorm_prev[pr] = ev_yn
                            ev_rl_read_prev[pr] = ev_yn
                        return tail_rest

                    pend_pt.append(make_pt())
                    pend_rest.append(make_rest())
            if jp + 2 <= 7:
                ev_kp_ready[par] = assemble_kp(jp + 2)
            flush_all()
            rbase = 64 * (jp % 2)
            ev_tre = {}
            for qg in range(4):
                last_tr = None
                for qq in range(4):
                    qt_ = qg * 4 + qq
                    last_tr = P.op(
                        "t",
                        (lambda e_, q_=qt_, b_=qg, rb=rbase:
                         e_.transpose(pbA[:, b_ * 512: b_ * 512 + 512]
                                      .bitcast(BF16)[rb:rb + 64, (q_ % 4) * 128:(q_ % 4 + 1) * 128],
                                      ab(O_YTM, 1024)[:, q_ * 64:(q_ + 1) * 64],
                                      ident_bf[:, :], tile_position=(0, rb))),
                        waits=(ev_ynorm_prev[0], ev_ynorm_prev[1],
                               ev_exp_prev[0], ev_ptevac_prev[0],
                               ev_ptevac_prev2[0], ev_ygrp_prev.get(qg)),
                        sig="t" if qq == 3 else None)
                ev_tre[qg] = last_tr
            ev_ytm_free = ev_tre[3]
            for qg in range(4):
                ev_ygrp_prev[qg] = P.op(
                    "v",
                    (lambda e_, b_=qg, rb=rbase:
                     e_.tensor_copy(ab(O_YGRP, T, rb, rb + 64)[:, b_ * 512:(b_ + 1) * 512],
                                    pbA[:, b_ * 512: b_ * 512 + 512].bitcast(BF16)[rb:rb + 64, 0:512])),
                    waits=(ev_tre[qg],), sig="v")
            ev_exp_prev[0] = ev_ygrp_prev[3]
            if jp % 2 == 1 and _K_WO:
                g = jp // 2
                last_mm = None
                for cc in range(NCT):
                    for tc in range(4):
                        b = tc
                        last_mm = P.op(
                            "t",
                            (lambda e_, c=cc, j=tc, gg=g, bb=b:
                             e_.matmul(bk(bb),
                                       wo_bf[:, gg * D + c * 128: gg * D + c * 128 + 128],
                                       ab(O_YGRP, T)[:, j * 512:(j + 1) * 512],
                                       start=True, stop=True)),
                            waits=(ev_ygrp_prev[0], ev_ygrp_prev[1],
                                   ev_ygrp_prev[2], ev_ygrp_prev[3], ev_wo,
                                   ev_exp_prev[0], ev_exp_prev[1],
                                   ev_wo_evac.get((cc - 1, tc))),
                            sig="t")
                        ev_wo_evac[(cc, tc)] = P.op(
                            "v",
                            (lambda e_, c=cc, j=tc, bb=b:
                             e_.scalar_tensor_tensor(X[c][:, j * 512:(j + 1) * 512],
                                                     bk(bb), 1.0,
                                                     X[c][:, j * 512:(j + 1) * 512],
                                                     ALU.mult, ALU.add)),
                            waits=(last_mm,) + ((tuple(ev_x1dma)) if g == 0 else ()),
                            sig="v")
                ev_wo_done[g] = last_mm
                ev_exp_prev[0] = ev_wo_evac[(NCT - 1, 3)]
                ev_exp_prev[1] = ev_wo_evac[(NCT - 1, 3)]
                ev_wo_evac = {}

        flush_all()
        ev_attn_done = ev_wo_done[3]
        ev_attn_vdone = ev_exp_prev[0]

        if upto == "attn":
            dump_X("X", (ev_attn_vdone,))
            _finish(P, sems)
            P.realize(nc, sems)
            return nc

        m2row = af(O_M2ROW, T, 0, 1)
        mrep = af(O_MREP, T)
        ev_sq2p = [None, None]
        ev_sumsq2 = None
        ev_sumsq2_ct = [None] * NCT
        for ct in range(NCT):
            sqb = af(O_M2SQ[ct % 2], T)
            ev_sq2p[ct % 2] = P.op(
                "s", (lambda e_, c=ct, o=sqb: e_.activation(o, X[c][:, :], ACT.Square)),
                waits=(ev_attn_vdone, ev_attn_done,
                       ev_sumsq2_ct[ct - 2] if ct >= 2 else None,
                       ev_ptevac_prev[0] if ct == 0 else None,
                       ev_ptevac_prev[1] if ct == 0 else None),
                sig="s")
            for j in range(4):
                ev_sumsq2 = P.op(
                    "t",
                    (lambda e_, o=sqb, jj=j, c=ct:
                     e_.matmul(bk(jj)[0:1, :], ones_c[:, 0:1],
                               o[:, jj * 512:(jj + 1) * 512],
                               start=(c == 0), stop=(c == NCT - 1))),
                    waits=(ev_sq2p[ct % 2],),
                    sig="t" if j == 3 else None)
            ev_sumsq2_ct[ct] = ev_sumsq2
        ev_q2 = P.op("v", (lambda e_: e_.tensor_scalar(m2row, pbA[0:1, :], 1.0 / D, None, ALU.mult)),
                     waits=(ev_sumsq2, ev_kp_last_dma), sig="v")
        ev_q2e = P.op("v", (lambda e_: e_.tensor_scalar(m2row, m2row, LN_EPS, None, ALU.add)),
                      waits=(ev_q2,), sig="v")
        ev_l2 = P.op("s", (lambda e_: e_.activation(m2row, m2row, ACT.Ln)),
                     waits=(ev_q2e,), sig="s")
        ev_r2b = P.op("s", (lambda e_: e_.activation(m2row, m2row, ACT.Exp, scale=-0.5)),
                      waits=(ev_l2,), sig="s")
        ev_mrep = replicate(mrep, m2row, (ev_r2b,), ev_q2)
        ev_xn2 = [None] * NCT
        for ct in range(NCT):
            sqb = af(O_M2SQ[ct % 2], T)
            P.op("v", (lambda e_, c=ct, o=sqb: e_.tensor_mul(o, X[c][:, :], mrep)),
                 waits=(ev_mrep, ev_sumsq2))
            ev_xn2[ct] = P.op(
                "v", (lambda e_, c=ct, a=sqb: e_.tensor_scalar(
                    xn_t(c, 2), a, gam2[:, c:c + 1], None, ALU.mult)),
                waits=(ev_gam2,), sig="v")

        if upto == "rms2":
            dump_X("X", tuple(ev_xn2))
            _finish(P, sems)
            P.realize(nc, sems)
            return nc

        wgate_bf = ab(O_WGATE, NCT * 8)
        comb_fm = af(O_COMB, T, 0, 8)
        LG = af(O_LG, 128)
        PR = af(O_PR, 128)
        MS = af(O_MS, 128)
        MS2 = af(O_MS2, 128)

        def a3(t_, d1=16, d2=8, bcast_from=None):
            h_ = t_.tensor if hasattr(t_, "tensor") else None
            return t_

        def v3(base_off):
            return bass.AP(arena, base_off // 4 * 0 + 0, [[0, 0]])

        def lg3(off):
            return bass.AP(arena, off // 4, [[AR_BYTES // 4, 128], [8, 16], [1, 8]]).bitcast_hack \
                if False else None

        arf = arena[:, :].bitcast(F32)

        def t3(off):
            return bass.AP(arf.tensor, off // 4, [[AR_BYTES // 4, 128], [8, 16], [1, 8]]) \
                if False else None

        def ap3(off, last_stride=1):
            base = af(off, 128)
            t_ = base.tensor
            return bass.AP(t_, base.offset, [[base.ap[0][0], 128], [8 if last_stride else 0, 16], [last_stride, 8]])

        def apb(off, col):
            base = af(off, 128)
            t_ = base.tensor
            return bass.AP(t_, base.offset + col, [[base.ap[0][0], 128], [1, 16], [0, 8]])

        ev_gate_prev = ev_xn2[NCT - 1]
        ev_gmm = None
        for tt in range(NT):
            for cc in range(NCT):
                ev_gmm = P.op(
                    "t",
                    (lambda e_, c=cc, t_=tt:
                     e_.matmul(bk(6)[:, 8 * t_: 8 * t_ + 8], xn_t(c, 2)[:, t_ * 128:(t_ + 1) * 128],
                               wgate_bf[:, c * 8:(c + 1) * 8],
                               start=(c == 0), stop=(c == NCT - 1))),
                    waits=(ev_xn2[cc], ev_wgate,
                           ev_dn_evac_guard if False else None,
                           ev_attn_done if (tt == 0 and cc == 0) else None),
                    sig="t" if (tt == NT - 1 and cc == NCT - 1) else None)
        eb3 = bass.AP(ebias_rep, 0, [[8, 128], [0, 16], [1, 8]])
        ev_lg = P.op(
            "v", (lambda e_: e_.tensor_tensor(ap3(O_LG), bk(6)[:, 0:128].rearrange(
                "p (a b) -> p a b", b=8), eb3, ALU.add)),
            waits=(ev_gmm, ev_ebr), sig="v")
        ev_m8 = P.op("v", (lambda e_: e_.reduce_max(MS[:, 0:16], ap3(O_LG), axis=AX, negate=True)),
                     hwaits=(ev_lg,), sig="v")
        ev_sub = P.op("v", (lambda e_: e_.tensor_tensor(ap3(O_LG), ap3(O_LG), apb(O_MS, 0), ALU.add)),
                      hwaits=(ev_m8,), sig="v")
        ev_e8 = P.op("s", (lambda e_: e_.activation(LG, LG, ACT.Exp)),
                     waits=(ev_sub,), sig="s")
        ev_l8 = P.op("v", (lambda e_: e_.reduce_sum(MS[:, 16:32], ap3(O_LG), axis=AX)),
                     waits=(ev_e8,), sig="v")
        ev_rs = P.op("v", (lambda e_: e_.reciprocal(MS[:, 32:48], MS[:, 16:32])),
                     hwaits=(ev_l8,), sig="v")
        ev_pr = P.op("v", (lambda e_: e_.tensor_tensor(ap3(O_PR), ap3(O_LG), apb(O_MS, 32), ALU.mult)),
                     hwaits=(ev_rs,), sig="v")
        ev_m1 = P.op("v", (lambda e_: e_.reduce_max(MS[:, 0:16], ap3(O_PR), axis=AX)),
                     hwaits=(ev_pr,), sig="v")
        ev_mk1 = P.op("v", (lambda e_: e_.tensor_tensor(ap3(O_LG), ap3(O_PR), apb(O_MS, 0), ALU.is_ge)),
                      hwaits=(ev_m1,), sig="v")
        ev_p2 = P.op("v", (lambda e_: e_.scalar_tensor_tensor(ap3(O_MS2), ap3(O_LG), -1e9,
                                                              ap3(O_PR), ALU.mult, ALU.add)),
                     hwaits=(ev_mk1,), sig="v")
        ev_m2 = P.op("v", (lambda e_: e_.reduce_max(MS[:, 48:64], ap3(O_MS2), axis=AX)),
                     hwaits=(ev_p2,), sig="v")
        ev_mk2 = P.op("v", (lambda e_: e_.tensor_tensor(ap3(O_MS2), ap3(O_MS2), apb(O_MS, 48), ALU.is_ge)),
                      hwaits=(ev_m2,), sig="v")
        ev_mks = P.op("v", (lambda e_: e_.tensor_tensor(ap3(O_LG), ap3(O_LG), ap3(O_MS2), ALU.add)),
                      hwaits=(ev_mk2,), sig="v")
        ev_ws = P.op("v", (lambda e_: e_.tensor_add(MS[:, 64:80], MS[:, 0:16], MS[:, 48:64])),
                     hwaits=(ev_m2,), sig="v")
        ev_rw = P.op("v", (lambda e_: e_.reciprocal(MS[:, 80:96], MS[:, 64:80])),
                     hwaits=(ev_ws,), sig="v")
        ev_cb1 = P.op("v", (lambda e_: e_.tensor_tensor(ap3(O_PR), ap3(O_PR), ap3(O_LG), ALU.mult)),
                      hwaits=(ev_mks,), sig="v")
        ev_comb = P.op("v", (lambda e_: e_.tensor_tensor(ap3(O_PR), ap3(O_PR), apb(O_MS, 80), ALU.mult)),
                       hwaits=(ev_rw, ev_cb1), sig="v")
        ev_combevac = None
        ev_ctr_prev = None
        for tt in range(NT):
            ev_ctr_prev = P.op(
                "t",
                (lambda e_, t_=tt: e_.transpose(pbB[0:8, 1536 + (t_ % 4) * 128: 1536 + (t_ % 4 + 1) * 128],
                                                PR[:, 8 * t_: 8 * t_ + 8], ident_f[:, :])),
                waits=(ev_comb, ev_combevac if tt % 4 == 0 and tt > 0 else None,
                       ev_attn_done if tt == 0 else None),
                sig="t")
            if tt % 4 == 3:
                ev_combevac = P.op(
                    "v",
                    (lambda e_, t_=tt: e_.tensor_copy(comb_fm[0:8, (t_ - 3) * 128:(t_ + 1) * 128],
                                                      pbB[0:8, 1536:2048])),
                    waits=(ev_ctr_prev,), sig="v")
        ev_comb_done = ev_combevac

        if upto == "gate":
            dump_X("X", (ev_comb_done,))
            if "comb" in dbg:
                P.dma("y", dbg["comb"][:, :], comb_fm[0:8, :], waits=(ev_comb_done,))
            _finish(P, sems)
            P.realize(nc, sems)
            return nc

        def wslot(i, n):
            return ab(MOE_WSLOT[i], n)

        slot_free = {0: ev_qh2[7], 1: ev_qh2[7], 2: ev_qh2[7],
                     3: ev_pair_done[5], 4: ev_pair_done[5], 5: ev_pair_done[5]}
        hfree = {i: ev_attn_done for i in range(9)}
        sgb = af(O_SG, 512)
        combrep = af(O_COMBREP, 512)
        csel_sb = af(O_CSEL, 1024, 0, 8)
        ev_csel = P.dma("y", csel_sb, d_csel, waits=(ev_attn_done,))
        ev_wg_dma = {}
        ev_wu_dma = {}
        ev_wd_dma = {}
        ev_hall = {}
        ev_prev_silu = None
        ev_down_read = None
        ev_hwr = {}
        ev_prev_hwr = None
        ev_sil_done = {}
        ev_umul_done = {}
        ev_dn_evac = {}
        ev_dn_free = {}
        ev_dn_last = {}
        ev_cre_prev = None
        ev_dn_done = None

        for tc in range(_K_MOE_TC):
            for e9 in range(_K_MOE_E):
                k = tc * 9 + e9
                sgu = (2 * k) % 4
                suu = sgu + 1
                ev_wg_dma[k] = P.dma("g", wslot(sgu, NCT * 512), d_wgu[e9],
                                     waits=(slot_free[sgu],))
                ev_wu_dma[k] = P.dma("g", wslot(suu, NCT * 512), d_wuu[e9],
                                     waits=(slot_free[suu],))
                if e9 < 8:
                    ev_crm = P.op(
                        "t",
                        (lambda e_, ee=e9, j=tc:
                         e_.matmul(bk(3), csel_sb[:, ee * 128:(ee + 1) * 128],
                                   comb_fm[0:8, j * 512:(j + 1) * 512],
                                   start=True, stop=True)),
                        waits=(ev_comb_done, ev_cre_prev, ev_csel,
                               ev_dn_free.get((tc, 3))), sig="t")
                    ev_cre = P.op(
                        "v", (lambda e_: e_.tensor_copy(combrep[:, :], bk(3))),
                        waits=(ev_crm,), sig="v")
                    ev_cre_prev = ev_cre
                else:
                    ev_cre = None
                ev_gmm = None
                ev_umm = None
                for ic in range(4):
                    bg = 4 + 2 * (ic % 2)
                    bu = bg + 1
                    for cc in range(NCT):
                        ev_gmm = P.op(
                            "t",
                            (lambda e_, c=cc, i_=ic, j=tc, s_=sgu, b_=bg:
                             e_.matmul(bk(b_),
                                       wslot(s_, NCT * 512)[:, c * 512 + i_ * 128: c * 512 + i_ * 128 + 128],
                                       xn_t(c, 2)[:, j * 512:(j + 1) * 512],
                                       start=(c == 0), stop=(c == NCT - 1))),
                            waits=(ev_wg_dma[k], ev_xn2[cc],
                                   ev_sil_done.get((k, ic - 2)) or ev_sil_done.get((k - 1, ic + 2)),
                                   ev_dn_free.get((tc, bg)) if e9 == 0 else None,
                                   ev_hwr.get((k - 1, ic))),
                            sig="t" if cc == NCT - 1 else None)
                    for cc in range(NCT):
                        ev_umm = P.op(
                            "t",
                            (lambda e_, c=cc, i_=ic, j=tc, s_=suu, b_=bu:
                             e_.matmul(bk(b_),
                                       wslot(s_, NCT * 512)[:, c * 512 + i_ * 128: c * 512 + i_ * 128 + 128],
                                       xn_t(c, 2)[:, j * 512:(j + 1) * 512],
                                       start=(c == 0), stop=(c == NCT - 1))),
                            waits=(ev_wu_dma[k],
                                   ev_umul_done.get((k, ic - 2)) or ev_umul_done.get((k - 1, ic + 2)),
                                   ev_dn_free.get((tc, bu)) if e9 == 0 else None),
                            sig="t" if cc == NCT - 1 else None)
                    ev_sl = P.op(
                        "s", (lambda e_, b_=bg: e_.activation(sgb, bk(b_), ACT.Silu)),
                        waits=(ev_gmm, ev_prev_silu, ev_prev_hwr), sig="s")
                    ev_prev_silu = ev_sl
                    ev_sil_done[(k, ic)] = ev_sl
                    ev_umul_done[(k, ic)] = P.op(
                        "v", (lambda e_, b_=bu: e_.tensor_mul(sgb, bk(b_), sgb)),
                        waits=(ev_sl, ev_umm), sig="v")
                    hdst = ab(MOE_H[e9], T)
                    if e9 < 8:
                        ev_h = P.op(
                            "v",
                            (lambda e_, i_=ic, hd=hdst:
                             e_.tensor_mul(hd[:, i_ * 512:(i_ + 1) * 512], sgb, combrep[:, :])),
                            waits=(ev_cre,
                                   hfree[e9] if ic == 0 else None,
                                   ev_down_read if ic == 0 else None),
                            sig="v")
                    else:
                        ev_h = P.op(
                            "v",
                            (lambda e_, i_=ic, hd=hdst:
                             e_.tensor_copy(hd[:, i_ * 512:(i_ + 1) * 512], sgb)),
                            waits=(hfree[e9] if ic == 0 else None,
                                   ev_down_read if ic == 0 else None),
                            sig="v")
                    ev_hwr[(k, ic)] = ev_h
                    ev_prev_hwr = ev_h
                slot_free[sgu] = ev_gmm
                slot_free[suu] = ev_umm
                ev_hall[(tc, e9)] = ev_hwr[(k, 3)]
            for e9 in range(2):
                k = tc * 9 + e9
                sdd = 4 + e9 % 2
                ev_wd_dma[k] = P.dma("g", wslot(sdd, 4 * D), d_wdd[e9],
                                     waits=(slot_free[sdd],))
            ev_dn_last = {}
            for e9 in range(_K_MOE_E):
                k = tc * 9 + e9
                sdd = 4 + e9 % 2
                ev_mm_e = None
                for cc in range(NCT):
                    for ic in range(4):
                        ev_mm_e = P.op(
                            "t",
                            (lambda e_, c=cc, i_=ic, ee=e9, s_=sdd:
                             e_.matmul(bk(c),
                                       wslot(s_, 4 * D)[:, i_ * D + c * 128: i_ * D + c * 128 + 128],
                                       ab(MOE_H[ee], T)[:, i_ * 512:(i_ + 1) * 512],
                                       start=(ee == 0 and i_ == 0),
                                       stop=(ee == _K_MOE_E - 1 and i_ == 3))),
                            waits=(ev_wd_dma[k], ev_hall[(tc, e9)],
                                   ev_dn_free.get((tc, cc)),
                                   ev_cre_prev if (cc == 3 and e9 == 0) else None,
                                   ev_prev_silu if (cc >= 4 and e9 == 0) else None,
                                   ev_prev_hwr if (cc >= 4 and e9 == 0) else None),
                            sig="t" if ic == 3 else None)
                    ev_dn_last[cc] = ev_mm_e
                slot_free[sdd] = ev_mm_e
                if e9 + 2 < _K_MOE_E:
                    k2 = tc * 9 + e9 + 2
                    ev_wd_dma[k2] = P.dma("g", wslot(sdd, 4 * D), d_wdd[e9 + 2],
                                          waits=(ev_mm_e,))
            ev_down_read = ev_mm_e
            for i in range(9):
                hfree[i] = None
            for cc in range(NCT):
                ev_e = P.op(
                    "v",
                    (lambda e_, c=cc, j=tc:
                     e_.scalar_tensor_tensor(X[c][:, j * 512:(j + 1) * 512], bk(c), 1.0,
                                             X[c][:, j * 512:(j + 1) * 512], ALU.mult, ALU.add)),
                    waits=(ev_dn_last[cc],), sig="v")
                ev_dn_free[(tc + 1, cc)] = ev_e
                ev_dn_evac[cc % 3] = ev_e
            ev_dn_done = ev_dn_evac[(NCT - 1) % 3]

        ev_moe_done = ev_dn_done

        if upto == "moe":
            dump_X("X", (ev_moe_done, ev_dn_evac.get(0), ev_dn_evac.get(1), ev_dn_evac.get(2)))
            _finish(P, sems)
            P.realize(nc, sems)
            return nc

        ev_dn_all = (ev_dn_evac.get(0), ev_dn_evac.get(1), ev_dn_evac.get(2))
        ev_x1b_free = [None, None]
        ev_xadd = [None] * NCT
        for ct in range(NCT):
            sl2 = ct % 2
            x1b = af(O_X1B + sl2 * 8192, T)
            ev_ld = P.dma("y", x1b, x1sc[ct * 128:(ct + 1) * 128, :],
                          waits=(ev_x1b_free[sl2], ev_moe_done,
                                 ev_dn_evac.get(0), ev_dn_evac.get(1), ev_dn_evac.get(2)))
            ev_xadd[ct] = P.op(
                "v", (lambda e_, c=ct, b_=x1b: e_.tensor_add(X[c][:, :], X[c][:, :], b_)),
                waits=(ev_ld,), sig="v")
            ev_x1b_free[sl2] = ev_xadd[ct]
        ev_tm_free = [None, None]
        ev_tr_prev = {0: None, 1: None}
        ev_ln2_first = tuple(x for x in ev_dn_all if x is not None)
        for tt in range(NT):
            sl = tt % 2
            tmb = af(O_TM + sl * 4096, D)
            sqb = af(O_SQ2 + sl * 4096, D)
            ev_tr = {}
            for cc in range(NCT):
                b = cc // 4
                ev_tr[b] = P.op(
                    "t",
                    (lambda e_, c=cc, t_=tt, b_=b:
                     e_.transpose(bk(b_)[:, (c % 4) * 128:(c % 4 + 1) * 128],
                                  X[c][:, t_ * 128:(t_ + 1) * 128], ident_f[:, :])),
                    waits=ev_ln2_first + (ev_xadd[cc],) + ((ev_tr_prev[b],) if cc % 4 == 0 else ()),
                    sig="t" if cc % 4 == 3 else None)
            ev_ln2_first = ()
            ev_cpa = P.op(
                "v", (lambda e_, o=tmb: e_.tensor_copy(o[:, 0:512], bk(0))),
                waits=(ev_tr[0], ev_tm_free[sl]), sig="v")
            ev_cpb = P.op(
                "v", (lambda e_, o=tmb: e_.tensor_copy(o[:, 512:1024], bk(1))),
                waits=(ev_tr[1],), sig="v")
            ev_tr_prev[0] = ev_cpa
            ev_tr_prev[1] = ev_cpb
            ev_sqq = P.op(
                "s", (lambda e_, a=tmb, o=sqb: e_.activation(o, a, ACT.Square)),
                waits=(ev_cpa, ev_cpb, ev_tm_free[sl]), sig="s")
            cb = 8 + 6 * sl
            ev_s1 = P.op("v", (lambda e_, a=tmb, c=cb: e_.reduce_sum(colw[:, c:c + 1], a, axis=AX)),
                         waits=(ev_cpb,), sig="v")
            ev_s2 = P.op("v", (lambda e_, o=sqb, c=cb: e_.reduce_sum(colw[:, c + 1:c + 2], o, axis=AX)),
                         waits=(ev_sqq,), sig="v")
            ev_c1 = P.op("v", (lambda e_, c=cb: e_.tensor_scalar(colw[:, c + 2:c + 3], colw[:, c:c + 1],
                                                                 1.0 / D, None, ALU.mult)),
                         hwaits=(ev_s1,), sig="v")
            ev_c2 = P.op("v", (lambda e_, c=cb: e_.tensor_scalar(colw[:, c + 3:c + 4], colw[:, c + 1:c + 2],
                                                                 1.0 / D, None, ALU.mult)),
                         hwaits=(ev_s2,), sig="v")
            ev_c3 = P.op("v", (lambda e_, c=cb: e_.tensor_mul(colw[:, c + 4:c + 5], colw[:, c + 2:c + 3],
                                                              colw[:, c + 2:c + 3])),
                         hwaits=(ev_c1,), sig="v")
            ev_c4 = P.op("v", (lambda e_, c=cb: e_.tensor_sub(colw[:, c + 3:c + 4], colw[:, c + 3:c + 4],
                                                              colw[:, c + 4:c + 5])),
                         hwaits=(ev_c2, ev_c3), sig="v")
            ev_c5 = P.op("v", (lambda e_, c=cb: e_.tensor_scalar(colw[:, c + 3:c + 4], colw[:, c + 3:c + 4],
                                                                 LN_EPS, None, ALU.add)),
                         hwaits=(ev_c4,), sig="v")
            ev_lnx = P.op("s", (lambda e_, c=cb: e_.activation(colw[:, c + 4:c + 5], colw[:, c + 3:c + 4],
                                                               ACT.Ln)),
                          waits=(ev_c5,), sig="s")
            ev_rx = P.op("s", (lambda e_, c=cb: e_.activation(colw[:, c + 4:c + 5], colw[:, c + 4:c + 5],
                                                              ACT.Exp, scale=-0.5)),
                         waits=(ev_lnx,), sig="s")
            ev_app = P.op(
                "v",
                (lambda e_, a=tmb, o=sqb, c=cb: e_.tensor_scalar(o, a, colw[:, c + 2:c + 3],
                                                                 colw[:, c + 4:c + 5],
                                                                 ALU.subtract, ALU.mult)),
                waits=(ev_rx,), hwaits=(ev_c1,), sig="v")
            ev_out = P.dma("y", out[tt * 128:(tt + 1) * 128, :], sqb, waits=(ev_app,))
            ev_tm_free[sl] = ev_out

        _finish(P, sems)
        P.realize(nc, sems)

    return nc


def _finish(P, sems):
    for sem in QSEMS:
        val = P.cnt[sem]
        if val:
            P.op("y", (lambda e_, s_=sem, v_=val: e_.wait_ge(sems[s_], v_)))
            P.op("t", (lambda e_, s_=sem, v_=val: e_.wait_ge(sems[s_], v_)))



_DEBUG_DUMP = ()


def kernel(**inputs):
    consts = _host_consts()
    weights = _host_weights(inputs)
    x_full = np.ascontiguousarray(np.asarray(inputs["L0_attn_x_t"], dtype=np.float32))

    nc = build_program(upto=_BUILD_UPTO, debug_dump=_DEBUG_DUMP)

    base = {}
    base.update({k: np.ascontiguousarray(v) for k, v in consts.items()})
    base.update({k: np.ascontiguousarray(v) for k, v in weights.items()})
    in_maps = []
    for core in range(8):
        m = dict(base)
        m["xin"] = np.ascontiguousarray(x_full[core])
        in_maps.append(m)

    res = run_bass_kernel_spmd(nc, in_maps, core_ids=list(range(8)))
    if _DEBUG_DUMP:
        kernel._last_results = res.results
    outp = np.stack([res.results[i]["out"] for i in range(8)], axis=0)
    return outp
